# version 04
# baseline (speedup 1.0000x reference)
import contextlib
import numpy as np
import ml_dtypes

import concourse.bass as bass
import concourse.mybir as mybir
from concourse.bass_utils import run_bass_kernel_spmd
from concourse.aot_env import allow_ici_all_to_all

F32 = mybir.dt.float32
BF16 = mybir.dt.bfloat16
F8 = mybir.dt.float8e4
AF = mybir.ActivationFunctionType
ALU = mybir.AluOpType
AXX = mybir.AxisListType.X

NCORES = 8
C_INV_DQL, C_INV_DKVL, C_INV_D, C_EPS_LN, C_EPS_RMS, C_SCALE_ATT, C_ONE, C_NEG1 = range(8)
NCONST = 8
ENGS = ('sync', 'gpsimd', 'tensor', 'scalar', 'vector')


def make_cfg(S=8192, D=2048, H=16, HD=128, RD=64, DQL=384, DKVL=512, SH=32, SN=128):
    c = dict(S=S, D=D, H=H, HD=HD, RD=RD, DQL=DQL, DKVL=DKVL, SH=SH, SN=SN)
    c['SL'] = S // NCORES
    c['TT'] = c['SL'] // 128
    c['DC'] = D // 128
    c['QLC'] = DQL // 128
    c['KLC'] = DKVL // 128
    c['SHL'] = SH // NCORES
    c['HDS'] = D // SH
    c['HPT'] = 128 // c['HDS']
    c['KT'] = c['SL'] // 128
    c['ND5'] = D // 512
    assert c['SL'] % 128 == 0 and D % 512 == 0 and RD == 64 and HD == 128
    assert DQL % 128 == 0 and DKVL % 128 == 0 and SN == 128 and c['HPT'] == 2
    assert c['TT'] <= 8 and SH % NCORES == 0
    return c


class Counter:
    __slots__ = ('h', 'v')
    def __init__(self, h): self.h = h; self.v = 0


class Rec:
    def __init__(self):
        self.ops = {e: [] for e in ENGS}
        self._waited = {e: {} for e in ENGS}

    def emit(self, eng, fn, waits=(), inc=None, incn=1):
        wl = []
        seen = self._waited[eng]
        for (c, v) in waits:
            if v <= 0:
                continue
            if seen.get(id(c), -1) >= v:
                continue
            seen[id(c)] = v
            wl.append((c, v))
        if inc is not None:
            inc.v += incn
        self.ops[eng].append((fn, wl, inc, incn))

    def drain(self, eng):
        self.ops[eng].append((lambda e: e.drain(), [], None, 0))

    def replay(self, blk):
        for eng in ENGS:
            ops = self.ops[eng]
            if not ops:
                continue
            def section(e, _ops=ops):
                for fn, waits, inc, incn in _ops:
                    for (c, v) in waits:
                        e.wait_ge(c.h, v)
                    instr = fn(e)
                    if inc is not None:
                        instr.then_inc(inc.h, incn)
            getattr(blk, eng)(section)


def W(c):
    return (c, c.v)


class Slots:
    def __init__(self, n):
        self.n = n
        self.free = [None] * n
    def acquire(self, i):
        f = self.free[i % self.n]
        return [f] if f is not None else []
    def release(self, i, counter):
        self.free[i % self.n] = (counter, counter.v)


class Stream:
    def __init__(self, sems):
        self.sems = sems
        self.n = len(sems)
    def sem(self, i):
        return self.sems[i % self.n]
    def val(self, i):
        s = self.sems[i % self.n]
        return (s, s.v)


def qchunks(total, w=512):
    o, outl = 0, []
    while o < total:
        s = min(w, total - o); outl.append((o, s)); o += s
    return outl



def build_nc(cfg):
    S, D, H, HD, RD = cfg['S'], cfg['D'], cfg['H'], cfg['HD'], cfg['RD']
    DQL, DKVL, SH, SN = cfg['DQL'], cfg['DKVL'], cfg['SH'], cfg['SN']
    SL, TT, DC, QLC, KLC = cfg['SL'], cfg['TT'], cfg['DC'], cfg['QLC'], cfg['KLC']
    SHL, HDS, HPT, KT, ND5 = cfg['SHL'], cfg['HDS'], cfg['HPT'], cfg['KT'], cfg['ND5']
    SSP = 2 * SN + SH
    QC = qchunks(SL)
    JH = NCORES * KT

    nc = bass.Bass("TRN2", target_bir_lowering=False, debug=False)

    def din(name, shape, dtype=F32):
        return nc.dram_tensor(name, shape, dtype, kind="ExternalInput")

    x_in = din('x_in', [SL, D])
    wqd = din('wqd', [D, DQL]); wkvd = din('wkvd', [D, DKVL])
    wqu = din('wqu', [DQL, H * HD]); wku = din('wku', [DKVL, H * HD]); wvu = din('wvu', [DKVL, H * HD])
    wqr = din('wqr', [D, H * RD]); wkr = din('wkr', [D, H * RD])
    wout_d = din('wout', [H * HD, D])
    qnwT = din('qnwT', [128, QLC]); kvnwT = din('kvnwT', [128, KLC])
    wx = din('wx', [D, SSP]); wg = din('wg', [D, D]); wo_d = din('wo', [D, D])
    bgT = din('bgT', [128, DC]); dpT = din('dpT', [128, DC])
    a_loc = din('a_loc', [SHL, SN])
    selb = din('selb', [SH, SHL * 128])
    fw = din('fw', [D, D], BF16); fb = din('fb', [D], BF16)
    c2t = din('c2t', [RD, SL]); s2t = din('s2t', [RD, SL])
    idb = din('idb', [128, 128], BF16); idf = din('idf', [128, 128])
    psw = din('psw', [RD, RD], BF16)
    consts = din('consts', [1, NCONST])
    out = nc.dram_tensor('out', [SL, D], F32, kind="ExternalOutput")

    LBLK = H * 128 * SL + H * 65 * SL + H * SL * 129 + H * SL
    OFF_KA = 0
    OFF_KB = H * 128 * SL
    OFF_V = OFF_KB + H * 65 * SL
    OFF_KN = OFF_V + H * SL * 129
    kv_in = nc.dram_tensor('kv_in', [LBLK], BF16)
    kv_out = nc.dram_tensor('kv_out', [NCORES * LBLK], BF16)
    red1_i = nc.dram_tensor('red1_i', [1, 1], F32); red1_o = nc.dram_tensor('red1_o', [1, 1], F32)
    red2_i = nc.dram_tensor('red2_i', [1, 1], F32); red2_o = nc.dram_tensor('red2_o', [1, 1], F32)
    red3_i = nc.dram_tensor('red3_i', [H], F32); red3_o = nc.dram_tensor('red3_o', [H], F32)
    dnb_d = nc.dram_tensor('dnb_d', [H * SL], F32)
    ssm_i = nc.dram_tensor('ssm_i', [SSP * SL], F32)
    ssm_o = nc.dram_tensor('ssm_o', [NCORES * SSP * SL], F32)
    ya_i = nc.dram_tensor('ya_i', [NCORES * SHL * SL], F32)
    ya_o = nc.dram_tensor('ya_o', [NCORES * SHL * SL], F32)
    bounce = nc.dram_tensor('bounce', [64], F32)

    def dview(buf, off, shape):
        n = int(np.prod(shape))
        ap = buf.ap()[off:off + n]
        names = ' '.join(f'd{i}' for i in range(len(shape)))
        return ap.rearrange(f'({names}) -> {names}', **{f'd{i}': s for i, s in enumerate(shape)})

    ctx = contextlib.ExitStack()
    nc._kernel_ctx = ctx

    def sb(name, shape, dtype, stk=None, side=None):
        kw = {'side': side} if side else {}
        return (stk or ctx).enter_context(nc.sbuf_tensor(name, shape, dtype, **kw))
    def psum(name, shape, dtype=F32, stk=None):
        return (stk or ctx).enter_context(nc.psum_tensor(name, shape, dtype))
    def sem(name):
        return Counter(ctx.enter_context(nc.semaphore(name)))

    sO = sem('sO')
    sCC = sem('sCC')
    sAXS = sem('sAXS')
    sAXG = sem('sAXG')
    sPRE = sem('sPRE'); sQSQ = sem('sQSQ'); sSTB = sem('sSTB')
    sXR = sem('sXR'); sBIA = sem('sBIA')
    sPE = sem('sPE'); sA = sem('sA'); sV = sem('sV'); sV2 = sem('sV2')
    sP = sem('sP'); sPV = sem('sPV'); sQK = sem('sQK')

    def stream(name, n):
        return Stream([sem(f'{name}_{i}') for i in range(n)])

    def aux(r, eng, fn, extra=()):
        s = sAXS if eng == 'sync' else sAXG
        r.emit(eng, fn, waits=[W(s)] + list(extra), inc=s, incn=16)
        return (s, s.v)

    def run_block(name, fn):
        r = Rec()
        fn(r)
        with nc.Block(name) as blk:
            r.replay(blk)

    consts_sb = sb('consts_sb', [128, NCONST], F32)
    idb_sb = sb('idb_sb', [128, 128], BF16)
    idf_sb = sb('idf_sb', [128, 128], F32)
    psw_sb = sb('psw_sb', [RD, RD], BF16)
    qnw_sb = sb('qnw_sb', [128, QLC], F32)
    kvnw_sb = sb('kvnw_sb', [128, KLC], F32)
    onesf = sb('onesf', [128, 1], F32)
    xnT = sb('xnT', [128, DC * SL], BF16)
    ws_sb = sb('ws_sb', [128, 5], F32)
    arow = sb('arow', [1, 32], F32)
    qspan = contextlib.ExitStack()
    qa = sb('qa', [128, H * SL], BF16, qspan, side='right')
    qbspan = contextlib.ExitStack()
    qb = sb('qb', [65, H * SL], BF16, qbspan, side='right')
    qsq_sb = sb('qsq_sb', [128, SL], F32, qbspan, side='right')
    proj_stk = contextlib.ExitStack()
    cqT = sb('cqT', [128, QLC * SL], BF16, proj_stk)
    ckT = sb('ckT', [128, KLC * SL], BF16, proj_stk)
    c2_sb = sb('c2_sb', [RD, SL], F32, proj_stk)
    s2_sb = sb('s2_sb', [RD, SL], F32, proj_stk)

    CONST = lambda i: consts_sb[:, i:i + 1]

    def b_pre(r):
        pairs = [(consts_sb[:], bass.AP(consts, 0, [[0, 128], [1, NCONST]])),
                 (idb_sb[:], idb.ap()), (idf_sb[:], idf.ap()), (psw_sb[:], psw.ap()),
                 (qnw_sb[:], qnwT.ap()), (kvnw_sb[:], kvnwT.ap()),
                 (c2_sb[:], c2t.ap()), (s2_sb[:], s2t.ap())]
        for dst, src in pairs:
            r.emit('sync', lambda e, d=dst, s=src: e.dma_start(d, s), inc=sPRE, incn=16)
        r.emit('vector', lambda e: e.memset(onesf[:], 1.0), waits=[W(sPRE)])
    run_block('pre', b_pre)

    def emit_ln(r, tag, stk, src_sb=None, amax_to=None):
        xslot = [sb(f'{tag}_x{i}', [128, D], F32, stk) for i in range(2)] if src_sb is None else None
        scr = sb(f'{tag}_scr', [128, D], F32, stk)
        xn = [sb(f'{tag}_xn{i}', [128, D], BF16, stk) for i in range(2)]
        st = sb(f'{tag}_st', [128, 8], F32, stk)
        ptr = psum(f'{tag}_ptr', [128, 1024], BF16, stk)
        xsl = Slots(2)
        xnsl = Slots(2)
        trsl = Slots(1)
        xst = stream(f'{tag}x', 2) if src_sb is None else None

        def src_ap(t):
            return xslot[t % 2][:] if src_sb is None else src_sb[:, t * D:(t + 1) * D]

        ld_done = {}
        def load(t):
            if src_sb is None and t < TT:
                r.emit('sync', lambda e, t=t: e.dma_start(
                    xslot[t % 2][:], x_in[t * 128:(t + 1) * 128, :]),
                    waits=xsl.acquire(t), inc=xst.sem(t), incn=16)
                ld_done[t] = xst.val(t)
        load(0); load(1)

        for t in range(TT):
            lw = [ld_done[t]] if src_sb is None else []
            r.emit('vector', lambda e, t=t: e.reduce_sum(st[:, 0:1], src_ap(t), axis=AXX), waits=lw)
            r.drain('vector')
            r.emit('vector', lambda e: e.tensor_scalar_mul(st[:, 2:3], st[:, 0:1], 1.0 / D))
            r.drain('vector')
            r.emit('vector', lambda e: e.tensor_mul(st[:, 3:4], st[:, 2:3], st[:, 2:3]))
            r.drain('vector')
            r.emit('scalar', lambda e, t=t: e.activation(scr[:], src_ap(t), AF.Square,
                                                         accum_out=st[:, 1:2]), waits=lw, inc=sA)
            r.drain('scalar')
            r.emit('vector', lambda e: e.scalar_tensor_tensor(
                st[:, 4:5], st[:, 1:2], 1.0 / D, st[:, 3:4], op0=ALU.mult, op1=ALU.subtract),
                waits=[W(sA)], inc=sV)
            r.drain('vector')
            r.emit('scalar', lambda e: e.activation(st[:, 5:6], st[:, 4:5], AF.Sqrt,
                                                    bias=CONST(C_EPS_LN)), waits=[W(sV)], inc=sA)
            r.drain('scalar')
            r.emit('vector', lambda e: e.reciprocal(st[:, 6:7], st[:, 5:6]), waits=[W(sA)])
            r.drain('vector')
            r.emit('vector', lambda e: e.scalar_tensor_tensor(
                st[:, 7:8], st[:, 2:3], -1.0, st[:, 6:7], op0=ALU.mult, op1=ALU.mult), inc=sV)
            r.drain('vector')
            r.emit('scalar', lambda e, t=t: e.activation(xn[t % 2][:], src_ap(t), AF.Identity,
                                                         scale=st[:, 6:7], bias=st[:, 7:8]),
                   waits=[W(sV)] + xnsl.acquire(t), inc=sA)
            xn_ready = W(sA)
            xsl.release(t, sA)
            load(t + 2)
            if amax_to is not None:
                dst = amax_to[:, 0:1] if t == 0 else amax_to[:, 1:2]
                r.emit('vector', lambda e, t=t, dst=dst: e.reduce_max(
                    dst, xn[t % 2][:], axis=AXX, apply_absolute_value=True), waits=[xn_ready])
                r.drain('vector')
                if t > 0:
                    r.emit('vector', lambda e: e.tensor_max(amax_to[:, 0:1], amax_to[:, 0:1],
                                                            amax_to[:, 1:2]))
                    r.drain('vector')
            for c in range(DC):
                k = t * DC + c
                r.emit('tensor', lambda e, c=c, k=k, t=t: e.transpose(
                    ptr[:, (k % 8) * 128:(k % 8) * 128 + 128],
                    xn[t % 2][:, c * 128:(c + 1) * 128], idb_sb[:]),
                    waits=[xn_ready] + trsl.acquire(k), inc=sPE)
                r.emit('vector', lambda e, c=c, k=k, t=t: e.tensor_copy(
                    xnT[:, c * SL + t * 128: c * SL + (t + 1) * 128],
                    ptr[:, (k % 8) * 128:(k % 8) * 128 + 128]), waits=[W(sPE)], inc=sV2)
                trsl.release(k, sV2)
            xnsl.release(t, sV2)
        if amax_to is not None:
            r.drain('vector')

    with contextlib.ExitStack() as stk:
        run_block('ln1', lambda r: emit_ln(r, 'ln1', stk, src_sb=None))

    with contextlib.ExitStack() as stk:
        wqd_bf = sb('wqd_bf', [128, DC * DQL], BF16, stk)
        wkvd_bf = sb('wkvd_bf', [128, DC * DKVL], BF16, stk)
        cqn = sb('cqn', [128, TT * DQL], BF16, stk)
        ckn = sb('ckn', [128, TT * DKVL], BF16, stk)
        sqs = sb('sqs_a', [128, 512], F32, stk)
        rsn = sb('rs_a', [128, 8], F32, stk)
        stg = [sb(f'p2a_stg{i}', [128, max(DQL, DKVL)], F32, stk) for i in range(2)]
        pcq = [psum(f'pcq{i}', [128, 512], F32, stk) for i in range(2)]
        ptr2 = psum('ptr2', [128, 1024], BF16, stk)

        def b_p2a(r):
            stsl = Slots(2)
            psl = Slots(2)
            trsl = Slots(1)
            wqst = stream('p2aw', 2)
            plan = [(wqd[c * 128:(c + 1) * 128, :], DQL, wqd_bf[:, c * DQL:(c + 1) * DQL])
                    for c in range(DC)] + \
                   [(wkvd[c * 128:(c + 1) * 128, :], DKVL, wkvd_bf[:, c * DKVL:(c + 1) * DKVL])
                    for c in range(DC)]
            wq_rdy = wk_rdy = None
            for i, (ap, w_, dst) in enumerate(plan):
                r.emit('sync', lambda e, ap=ap, w_=w_, i=i: e.dma_start(stg[i % 2][:, 0:w_], ap),
                       waits=stsl.acquire(i), inc=wqst.sem(i), incn=16)
                r.emit('vector', lambda e, w_=w_, dst=dst, i=i: e.tensor_copy(dst, stg[i % 2][:, 0:w_]),
                       waits=[wqst.val(i)], inc=sV)
                r.drain('vector')
                stsl.release(i, sV)
                if i == DC - 1: wq_rdy = W(sV)
                if i == 2 * DC - 1: wk_rdy = W(sV)

            def one(wrdy, WD, invc, wbf, cn, cT, nLC, po):
                for t in range(TT):
                    pq = pcq[t % 2]
                    for c in range(DC):
                        r.emit('tensor', lambda e, c=c, t=t, pq=pq, WD=WD, wbf=wbf: e.matmul(
                            pq[:, 0:WD], xnT[:, c * SL + t * 128: c * SL + (t + 1) * 128],
                            wbf[:, c * WD:(c + 1) * WD], start=(c == 0), stop=(c == DC - 1)),
                            waits=([wrdy] + psl.acquire(t)) if c == 0 else (),
                            inc=(sPE if c == DC - 1 else None))
                    r.emit('scalar', lambda e, pq=pq, WD=WD: e.activation(
                        sqs[:, 0:WD], pq[:, 0:WD], AF.Square, accum_out=rsn[:, po:po + 1]),
                        waits=[W(sPE)], inc=sA)
                    r.drain('scalar')
                    r.emit('scalar', lambda e: e.activation(
                        rsn[:, po + 1:po + 2], rsn[:, po:po + 1], AF.Sqrt,
                        scale=CONST(invc), bias=CONST(C_EPS_RMS)), inc=sA)
                    r.emit('vector', lambda e: e.reciprocal(rsn[:, po + 2:po + 3], rsn[:, po + 1:po + 2]),
                           waits=[W(sA)], inc=sV)
                    r.drain('vector')
                    r.emit('scalar', lambda e, pq=pq, WD=WD, t=t, cn=cn: e.activation(
                        cn[:, t * WD:(t + 1) * WD], pq[:, 0:WD], AF.Identity,
                        scale=rsn[:, po + 2:po + 3]), waits=[W(sV)], inc=sA)
                    psl.release(t, sA)
                    cn_rdy = W(sA)
                    for c in range(nLC):
                        k = t * nLC + c
                        r.emit('tensor', lambda e, c=c, t=t, k=k, cn=cn, WD=WD: e.transpose(
                            ptr2[:, (k % 8) * 128:(k % 8) * 128 + 128],
                            cn[:, t * WD + c * 128: t * WD + (c + 1) * 128], idb_sb[:]),
                            waits=[cn_rdy] + trsl.acquire(k), inc=sPE)
                        r.emit('vector', lambda e, c=c, t=t, k=k, cT=cT: e.tensor_copy(
                            cT[:, c * SL + t * 128: c * SL + (t + 1) * 128],
                            ptr2[:, (k % 8) * 128:(k % 8) * 128 + 128]), waits=[W(sPE)], inc=sV2)
                        trsl.release(k, sV2)
            one(wq_rdy, DQL, C_INV_DQL, wqd_bf, cqn, cqT, QLC, 0)
            one(wk_rdy, DKVL, C_INV_DKVL, wkvd_bf, ckn, ckT, KLC, 4)
        run_block('p2a', b_p2a)

    def rope_combine(r, h, prp, pswp, rsc, rt1, rt2, dstB_ap, dstB_w, prev):
        r.emit('scalar', lambda e: e.activation(rsc[0:64, 0:SL], prp[0:64, 0:SL], AF.Identity),
               waits=[W(sPE), (sPE, prev.get('swap', 0)), (sV, prev.get('comb', 0))], inc=sA)
        rsc_rdy = W(sA)
        for (qo, qw) in QC:
            r.emit('tensor', lambda e, qo=qo, qw=qw: e.matmul(
                pswp[0:64, qo:qo + qw], psw_sb[:], rsc[0:64, qo:qo + qw],
                start=True, stop=True),
                waits=[rsc_rdy, (sV, prev.get('comb2', 0))],
                inc=(sPE if qo + qw == SL else None))
        prev['swap'] = sPE.v
        r.emit('vector', lambda e: e.tensor_mul(rt1[0:64, :], rsc[0:64, 0:SL], c2_sb[0:64, :]),
               waits=[rsc_rdy])
        r.drain('vector')
        r.emit('vector', lambda e: e.tensor_mul(rt2[0:64, :], pswp[0:64, 0:SL], s2_sb[0:64, :]),
               waits=[W(sPE)], inc=sV)
        prev['comb2'] = sV.v
        r.drain('vector')
        r.emit('vector', lambda e, dstB_ap=dstB_ap: e.tensor_add(dstB_ap, rt1[0:64, :], rt2[0:64, :]),
               waits=list(dstB_w), inc=sV)
        prev['comb'] = sV.v
        r.drain('vector')
        return W(sV)

    def up_rope_phase(r, stk, tag, w_up, w_rope, nLC, cT_src, fold_sb, dstA_fn, dstB_fn, sink_fn=None):
        stgq = [sb(f'{tag}_stg{i}', [128, 128], F32, stk) for i in range(2)]
        wtb = [sb(f'{tag}_wt{i}', [128, 128], BF16, stk) for i in range(2)]
        rsc = sb(f'{tag}_rsc', [64, SL], BF16, stk)
        rt1 = sb(f'{tag}_rt1', [64, SL], F32, stk)
        rt2 = sb(f'{tag}_rt2', [64, SL], F32, stk)
        pqa = [psum(f'{tag}_pqa0', [128, 1024], F32, stk)]
        prp = psum(f'{tag}_prp', [128, 1024], F32, stk)
        pswp = psum(f'{tag}_psw', [128, 1024], F32, stk)

        plan = []
        for h in range(H):
            for c in range(nLC):
                plan.append((w_up[c * 128:(c + 1) * 128, h * HD:(h + 1) * HD], HD, fold_sb[:, c:c + 1]))
            for c in range(DC):
                plan.append((w_rope[c * 128:(c + 1) * 128, h * RD:(h + 1) * RD], RD, None))
        cast_done = {}
        stsl = Slots(2)
        wtsl = Slots(2)
        wst = stream(f'{tag}w', 2)
        cursor = [0]
        def load_next():
            i = cursor[0]
            if i >= len(plan):
                return
            cursor[0] += 1
            ap, w_, fold = plan[i]
            r.emit('sync', lambda e, ap=ap, w_=w_, i=i: e.dma_start(stgq[i % 2][:, 0:w_], ap),
                   waits=stsl.acquire(i), inc=wst.sem(i), incn=16)
            if fold is not None:
                r.emit('vector', lambda e, w_=w_, i=i, fold=fold: e.tensor_scalar_mul(
                    wtb[i % 2][:, 0:w_], stgq[i % 2][:, 0:w_], fold),
                    waits=[wst.val(i)] + wtsl.acquire(i), inc=sV)
            else:
                r.emit('vector', lambda e, w_=w_, i=i: e.tensor_copy(
                    wtb[i % 2][:, 0:w_], stgq[i % 2][:, 0:w_]),
                    waits=[wst.val(i)] + wtsl.acquire(i), inc=sV)
            r.drain('vector')
            stsl.release(i, sV)
            cast_done[i] = sV.v
        load_next(); load_next()

        pqsl = Slots(1)
        prev = {}
        for h in range(H):
            base = h * (nLC + DC)
            pq = pqa[0]
            for c in range(nLC):
                i = base + c
                mw = [(sV, cast_done[i])] + (pqsl.acquire(h) if c == 0 else [])
                for (qo, qw) in QC:
                    r.emit('tensor', lambda e, c=c, i=i, qo=qo, qw=qw, pq=pq: e.matmul(
                        pq[:, qo:qo + qw], wtb[i % 2][:, 0:HD],
                        cT_src[:, c * SL + qo: c * SL + qo + qw],
                        start=(c == 0), stop=(c == nLC - 1)), waits=mw,
                        inc=(sPE if qo + qw == SL else None))
                wtsl.release(i, sPE)
                load_next()
            da, daw = dstA_fn(h)
            r.emit('scalar', lambda e, da=da, pq=pq: e.activation(da, pq[:, 0:SL], AF.Identity),
                   waits=[W(sPE)] + list(daw), inc=sA)
            pqsl.release(h, sA)
            a_rdy = W(sA)
            for c in range(DC):
                i = base + nLC + c
                mw = [(sV, cast_done[i])] + ([(sPE, prev.get('swap', 0))] if c == 0 else [])
                for (qo, qw) in QC:
                    r.emit('tensor', lambda e, c=c, i=i, qo=qo, qw=qw: e.matmul(
                        prp[0:64, qo:qo + qw], wtb[i % 2][:, 0:RD],
                        xnT[:, c * SL + qo: c * SL + qo + qw],
                        start=(c == 0), stop=(c == DC - 1)), waits=mw,
                        inc=(sPE if qo + qw == SL else None))
                wtsl.release(i, sPE)
                load_next()
            db_ap, db_w = dstB_fn(h)
            b_rdy = rope_combine(r, h, prp, pswp, rsc, rt1, rt2, db_ap, db_w, prev)
            if sink_fn is not None:
                sink_fn(r, h, a_rdy, b_rdy)

    with contextlib.ExitStack() as stk:
        sqsb = sb('p2b_sqs', [128, SL], F32, stk)
        qrow = sb('p2b_qrow', [1, SL], F32, stk)
        pq1 = psum('p2b_pq1', [128, 1024], F32, stk)
        sqprev = {}

        def q_sink(r, h, a_rdy, b_rdy):
            r.emit('scalar', lambda e, h=h: e.activation(
                sqsb[:, 0:SL], qa[:, h * SL:(h + 1) * SL], AF.Square),
                waits=[a_rdy, (sPE, sqprev.get('mm', 0))], inc=sA)
            r.drain('scalar')
            for (qo, qw) in QC:
                r.emit('tensor', lambda e, qo=qo, qw=qw: e.matmul(
                    pq1[0:1, qo:qo + qw], onesf[:], sqsb[:, qo:qo + qw],
                    start=True, stop=False),
                    waits=[W(sA), (sA, sqprev.get('evac', 0))],
                    inc=(sPE if qo + qw == SL else None))
            r.emit('scalar', lambda e, h=h: e.activation(
                sqsb[0:64, 0:SL], qb[0:64, h * SL:(h + 1) * SL], AF.Square),
                waits=[b_rdy, W(sPE)], inc=sA)
            r.drain('scalar')
            for (qo, qw) in QC:
                r.emit('tensor', lambda e, qo=qo, qw=qw: e.matmul(
                    pq1[0:1, qo:qo + qw], onesf[0:64, :], sqsb[0:64, qo:qo + qw],
                    start=False, stop=True), waits=[W(sA)],
                    inc=(sPE if qo + qw == SL else None))
            sqprev['mm'] = sPE.v
            r.emit('scalar', lambda e: e.activation(qrow[0:1, 0:SL], pq1[0:1, 0:SL], AF.Identity),
                   waits=[W(sPE), (sQSQ, sQSQ.v)], inc=sA)
            sqprev['evac'] = sA.v
            r.emit('gpsimd', lambda e, h=h: e.dma_start(qsq_sb[h:h + 1, :], qrow[0:1, 0:SL]),
                   waits=[W(sA)], inc=sQSQ, incn=16)

        def b_p2b(r):
            up_rope_phase(r, stk, 'p2b', wqu, wqr, QLC, cqT, qnw_sb,
                          lambda h: (qa[:, h * SL:(h + 1) * SL], []),
                          lambda h: (qb[0:64, h * SL:(h + 1) * SL], []), q_sink)
            r.emit('gpsimd', lambda e: e.engine_nop(), waits=[W(sQSQ)])
        run_block('p2b', b_p2b)

    kv_store_sems = []
    ws_ready = [None]
    knar_ready = [None]
    with contextlib.ExitStack() as stk:
        knm_loc = sb('knm_loc', [1, H], F32, stk)
        kaS = [sb(f'kaS{i}', [128, SL], BF16, stk) for i in range(2)]
        kbS = [sb(f'kbS{i}', [65, SL], BF16, stk) for i in range(2)]
        sqsk = sb('p2c_sqs', [128, SL], F32, stk)
        krow = sb('p2c_krow', [1, SL], BF16, stk)
        pkn = psum('p2c_pkn', [128, 1024], F32, stk)
        kasl = Slots(2); kbsl = Slots(2); krsl = Slots(1)
        kast = stream('kaw', 2); kbst = stream('kbw', 2); krst = stream('krw', 1)
        kv_store_sems.extend(kast.sems + kbst.sems + krst.sems)
        kprev = {}

        def k_sink(r, h, a_rdy, b_rdy):
            r.emit('sync', lambda e, h=h: e.dma_start(
                dview(kv_in, OFF_KA + h * 128 * SL, [128, SL]), kaS[h % 2][:]),
                waits=[a_rdy], inc=kast.sem(h), incn=16)
            kasl.release(h, kast.sem(h))
            r.emit('sync', lambda e, h=h: e.dma_start(
                dview(kv_in, OFF_KB + h * 65 * SL, [65, SL]), kbS[h % 2][:]),
                waits=[b_rdy], inc=kbst.sem(h), incn=16)
            kbsl.release(h, kbst.sem(h))
            r.emit('scalar', lambda e, h=h: e.activation(
                sqsk[:, 0:SL], kaS[h % 2][:], AF.Square),
                waits=[a_rdy, (sPE, kprev.get('mm', 0))], inc=sA)
            r.drain('scalar')
            for (qo, qw) in QC:
                r.emit('tensor', lambda e, qo=qo, qw=qw: e.matmul(
                    pkn[0:1, qo:qo + qw], onesf[:], sqsk[:, qo:qo + qw],
                    start=True, stop=False),
                    waits=[W(sA), (sA, kprev.get('evac', 0))],
                    inc=(sPE if qo + qw == SL else None))
            r.emit('scalar', lambda e, h=h: e.activation(
                sqsk[0:64, 0:SL], kbS[h % 2][0:64, :], AF.Square),
                waits=[b_rdy, W(sPE)], inc=sA)
            r.drain('scalar')
            for (qo, qw) in QC:
                r.emit('tensor', lambda e, qo=qo, qw=qw: e.matmul(
                    pkn[0:1, qo:qo + qw], onesf[0:64, :], sqsk[0:64, qo:qo + qw],
                    start=False, stop=True), waits=[W(sA)],
                    inc=(sPE if qo + qw == SL else None))
            kprev['mm'] = sPE.v
            r.emit('scalar', lambda e: e.activation(krow[0:1, 0:SL], pkn[0:1, 0:SL], AF.Identity),
                   waits=[W(sPE)] + krsl.acquire(h), inc=sA)
            kprev['evac'] = sA.v
            r.emit('sync', lambda e, h=h: e.dma_start(
                dview(kv_in, OFF_KN + h * SL, [1, SL]), krow[0:1, :]), waits=[W(sA)],
                inc=krst.sem(0), incn=16)
            r.emit('vector', lambda e, h=h: e.reduce_max(
                knm_loc[0:1, h:h + 1], krow[0:1, :], axis=AXX), waits=[W(sA)], inc=sV)
            r.drain('vector')
            krsl.release(h, krst.sem(0))
            kprev['knm'] = sV.v

        def b_p2c(r):
            r.emit('vector', lambda e: e.memset(kbS[0][64:65, :], 1.0))
            r.emit('vector', lambda e: e.memset(kbS[1][64:65, :], 1.0))
            r.drain('vector')
            up_rope_phase(r, stk, 'p2c', wku, wkr, KLC, ckT, kvnw_sb,
                          lambda h: (kaS[h % 2][:], kasl.acquire(h)),
                          lambda h: (kbS[h % 2][0:64, :], kbsl.acquire(h)), k_sink)
            w1 = aux(r, 'sync', lambda e: e.dma_start(red3_i.ap(), knm_loc[0:1, :]),
                     extra=[(sV, kprev['knm'])])
            r.emit('gpsimd', lambda e: e.collective_compute(
                "AllReduce", ALU.max, replica_groups=[list(range(NCORES))],
                ins=[red3_i.ap().opt()], outs=[red3_o.ap().opt()]), waits=[w1], inc=sCC)
            knar_ready[0] = W(sCC)
            r.emit('sync', lambda e: e.nop(),
                   waits=[(s, s.v) for s in (kast.sems + kbst.sems + krst.sems)])
        run_block('p2c', b_p2c)

    with contextlib.ExitStack() as stk2:
        wvu_bf = sb('wvu_bf', [128, KLC * H * HD], BF16, stk2)
        stgv = [sb(f'p2v_stg{i}', [128, 512], F32, stk2) for i in range(2)]
        vSl = [sb(f'p2v_v{i}', [128, H * 129], BF16, stk2) for i in range(2)]
        fstg = [sb(f'p2v_f{i}', [128, D], BF16, stk2) for i in range(2)]
        famx = sb('p2v_famx', [128, 2], F32, stk2)
        pv = psum('p2v_pv', [128, 512], F32, stk2)
        ptr3 = psum('p2v_ptr3', [128, 512], F32, stk2)

        def b_p2v(r):
            np5 = (H * HD) // 512
            stsl = Slots(2)
            wvst = stream('wvw', 2)
            vstt = stream('vsw', 2)
            fwst = stream('fww', 2)
            i = 0
            for c in range(KLC):
                for p5 in range(np5):
                    ap = wvu[c * 128:(c + 1) * 128, p5 * 512:(p5 + 1) * 512]
                    r.emit('sync', lambda e, ap=ap, i=i: e.dma_start(stgv[i % 2][:], ap),
                           waits=stsl.acquire(i), inc=wvst.sem(i), incn=16)
                    r.emit('vector', lambda e, c=c, p5=p5, i=i: e.tensor_scalar_mul(
                        wvu_bf[:, c * H * HD + p5 * 512: c * H * HD + (p5 + 1) * 512],
                        stgv[i % 2][:], kvnw_sb[:, c:c + 1]), waits=[wvst.val(i)], inc=sV)
                    r.drain('vector')
                    stsl.release(i, sV)
                    i += 1
            wv_rdy = W(sV)
            vsl = Slots(2)
            evac_v = 0
            for t in range(TT):
                r.emit('vector', lambda e, t=t: e.memset(vSl[t % 2][:], 1.0),
                       waits=vsl.acquire(t))
                r.drain('vector')
                r.emit('vector', lambda e: e.engine_nop(), inc=sV)
                ones_rdy = W(sV)
                for n5 in range(np5):
                    for c in range(KLC):
                        r.emit('tensor', lambda e, t=t, n5=n5, c=c: e.matmul(
                            pv[:], ckT[:, c * SL + t * 128: c * SL + (t + 1) * 128],
                            wvu_bf[:, c * H * HD + n5 * 512: c * H * HD + (n5 + 1) * 512],
                            start=(c == 0), stop=(c == KLC - 1)),
                            waits=([wv_rdy, (sA, evac_v)] if c == 0 else ()),
                            inc=(sPE if c == KLC - 1 else None))
                    for hh in range(4):
                        h = n5 * 4 + hh
                        r.emit('scalar', lambda e, t=t, h=h, hh=hh: e.activation(
                            vSl[t % 2][:, h * 129: h * 129 + 128],
                            pv[:, hh * 128:(hh + 1) * 128], AF.Identity),
                            waits=[W(sPE), ones_rdy], inc=sA)
                    evac_v = sA.v
                for h in range(H):
                    r.emit('sync', lambda e, t=t, h=h: e.dma_start(
                        dview(kv_in, OFF_V + h * SL * 129 + t * 128 * 129, [128, 129]),
                        vSl[t % 2][:, h * 129:(h + 1) * 129]), waits=[W(sA)],
                        inc=vstt.sem(t), incn=16)
                vsl.release(t, vstt.sem(t))
            agw = [(s, s.v) for s in (kv_store_sems + vstt.sems)]
            r.emit('gpsimd', lambda e: e.collective_compute(
                "AllGather", ALU.bypass, replica_groups=[list(range(NCORES))],
                ins=[kv_in.ap().opt()], outs=[kv_out.ap().opt()]), waits=agw, inc=sCC)
            fsl = Slots(2)
            for c in range(DC):
                r.emit('sync', lambda e, c=c: e.dma_start(
                    fstg[c % 2][:], fw[c * 128:(c + 1) * 128, :]),
                    waits=fsl.acquire(c), inc=fwst.sem(c), incn=16)
                if c == 0:
                    r.emit('vector', lambda e: e.reduce_max(
                        famx[:, 0:1], fstg[0][:], axis=AXX, apply_absolute_value=True),
                        waits=[fwst.val(c)], inc=sV)
                else:
                    r.emit('vector', lambda e, c=c: e.reduce_max(
                        famx[:, 1:2], fstg[c % 2][:], axis=AXX, apply_absolute_value=True),
                        waits=[fwst.val(c)])
                    r.drain('vector')
                    r.emit('vector', lambda e: e.tensor_max(famx[:, 0:1], famx[:, 0:1],
                                                            famx[:, 1:2]), inc=sV)
                r.drain('vector')
                fsl.release(c, sV)
            r.emit('tensor', lambda e: e.transpose(ptr3[0:1, 0:128], famx[:, 0:1], idf_sb[:]),
                   waits=[W(sV)], inc=sPE)
            r.emit('vector', lambda e: e.reduce_max(arow[0:1, 0:1], ptr3[0:1, 0:128], axis=AXX),
                   waits=[W(sPE)])
            r.drain('vector')
            r.emit('vector', lambda e: e.reciprocal(arow[0:1, 1:2], arow[0:1, 0:1]), inc=sV)
            r.drain('vector')
            aux(r, 'sync', lambda e: e.dma_start(bounce.ap()[0:2], arow[0:1, 0:2]),
                extra=[W(sV)])
            aux(r, 'sync', lambda e: e.dma_start(
                ws_sb[:, 0:1], bass.AP(bounce, 0, [[0, 128], [1, 1]])))
            aux(r, 'sync', lambda e: e.dma_start(
                ws_sb[:, 1:2], bass.AP(bounce, 1, [[0, 128], [1, 1]])))
            ws_ready[0] = (sAXS, sAXS.v)
        run_block('p2v', b_p2v)

    proj_stk.close()

    with contextlib.ExitStack() as stk:
        knm = sb('knm', [H, 2], F32, stk)
        stmp = sb('stmp', [H, SL], F32, stk)
        stb = sb('stb', [H, SL], BF16, stk)

        def b_stab(r):
            kngst = stream('kng', 1)
            r.emit('gpsimd', lambda e: e.dma_start(
                knm[:, 0:1], bass.AP(red3_o, 0, [[1, H], [1, 1]])),
                waits=[knar_ready[0]], inc=kngst.sem(0), incn=16)
            r.emit('vector', lambda e: e.engine_nop(), waits=[kngst.val(0)], inc=sV)
            r.emit('vector', lambda e: e.tensor_scalar_mul(stmp[:], qsq_sb[0:H, :], knm[:, 0:1]),
                   waits=[W(sV), (sQSQ, 16 * H)], inc=sV)
            r.drain('vector')
            r.emit('scalar', lambda e: e.activation(stmp[:], stmp[:], AF.Sqrt), waits=[W(sV)], inc=sA)
            r.emit('vector', lambda e: e.tensor_scalar_mul(stb[:], stmp[:], -1.0), waits=[W(sA)], inc=sV)
            r.drain('vector')
            for h in range(H):
                r.emit('gpsimd', lambda e, h=h: e.dma_start(
                    qb[64:65, h * SL:(h + 1) * SL], stb[h:h + 1, :]), waits=[W(sV)], inc=sSTB, incn=16)
            r.emit('gpsimd', lambda e: e.engine_nop(), waits=[(sSTB, 16 * H)])
        run_block('stab', b_stab)

    NSLAB = 3
    NP = 3
    with contextlib.ExitStack() as stk:
        kaA = [sb(f'at_ka{i}', [128, SL], BF16, stk) for i in range(NSLAB)]
        kbA = [sb(f'at_kb{i}', [65, SL], BF16, stk) for i in range(NSLAB)]
        vA = [sb(f'at_v{i}', [128, KT * 129], BF16, stk) for i in range(NSLAB)]
        pS = [sb(f'at_p{i}', [128, SL], BF16, stk) for i in range(NP)]
        recb = sb('at_rec', [1, SL], F32, stk)
        dnb = sb('at_dnb', [128, SL], F32, stk)
        psc = [psum(f'at_sc{i}', [128, 1024], F32, stk) for i in range(2)]
        po2 = psum('at_po', [128, 1024], F32, stk)
        pd = psum('at_pd', [1, 1024], F32, stk)

        slabs = [(h, src) for h in range(H) for src in range(NCORES)]

        def b_attn(r):
            kv_done = {}
            kvsl = Slots(NSLAB)
            kvst = stream('kvl', NSLAB)
            dnst = stream('dnb', 1)
            def load_slab(i):
                if i >= len(slabs):
                    return
                h, src = slabs[i]
                base = src * LBLK
                aw = kvsl.acquire(i) + [W(sCC)]
                r.emit('gpsimd', lambda e, h=h, src=src, i=i: e.dma_start(
                    kaA[i % NSLAB][:], dview(kv_out, base + OFF_KA + h * 128 * SL, [128, SL])),
                    waits=aw, inc=kvst.sem(i), incn=16)
                r.emit('gpsimd', lambda e, h=h, src=src, i=i: e.dma_start(
                    kbA[i % NSLAB][:], dview(kv_out, base + OFF_KB + h * 65 * SL, [65, SL])),
                    inc=kvst.sem(i), incn=16)
                for kt in range(KT):
                    r.emit('gpsimd', lambda e, h=h, src=src, i=i, kt=kt: e.dma_start(
                        vA[i % NSLAB][:, kt * 129:(kt + 1) * 129],
                        dview(kv_out, base + OFF_V + h * SL * 129 + kt * 128 * 129, [128, 129])),
                        inc=kvst.sem(i), incn=16)
                kv_done[i] = kvst.val(i)
            for i in range(min(NSLAB, len(slabs))):
                load_slab(i)

            p_done = {}
            pv_done = {}
            ao_done = {}
            j = 0
            for i, (h, src) in enumerate(slabs):
                jh0 = (i % NCORES) * KT
                for kt in range(KT):
                    jh = jh0 + kt
                    qkw = [kv_done[i]]
                    if j >= 2:
                        qkw.append((sP, p_done[j - 2]))
                    if jh == 0:
                        qkw.append((sSTB, 16 * H))
                    for (qo, qw) in QC:
                        r.emit('tensor', lambda e, i=i, kt=kt, qo=qo, qw=qw, h=h, j=j: e.matmul(
                            psc[j % 2][:, qo:qo + qw],
                            kaA[i % NSLAB][:, kt * 128:(kt + 1) * 128],
                            qa[:, h * SL + qo: h * SL + qo + qw], start=True, stop=False),
                            waits=qkw if qo == 0 else ())
                        r.emit('tensor', lambda e, i=i, kt=kt, qo=qo, qw=qw, h=h, j=j: e.matmul(
                            psc[j % 2][:, qo:qo + qw],
                            kbA[i % NSLAB][:, kt * 128:(kt + 1) * 128],
                            qb[:, h * SL + qo: h * SL + qo + qw], start=False, stop=True),
                            inc=(sQK if qo + qw == SL else None))
                    qk_v = sQK.v
                    ew = [(sQK, qk_v)]
                    if j >= NP:
                        ew.append((sPV, pv_done[j - NP]))
                    r.emit('scalar', lambda e, j=j: e.activation(
                        pS[j % NP][:, 0:SL], psc[j % 2][:, 0:SL], AF.Exp,
                        scale=CONST(C_SCALE_ATT)), waits=ew, inc=sP)
                    p_done[j] = sP.v
                    pvw = [(sP, p_done[j])]
                    if jh == 0 and h >= 1:
                        pvw.append((sV, ao_done[h - 1]))
                    first = True
                    for (qo, qw) in QC:
                        r.emit('tensor', lambda e, i=i, kt=kt, qo=qo, qw=qw, j=j, jh=jh: e.matmul(
                            po2[:, qo:qo + qw],
                            vA[i % NSLAB][:, kt * 129: kt * 129 + 128],
                            pS[j % NP][:, qo:qo + qw],
                            start=(jh == 0), stop=(jh == JH - 1)),
                            waits=(pvw if first else ()))
                        first = False
                        r.emit('tensor', lambda e, i=i, kt=kt, qo=qo, qw=qw, j=j, jh=jh: e.matmul(
                            pd[0:1, qo:qo + qw],
                            vA[i % NSLAB][:, kt * 129 + 128: kt * 129 + 129],
                            pS[j % NP][:, qo:qo + qw],
                            start=(jh == 0), stop=(jh == JH - 1)),
                            inc=(sPV if qo + qw == SL else None))
                    pv_done[j] = sPV.v
                    j += 1
                if i + NSLAB < len(slabs):
                    kvsl.release(i, sPV)
                    load_slab(i + NSLAB)
                if (i % NCORES) == NCORES - 1:
                    r.emit('vector', lambda e: e.reciprocal(recb[0:1, :], pd[0:1, 0:SL]),
                           waits=[W(sPV)], inc=sV)
                    r.drain('vector')
                    r.emit('sync', lambda e, h=h: e.dma_start(
                        dview(dnb_d, h * SL, [1, SL]), recb[0:1, :]), waits=[W(sV)],
                        inc=dnst.sem(0), incn=16)
                    r.emit('gpsimd', lambda e, h=h: e.dma_start(
                        dnb[:], bass.AP(dnb_d, h * SL, [[0, 128], [1, SL]])),
                        waits=[dnst.val(0)], inc=dnst.sem(0), incn=16)
                    r.emit('vector', lambda e, h=h: e.tensor_mul(
                        qa[:, h * SL:(h + 1) * SL], po2[:, 0:SL], dnb[:]),
                        waits=[dnst.val(0)], inc=sV)
                    r.drain('vector')
                    ao_done[h] = sV.v
        run_block('attn', b_attn)
    qbspan.close()

    wq8T = sb('wq8T', [128, DC * D], F8)
    xres = sb('xres', [128, TT * D], F32)
    with contextlib.ExitStack() as stk:
        wstg = [sb(f'wo_stg{i}', [128, 512], F32, stk) for i in range(2)]
        wob = sb('wo_b', [128, DC * 512], BF16, stk)
        pw = [psum(f'wo_pw{i}', [128, 512], F32, stk) for i in range(2)]

        def b_wout(r):
            for t in range(TT):
                r.emit('sync', lambda e, t=t: e.dma_start(
                    xres[:, t * D:(t + 1) * D], x_in[t * 128:(t + 1) * 128, :]), inc=sXR, incn=16)
            xres_ld = (sXR, 16 * TT)
            stsl = Slots(2)
            pwsl = Slots(2)
            wost = stream('wow', 2)
            li = 0
            add_done = {}
            mm_prev = 0
            for n in range(ND5):
                cast_done = {}
                for hc in range(DC):
                    ap = wout_d[hc * 128:(hc + 1) * 128, n * 512:(n + 1) * 512]
                    r.emit('sync', lambda e, ap=ap, li=li: e.dma_start(wstg[li % 2][:], ap),
                           waits=stsl.acquire(li), inc=wost.sem(li), incn=16)
                    r.emit('vector', lambda e, hc=hc, li=li: e.tensor_copy(
                        wob[:, hc * 512:(hc + 1) * 512], wstg[li % 2][:]),
                        waits=[wost.val(li), (sPE, mm_prev)], inc=sV)
                    r.drain('vector')
                    stsl.release(li, sV)
                    cast_done[hc] = sV.v
                    li += 1
                for t in range(TT):
                    k = n * TT + t
                    for hc in range(DC):
                        mw = []
                        if hc == 0:
                            mw = [(sV, cast_done[DC - 1])] + pwsl.acquire(k)
                        r.emit('tensor', lambda e, t=t, hc=hc, k=k: e.matmul(
                            pw[k % 2][:], qa[:, hc * SL + t * 128: hc * SL + (t + 1) * 128],
                            wob[:, hc * 512:(hc + 1) * 512],
                            start=(hc == 0), stop=(hc == DC - 1)), waits=mw,
                            inc=(sPE if hc == DC - 1 else None))
                    r.emit('vector', lambda e, t=t, n=n, k=k: e.tensor_add(
                        xres[:, t * D + n * 512: t * D + (n + 1) * 512],
                        xres[:, t * D + n * 512: t * D + (n + 1) * 512], pw[k % 2][:]),
                        waits=[W(sPE), xres_ld], inc=sV)
                    r.drain('vector')
                    pwsl.release(k, sV)
                mm_prev = sPE.v
        run_block('wout', b_wout)
    qspan.close()

    bias_bc = sb('bias_bc', [128, D], BF16)
    with contextlib.ExitStack() as stk:
        fstg2 = [sb(f'fw2_{i}', [128, D], BF16, stk) for i in range(2)]
        ptrf = psum('fw2_ptr', [128, 1024], BF16, stk)

        def b_fwq(r):
            fwst2 = stream('fw2', 2)
            r.emit('sync', lambda e: e.dma_start(bias_bc[:], bass.AP(fb, 0, [[0, 128], [1, D]])),
                   inc=sBIA, incn=16)
            fsl = Slots(2)
            trsl = Slots(1)
            for c in range(DC):
                r.emit('sync', lambda e, c=c: e.dma_start(fstg2[c % 2][:], fw[c * 128:(c + 1) * 128, :]),
                       waits=fsl.acquire(c), inc=fwst2.sem(c), incn=16)
                ld_v = fwst2.val(c)
                for cc in range(DC):
                    k = c * DC + cc
                    r.emit('tensor', lambda e, c=c, cc=cc, k=k: e.transpose(
                        ptrf[:, (k % 8) * 128:(k % 8) * 128 + 128],
                        fstg2[c % 2][:, cc * 128:(cc + 1) * 128], idb_sb[:]),
                        waits=[ld_v] + trsl.acquire(k), inc=sPE)
                    r.emit('scalar', lambda e, c=c, cc=cc, k=k: e.activation(
                        wq8T[:, cc * D + c * 128: cc * D + (c + 1) * 128],
                        ptrf[:, (k % 8) * 128:(k % 8) * 128 + 128], AF.Identity,
                        scale=ws_sb[:, 1:2]), waits=[W(sPE), ws_ready[0]], inc=sA)
                    trsl.release(k, sA)
                fsl.release(c, sA)
        run_block('fwq', b_fwq)

    def emit_amax_ar(r, tag, amx, red_i, red_o, ptams):
        r.emit('tensor', lambda e: e.transpose(ptams[0:1, 0:128], amx[:, 0:1], idf_sb[:]),
               waits=[W(sV)], inc=sPE)
        r.emit('vector', lambda e: e.reduce_max(arow[0:1, 8:9], ptams[0:1, 0:128], axis=AXX),
               waits=[W(sPE)], inc=sV)
        r.drain('vector')
        w1 = aux(r, 'sync', lambda e: e.dma_start(red_i.ap(), arow[0:1, 8:9]), extra=[W(sV)])
        r.emit('gpsimd', lambda e: e.collective_compute(
            "AllReduce", ALU.max, replica_groups=[list(range(NCORES))],
            ins=[red_i.ap().opt()], outs=[red_o.ap().opt()]), waits=[w1], inc=sCC)
        w2 = aux(r, 'gpsimd', lambda e: e.dma_start(arow[0:1, 9:10], red_o.ap()), extra=[W(sCC)])
        r.emit('vector', lambda e: e.reciprocal(arow[0:1, 10:11], arow[0:1, 9:10]),
               waits=[w2], inc=sV)
        r.drain('vector')
        aux(r, 'sync', lambda e: e.dma_start(bounce.ap()[4:6], arow[0:1, 9:11]), extra=[W(sV)])
        aux(r, 'sync', lambda e: e.dma_start(ws_sb[:, 2:3], bass.AP(bounce, 5, [[0, 128], [1, 1]])))
        w3 = aux(r, 'sync', lambda e: e.dma_start(ws_sb[:, 4:5], bass.AP(bounce, 4, [[0, 128], [1, 1]])))
        r.emit('vector', lambda e: e.tensor_tensor(ws_sb[:, 3:4], ws_sb[:, 0:1], ws_sb[:, 4:5],
                                                   ALU.mult), waits=[w3], inc=sV)
        r.drain('vector')

    def emit_ffn(r, tag, stk):
        xq = sb(f'{tag}_xq', [128, DC * SL], F8, stk)
        tmp = [sb(f'{tag}_tmp{i}', [128, 512], BF16, stk) for i in range(2)]
        pf = [psum(f'{tag}_pf{i}', [128, 512], F32, stk) for i in range(2)]
        for c in range(DC):
            r.emit('scalar', lambda e, c=c: e.activation(
                xq[:, c * SL:(c + 1) * SL], xnT[:, c * SL:(c + 1) * SL], AF.Identity,
                scale=ws_sb[:, 2:3]), waits=[W(sV), W(sV2), W(sAXS)], inc=sA)
        xq_rdy = W(sA)
        pfsl = Slots(2)
        tmsl = Slots(2)
        k = 0
        for t in range(TT):
            for n in range(ND5):
                for c in range(DC):
                    mw = ([xq_rdy] + pfsl.acquire(k)) if c == 0 else ()
                    r.emit('tensor', lambda e, t=t, n=n, c=c, k=k: e.matmul(
                        pf[k % 2][:], xq[:, c * SL + t * 128: c * SL + (t + 1) * 128],
                        wq8T[:, c * D + n * 512: c * D + (n + 1) * 512],
                        start=(c == 0), stop=(c == DC - 1)), waits=mw,
                        inc=(sPE if c == DC - 1 else None))
                r.emit('scalar', lambda e, k=k: e.activation(
                    tmp[k % 2][:], pf[k % 2][:], AF.Identity, scale=ws_sb[:, 3:4]),
                    waits=[W(sPE)] + tmsl.acquire(k), inc=sA)
                pfsl.release(k, sA)
                r.emit('vector', lambda e, k=k, n=n: e.tensor_add(
                    tmp[k % 2][:], tmp[k % 2][:], bias_bc[:, n * 512:(n + 1) * 512]),
                    waits=[W(sA), (sBIA, 16)])
                r.drain('vector')
                r.emit('vector', lambda e, t=t, n=n, k=k: e.tensor_add(
                    xres[:, t * D + n * 512: t * D + (n + 1) * 512],
                    xres[:, t * D + n * 512: t * D + (n + 1) * 512], tmp[k % 2][:]), inc=sV)
                r.drain('vector')
                tmsl.release(k, sV)
                k += 1

    with contextlib.ExitStack() as stk:
        amx2 = sb('amx2', [128, 2], F32, stk)
        ptam = psum('ptam', [128, 512], F32, stk)
        run_block('ln2', lambda r: emit_ln(r, 'ln2', stk, src_sb=xres, amax_to=amx2))
        run_block('ar1', lambda r: emit_amax_ar(r, 'ar1', amx2, red1_i, red1_o, ptam))
        run_block('ffn1', lambda r: emit_ffn(r, 'ffn1', stk))

    with contextlib.ExitStack() as stk:
        run_block('ln3', lambda r: emit_ln(r, 'ln3', stk, src_sb=xres))
    with contextlib.ExitStack() as stk:
        wx_bf = sb('wx_bf', [128, DC * SSP], BF16, stk)
        stgx = [sb(f'sx_stg{i}', [128, SSP], F32, stk) for i in range(2)]
        ssro = sb('ssro', [128, SL], F32, stk)
        pss = [psum(f'pss{i}', [128, 512], F32, stk) for i in range(2)]

        def b_ssmp(r):
            stsl = Slots(2)
            wxst = stream('wxw', 2)
            ssst = stream('ssw', 1)
            for c in range(DC):
                r.emit('sync', lambda e, c=c: e.dma_start(stgx[c % 2][:], wx[c * 128:(c + 1) * 128, :]),
                       waits=stsl.acquire(c), inc=wxst.sem(c), incn=16)
                r.emit('vector', lambda e, c=c: e.tensor_copy(
                    wx_bf[:, c * SSP:(c + 1) * SSP], stgx[c % 2][:]), waits=[wxst.val(c)], inc=sV)
                r.drain('vector')
                stsl.release(c, sV)
            w_rdy = W(sV)
            rows = [(0, SN), (SN, SN), (2 * SN, SH)]
            k = 0
            pssl = Slots(2)
            dma_prev = None
            for (ro, rh) in rows:
                for (qo, qw) in QC:
                    for c in range(DC):
                        mw = ([w_rdy] + pssl.acquire(k)) if c == 0 else ()
                        r.emit('tensor', lambda e, ro=ro, rh=rh, qo=qo, qw=qw, c=c, k=k: e.matmul(
                            pss[k % 2][0:rh, 0:qw],
                            wx_bf[:, c * SSP + ro: c * SSP + ro + rh],
                            xnT[:, c * SL + qo: c * SL + qo + qw],
                            start=(c == 0), stop=(c == DC - 1)), waits=mw,
                            inc=(sPE if c == DC - 1 else None))
                    r.emit('scalar', lambda e, rh=rh, qo=qo, qw=qw, k=k: e.activation(
                        ssro[0:rh, qo:qo + qw], pss[k % 2][0:rh, 0:qw], AF.Identity),
                        waits=[W(sPE)] + ([dma_prev] if dma_prev else []), inc=sA)
                    pssl.release(k, sA)
                    k += 1
                r.emit('sync', lambda e, ro=ro, rh=rh: e.dma_start(
                    dview(ssm_i, ro * SL, [rh, SL]), ssro[0:rh, :]), waits=[W(sA)],
                    inc=ssst.sem(0), incn=16)
                dma_prev = ssst.val(0)
            r.emit('gpsimd', lambda e: e.collective_compute(
                "AllGather", ALU.bypass, replica_groups=[list(range(NCORES))],
                ins=[ssm_i.ap().opt()], outs=[ssm_o.ap().opt()]), waits=[ssst.val(0)], inc=sCC)
        run_block('ssmp', b_ssmp)

    with contextlib.ExitStack() as stk:
        asb = sb('asb', [SHL, SN], F32, stk)
        atc = sb('atc', [128, max(SHL, 1)], F32, stk)
        ssb = sb('ssb', [SH, SHL * 128], F32, stk)
        Bc = [sb(f'sc_B{i}', [SN, SL], F32, stk) for i in range(2)]
        Cc = [sb(f'sc_C{i}', [SN, SL], F32, stk) for i in range(2)]
        dbc = [sb(f'sc_db{i}', [SH, SL], F32, stk) for i in range(2)]
        sin = sb('sc_sin', [128, SL], F32, stk)
        bbuf = sb('sc_b', [128, SL], F32, stk)
        abuf = sb('sc_a', [128, SL], F32, stk)
        hs = sb('sc_hs', [128, SL], F32, stk)
        hc = sb('sc_hc', [128, SL], F32, stk)
        states = sb('sc_st', [128, SHL], F32, stk)
        yrow = sb('sc_y', [1, SL], F32, stk)
        pdb = psum('sc_pdb', [128, 1024], F32, stk)
        py = psum('sc_py', [128, 1024], F32, stk)
        ptr4 = psum('sc_ptr4', [128, 512], F32, stk)

        def b_scan(r):
            aux(r, 'sync', lambda e: e.dma_start(asb[:], a_loc.ap()))
            wsel = aux(r, 'sync', lambda e: e.dma_start(ssb[:], selb.ap()))
            bcst = stream('bcd', 2)
            ywst = stream('yww', 1)
            r.emit('tensor', lambda e: e.transpose(ptr4[0:SN, 0:SHL], asb[:], idf_sb[0:SHL, 0:SHL]),
                   waits=[wsel], inc=sPE)
            r.emit('vector', lambda e: e.tensor_copy(atc[0:SN, 0:SHL], ptr4[0:SN, 0:SHL]),
                   waits=[W(sPE)], inc=sV)
            r.drain('vector')
            NCH = S // SL
            csl = Slots(2)
            prev = {}
            for c in range(NCH):
                base = c * SSP * SL
                lw = [W(sCC)] + csl.acquire(c)
                r.emit('sync', lambda e, c=c, base=base: e.dma_start(
                    Bc[c % 2][:], dview(ssm_o, base, [SN, SL])), waits=lw, inc=bcst.sem(c), incn=16)
                r.emit('sync', lambda e, c=c, base=base: e.dma_start(
                    Cc[c % 2][:], dview(ssm_o, base + SN * SL, [SN, SL])), inc=bcst.sem(c), incn=16)
                r.emit('sync', lambda e, c=c, base=base: e.dma_start(
                    dbc[c % 2][:], dview(ssm_o, base + 2 * SN * SL, [SH, SL])), inc=bcst.sem(c), incn=16)
                ld_v = bcst.val(c)
                for jj in range(SHL):
                    for (qo, qw) in QC:
                        r.emit('tensor', lambda e, jj=jj, c=c, qo=qo, qw=qw: e.matmul(
                            pdb[:, qo:qo + qw], ssb[:, jj * 128:(jj + 1) * 128],
                            dbc[c % 2][:, qo:qo + qw], start=True, stop=True),
                            waits=[ld_v, (sV, prev.get('sin', 0))],
                            inc=(sPE if qo + qw == SL else None))
                    r.emit('vector', lambda e, c=c: e.tensor_add(sin[:], pdb[:, 0:SL], Bc[c % 2][:]),
                           waits=[W(sPE), (sA, prev.get('d', 0))], inc=sV)
                    prev['sin'] = sV.v
                    r.drain('vector')
                    r.emit('scalar', lambda e: e.activation(sin[:], sin[:], AF.Exp), waits=[W(sV)])
                    r.drain('scalar')
                    r.emit('scalar', lambda e: e.activation(sin[:], sin[:], AF.Ln, bias=CONST(C_ONE)),
                           inc=sA)
                    r.drain('scalar')
                    r.emit('scalar', lambda e: e.activation(bbuf[:], sin[:], AF.Square),
                           waits=[(sV, prev.get('scan', 0))], inc=sA)
                    r.drain('scalar')
                    prev['d'] = sA.v
                    r.emit('vector', lambda e, jj=jj: e.tensor_scalar_mul(abuf[:], sin[:],
                                                                          atc[:, jj:jj + 1]),
                           waits=[W(sA), (sA, prev.get('aexp', 0))], inc=sV)
                    r.drain('vector')
                    r.emit('scalar', lambda e: e.activation(abuf[:], abuf[:], AF.Exp),
                           waits=[W(sV)], inc=sA)
                    prev['aexp'] = sA.v
                    init = 0.0 if c == 0 else states[:, jj:jj + 1]
                    r.emit('vector', lambda e, init=init: e.tensor_tensor_scan(
                        hs[:], abuf[:], bbuf[:], init, op0=ALU.mult, op1=ALU.add),
                        waits=[W(sA)], inc=sV)
                    r.drain('vector')
                    r.emit('vector', lambda e, jj=jj: e.tensor_copy(states[:, jj:jj + 1],
                                                                    hs[:, SL - 1:SL]))
                    r.drain('vector')
                    r.emit('vector', lambda e, c=c: e.tensor_mul(hc[:], hs[:], Cc[c % 2][:]),
                           waits=[(sPE, prev.get('ymm', 0))], inc=sV)
                    prev['scan'] = sV.v
                    r.drain('vector')
                    for (qo, qw) in QC:
                        r.emit('tensor', lambda e, qo=qo, qw=qw: e.matmul(
                            py[0:1, qo:qo + qw], onesf[:], hc[:, qo:qo + qw],
                            start=True, stop=True),
                            waits=[W(sV), (sA, prev.get('yevac', 0))],
                            inc=(sPE if qo + qw == SL else None))
                    prev['ymm'] = sPE.v
                    r.emit('scalar', lambda e: e.activation(yrow[0:1, 0:SL], py[0:1, 0:SL], AF.Identity),
                           waits=[W(sPE)] + ([prev['ydma']] if 'ydma' in prev else []), inc=sA)
                    prev['yevac'] = sA.v
                    r.emit('gpsimd', lambda e, c=c, jj=jj: e.dma_start(
                        dview(ya_i, (c * SHL + jj) * SL, [1, SL]), yrow[0:1, :]),
                        waits=[W(sA)], inc=ywst.sem(0), incn=16)
                    prev['ydma'] = ywst.val(0)
                csl.release(c, sV)
            r.emit('gpsimd', lambda e: e.collective_compute(
                "AllToAll", ALU.bypass, replica_groups=[list(range(NCORES))],
                ins=[ya_i.ap().opt()], outs=[ya_o.ap().opt()]), waits=[ywst.val(0)], inc=sCC)
        with allow_ici_all_to_all():
            run_block('scan', b_scan)

    with contextlib.ExitStack() as stk:
        gateT = sb('gateT', [128, DC * SL], BF16, stk)
        bg_sb = sb('bg_sb', [128, DC], F32, stk)
        dp_sb = sb('dp_sb', [128, DC], F32, stk)
        stgg = [sb(f'g_stg{i}', [128, 128], F32, stk) for i in range(2)]
        wgb = [sb(f'g_wgb{i}', [128, DC * 128], BF16, stk) for i in range(2)]
        ybc = sb('g_ybc', [128, SL], F32, stk)
        yt1 = sb('g_yt1', [128, SL], F32, stk)
        pg = [psum(f'g_pg{i}', [128, 1024], F32, stk) for i in range(2)]

        def b_gate(r):
            aux(r, 'sync', lambda e: e.dma_start(bg_sb[:], bgT.ap()))
            aux_v = aux(r, 'sync', lambda e: e.dma_start(dp_sb[:], dpT.ap()))
            stsl = Slots(2)
            wgsl = Slots(2)
            pgsl = Slots(2)
            wgst = stream('wgw', 2)
            ybst = stream('ybl', 1)
            li = 0
            cast_done = {}
            def load_m(m):
                nonlocal li
                if m >= DC:
                    return
                for c in range(DC):
                    ap = wg[c * 128:(c + 1) * 128, m * 128:(m + 1) * 128]
                    r.emit('sync', lambda e, ap=ap, li=li: e.dma_start(stgg[li % 2][:], ap),
                           waits=stsl.acquire(li) + (wgsl.acquire(m) if c == 0 else []),
                           inc=wgst.sem(li), incn=16)
                    r.emit('vector', lambda e, c=c, m=m, li=li: e.tensor_copy(
                        wgb[m % 2][:, c * 128:(c + 1) * 128], stgg[li % 2][:]),
                        waits=[wgst.val(li)], inc=sV)
                    r.drain('vector')
                    stsl.release(li, sV)
                    li += 1
                cast_done[m] = sV.v
            load_m(0); load_m(1)
            sig_done = {}
            for m in range(DC):
                pgw = [(sV, cast_done[m])] + pgsl.acquire(m)
                for c in range(DC):
                    for (qo, qw) in QC:
                        r.emit('tensor', lambda e, m=m, c=c, qo=qo, qw=qw: e.matmul(
                            pg[m % 2][:, qo:qo + qw], wgb[m % 2][:, c * 128:(c + 1) * 128],
                            xnT[:, c * SL + qo: c * SL + qo + qw],
                            start=(c == 0), stop=(c == DC - 1)),
                            waits=pgw if (c == 0 and qo == 0) else (),
                            inc=(sPE if (c == DC - 1 and qo + qw == SL) else None))
                wgsl.release(m, sPE)
                load_m(m + 2)
                r.emit('scalar', lambda e, m=m: e.activation(
                    gateT[:, m * SL:(m + 1) * SL], pg[m % 2][:, 0:SL], AF.Sigmoid,
                    bias=bg_sb[:, m:m + 1]), waits=[W(sPE), aux_v], inc=sA)
                sig_done[m] = sA.v
                pgsl.release(m, sA)
            gates_all = W(sPE)
            yb_prev = 0
            for m in range(DC):
                for half in range(HPT):
                    r.emit('gpsimd', lambda e, m=m, half=half: e.dma_start(
                        ybc[half * HDS:(half + 1) * HDS, :],
                        bass.AP(ya_o, (HPT * m + half) * SL, [[0, HDS], [1, SL]])),
                        waits=[W(sCC), (sV, yb_prev)], inc=ybst.sem(0), incn=16)
                r.emit('vector', lambda e, m=m: e.tensor_scalar_mul(
                    yt1[:], xnT[:, m * SL:(m + 1) * SL], dp_sb[:, m:m + 1]),
                    waits=[gates_all, (sA, sig_done[m])], inc=None)
                r.drain('vector')
                r.emit('vector', lambda e, m=m: e.tensor_add(yt1[:], yt1[:], ybc[:]),
                       waits=[ybst.val(0)])
                r.drain('vector')
                r.emit('vector', lambda e, m=m: e.tensor_mul(
                    xnT[:, m * SL:(m + 1) * SL], yt1[:], gateT[:, m * SL:(m + 1) * SL]), inc=sV)
                r.drain('vector')
                yb_prev = sV.v
        run_block('gate', b_gate)

    with contextlib.ExitStack() as stk:
        wstg2 = [sb(f'woo_stg{i}', [128, 512], F32, stk) for i in range(2)]
        woob = sb('woo_b', [128, DC * 512], BF16, stk)
        pw2 = [psum(f'woo_pw{i}', [128, 512], F32, stk) for i in range(2)]

        def b_wo(r):
            stsl = Slots(2)
            pwsl = Slots(2)
            wost2 = stream('wo2', 2)
            li = 0
            mm_prev = 0
            for n in range(ND5):
                cast_done = None
                for c in range(DC):
                    ap = wo_d[c * 128:(c + 1) * 128, n * 512:(n + 1) * 512]
                    r.emit('sync', lambda e, ap=ap, li=li: e.dma_start(wstg2[li % 2][:], ap),
                           waits=stsl.acquire(li), inc=wost2.sem(li), incn=16)
                    r.emit('vector', lambda e, c=c, li=li: e.tensor_copy(
                        woob[:, c * 512:(c + 1) * 512], wstg2[li % 2][:]),
                        waits=[wost2.val(li), (sPE, mm_prev)], inc=sV)
                    r.drain('vector')
                    stsl.release(li, sV)
                    li += 1
                cast_done = W(sV)
                for t in range(TT):
                    k = n * TT + t
                    for c in range(DC):
                        mw = ([cast_done] + pwsl.acquire(k)) if c == 0 else ()
                        r.emit('tensor', lambda e, t=t, c=c, k=k: e.matmul(
                            pw2[k % 2][:], xnT[:, c * SL + t * 128: c * SL + (t + 1) * 128],
                            woob[:, c * 512:(c + 1) * 512],
                            start=(c == 0), stop=(c == DC - 1)), waits=mw,
                            inc=(sPE if c == DC - 1 else None))
                    r.emit('vector', lambda e, t=t, n=n, k=k: e.tensor_add(
                        xres[:, t * D + n * 512: t * D + (n + 1) * 512],
                        xres[:, t * D + n * 512: t * D + (n + 1) * 512], pw2[k % 2][:]),
                        waits=[W(sPE)], inc=sV)
                    r.drain('vector')
                    pwsl.release(k, sV)
                mm_prev = sPE.v
        run_block('wo', b_wo)

    with contextlib.ExitStack() as stk:
        amx4 = sb('amx4', [128, 2], F32, stk)
        ptam2 = psum('ptam2', [128, 512], F32, stk)
        run_block('ln4', lambda r: emit_ln(r, 'ln4', stk, src_sb=xres, amax_to=amx4))
        run_block('ar2', lambda r: emit_amax_ar(r, 'ar2', amx4, red2_i, red2_o, ptam2))
        run_block('ffn2', lambda r: emit_ffn(r, 'ffn2', stk))

    def b_store(r):
        for t in range(TT):
            r.emit('sync', lambda e, t=t: e.dma_start(
                out[t * 128:(t + 1) * 128, :], xres[:, t * D:(t + 1) * D]),
                waits=[W(sV), W(sA)], inc=sO, incn=16)
        r.emit('sync', lambda e: e.nop(), waits=[W(sO)])
    run_block('store', b_store)

    return nc



def _host_consts(cfg):
    v = np.zeros(NCONST, dtype=np.float64)
    v[C_INV_DQL] = 1.0 / cfg['DQL']
    v[C_INV_DKVL] = 1.0 / cfg['DKVL']
    v[C_INV_D] = 1.0 / cfg['D']
    v[C_EPS_LN] = 1e-5
    v[C_EPS_RMS] = 1e-6
    v[C_SCALE_ATT] = 1.0 / np.sqrt(cfg['HD'] + cfg['RD'])
    v[C_ONE] = 1.0
    v[C_NEG1] = -1.0
    return v.astype(np.float32).reshape(1, NCONST)


def _host_rope(cfg, core):
    S, SL, RD = cfg['S'], cfg['SL'], cfg['RD']
    f = 10000.0 ** (-(np.arange(0, RD, 2, dtype=np.float64)) / RD)
    t = np.arange(core * SL, (core + 1) * SL, dtype=np.float64)
    ang = np.outer(f, t)
    C2 = np.repeat(np.cos(ang), 2, axis=0)
    S2 = np.repeat(np.sin(ang), 2, axis=0)
    S2[0::2, :] *= -1.0
    return C2.astype(np.float32), S2.astype(np.float32)


def make_inputs(cfg, core, full):
    SL, D, H, RD = cfg['SL'], cfg['D'], cfg['H'], cfg['RD']
    DQL, DKVL, SH, SN, SHL = cfg['DQL'], cfg['DKVL'], cfg['SH'], cfg['SN'], cfg['SHL']
    QLC, KLC, DC, HDS = cfg['QLC'], cfg['KLC'], cfg['DC'], cfg['HDS']
    f32 = lambda a: np.ascontiguousarray(a, dtype=np.float32)
    C2, S2 = _host_rope(cfg, core)
    psw_m = np.zeros((RD, RD), dtype=np.float32)
    for rrr in range(RD):
        psw_m[rrr, rrr ^ 1] = 1.0
    sel = np.zeros((SH, SHL * 128), dtype=np.float32)
    for j in range(SHL):
        sel[SHL * core + j, j * 128:(j + 1) * 128] = 1.0
    d = {
        'x_in': f32(full['x'][core * SL:(core + 1) * SL]),
        'wqd': f32(full['Wq_down']), 'wkvd': f32(full['Wkv_down']),
        'wqu': f32(full['Wq_up']), 'wku': f32(full['Wk_up']), 'wvu': f32(full['Wv_up']),
        'wqr': f32(full['Wq_rope']), 'wkr': f32(full['Wk_rope']),
        'wout': f32(full['Wout']),
        'qnwT': f32(full['q_norm_w'].reshape(QLC, 128).T),
        'kvnwT': f32(full['kv_norm_w'].reshape(KLC, 128).T),
        'wx': f32(full['Wx']), 'wg': f32(full['Wg']), 'wo': f32(full['Wo']),
        'bgT': f32(full['bg'].reshape(DC, 128).T),
        'dpT': f32(np.repeat(full['Dp'], HDS).reshape(DC, 128).T),
        'a_loc': f32(full['A'][SHL * core: SHL * (core + 1)]),
        'selb': sel,
        'fw': np.ascontiguousarray(full['fw'], dtype=ml_dtypes.bfloat16),
        'fb': np.ascontiguousarray(full['fb'], dtype=ml_dtypes.bfloat16),
        'c2t': C2, 's2t': S2,
        'idb': np.eye(128, dtype=ml_dtypes.bfloat16),
        'idf': np.eye(128, dtype=np.float32),
        'psw': psw_m.astype(ml_dtypes.bfloat16),
        'consts': _host_consts(cfg),
    }
    return d


def kernel(L0_attn_a_x, L0_attn_a_Wq_down, L0_attn_a_q_norm_w, L0_attn_a_Wq_up,
           L0_attn_a_Wq_rope, L0_attn_a_Wkv_down, L0_attn_a_kv_norm_w, L0_attn_a_Wk_up,
           L0_attn_a_Wv_up, L0_attn_a_Wk_rope, L0_attn_a_Wout, L0_attn_b_Wx, L0_attn_b_Wg,
           L0_attn_b_bg, L0_attn_b_Wo, L0_attn_b_A, L0_attn_b_Dp, L0_ffn_weight, L0_ffn_bias):
    cfg = make_cfg()
    S, D, SL = cfg['S'], cfg['D'], cfg['SL']
    full = dict(
        x=np.asarray(L0_attn_a_x, dtype=np.float32).reshape(S, D),
        Wq_down=L0_attn_a_Wq_down, q_norm_w=L0_attn_a_q_norm_w, Wq_up=L0_attn_a_Wq_up,
        Wq_rope=L0_attn_a_Wq_rope, Wkv_down=L0_attn_a_Wkv_down, kv_norm_w=L0_attn_a_kv_norm_w,
        Wk_up=L0_attn_a_Wk_up, Wv_up=L0_attn_a_Wv_up, Wk_rope=L0_attn_a_Wk_rope,
        Wout=L0_attn_a_Wout, Wx=L0_attn_b_Wx, Wg=L0_attn_b_Wg, bg=L0_attn_b_bg,
        Wo=L0_attn_b_Wo, A=L0_attn_b_A, Dp=L0_attn_b_Dp,
        fw=L0_ffn_weight, fb=L0_ffn_bias,
    )
    nc = build_nc(cfg)
    in_maps = [make_inputs(cfg, core, full) for core in range(NCORES)]
    res = run_bass_kernel_spmd(nc, in_maps, core_ids=list(range(NCORES)))
    outs = [np.asarray(r['out'], dtype=np.float32) for r in res.results]
    return np.concatenate(outs, axis=0).reshape(1, S, D)


# version 05 (final)
# speedup vs baseline: 1.0940x; 1.0940x over previous
import contextlib
import numpy as np
import ml_dtypes

import concourse.bass as bass
import concourse.mybir as mybir
from concourse.bass_utils import run_bass_kernel_spmd
from concourse.aot_env import allow_ici_all_to_all

F32 = mybir.dt.float32
BF16 = mybir.dt.bfloat16
F8 = mybir.dt.float8e4
AF = mybir.ActivationFunctionType
ALU = mybir.AluOpType
AXX = mybir.AxisListType.X

NCORES = 8
C_INV_DQL, C_INV_DKVL, C_INV_D, C_EPS_LN, C_EPS_RMS, C_SCALE_ATT, C_ONE, C_NEG1 = range(8)
NCONST = 8
ENGS = ('sync', 'gpsimd', 'tensor', 'scalar', 'vector')


def make_cfg(S=8192, D=2048, H=16, HD=128, RD=64, DQL=384, DKVL=512, SH=32, SN=128):
    c = dict(S=S, D=D, H=H, HD=HD, RD=RD, DQL=DQL, DKVL=DKVL, SH=SH, SN=SN)
    c['SL'] = S // NCORES
    c['TT'] = c['SL'] // 128
    c['DC'] = D // 128
    c['QLC'] = DQL // 128
    c['KLC'] = DKVL // 128
    c['SHL'] = SH // NCORES
    c['HDS'] = D // SH
    c['HPT'] = 128 // c['HDS']
    c['KT'] = c['SL'] // 128
    c['ND5'] = D // 512
    assert c['SL'] % 128 == 0 and D % 512 == 0 and RD == 64 and HD == 128
    assert DQL % 128 == 0 and DKVL % 128 == 0 and SN == 128 and c['HPT'] == 2
    assert c['TT'] <= 8 and SH % NCORES == 0
    return c


class Counter:
    __slots__ = ('h', 'v')
    def __init__(self, h): self.h = h; self.v = 0


class Rec:
    def __init__(self):
        self.ops = {e: [] for e in ENGS}
        self._waited = {e: {} for e in ENGS}

    def emit(self, eng, fn, waits=(), inc=None, incn=1):
        wl = []
        seen = self._waited[eng]
        for (c, v) in waits:
            if v <= 0:
                continue
            if seen.get(id(c), -1) >= v:
                continue
            seen[id(c)] = v
            wl.append((c, v))
        if inc is not None:
            inc.v += incn
        self.ops[eng].append((fn, wl, inc, incn))

    def drain(self, eng):
        self.ops[eng].append((lambda e: e.drain(), [], None, 0))

    def replay(self, blk):
        for eng in ENGS:
            ops = self.ops[eng]
            if not ops:
                continue
            def section(e, _ops=ops):
                for fn, waits, inc, incn in _ops:
                    for (c, v) in waits:
                        e.wait_ge(c.h, v)
                    instr = fn(e)
                    if inc is not None:
                        instr.then_inc(inc.h, incn)
            getattr(blk, eng)(section)


def W(c):
    return (c, c.v)


class Slots:
    def __init__(self, n):
        self.n = n
        self.free = [None] * n
    def acquire(self, i):
        f = self.free[i % self.n]
        return [f] if f is not None else []
    def release(self, i, counter):
        self.free[i % self.n] = (counter, counter.v)


class Stream:
    def __init__(self, sems):
        self.sems = sems
        self.n = len(sems)
    def sem(self, i):
        return self.sems[i % self.n]
    def val(self, i):
        s = self.sems[i % self.n]
        return (s, s.v)


def qchunks(total, w=512):
    o, outl = 0, []
    while o < total:
        s = min(w, total - o); outl.append((o, s)); o += s
    return outl



def build_nc(cfg):
    S, D, H, HD, RD = cfg['S'], cfg['D'], cfg['H'], cfg['HD'], cfg['RD']
    DQL, DKVL, SH, SN = cfg['DQL'], cfg['DKVL'], cfg['SH'], cfg['SN']
    SL, TT, DC, QLC, KLC = cfg['SL'], cfg['TT'], cfg['DC'], cfg['QLC'], cfg['KLC']
    SHL, HDS, HPT, KT, ND5 = cfg['SHL'], cfg['HDS'], cfg['HPT'], cfg['KT'], cfg['ND5']
    SSP = 2 * SN + SH
    QC = qchunks(SL)
    JH = NCORES * KT

    nc = bass.Bass("TRN2", target_bir_lowering=False, debug=False)

    def din(name, shape, dtype=F32):
        return nc.dram_tensor(name, shape, dtype, kind="ExternalInput")

    x_in = din('x_in', [SL, D])
    wqd = din('wqd', [D, DQL]); wkvd = din('wkvd', [D, DKVL])
    wqu = din('wqu', [DQL, H * HD]); wku = din('wku', [DKVL, H * HD]); wvu = din('wvu', [DKVL, H * HD])
    wqr = din('wqr', [D, H * RD]); wkr = din('wkr', [D, H * RD])
    wout_d = din('wout', [H * HD, D])
    qnwT = din('qnwT', [128, QLC]); kvnwT = din('kvnwT', [128, KLC])
    wx = din('wx', [D, SSP]); wg = din('wg', [D, D]); wo_d = din('wo', [D, D])
    bgT = din('bgT', [128, DC]); dpT = din('dpT', [128, DC])
    a_loc = din('a_loc', [SHL, SN])
    selb = din('selb', [SH, SHL * 128])
    fw = din('fw', [D, D], BF16); fb = din('fb', [D], BF16)
    c2t = din('c2t', [RD, SL]); s2t = din('s2t', [RD, SL])
    idb = din('idb', [128, 128], BF16); idf = din('idf', [128, 128])
    psw = din('psw', [RD, RD], BF16)
    consts = din('consts', [1, NCONST])
    out = nc.dram_tensor('out', [SL, D], F32, kind="ExternalOutput")

    LBLK = H * 128 * SL + H * 65 * SL + H * SL * 129 + H * SL
    OFF_KA = 0
    OFF_KB = H * 128 * SL
    OFF_V = OFF_KB + H * 65 * SL
    OFF_KN = OFF_V + H * SL * 129
    kv_in = nc.dram_tensor('kv_in', [LBLK], BF16)
    kv_out = nc.dram_tensor('kv_out', [NCORES * LBLK], BF16)
    red1_i = nc.dram_tensor('red1_i', [1, 1], F32); red1_o = nc.dram_tensor('red1_o', [1, 1], F32)
    red2_i = nc.dram_tensor('red2_i', [1, 1], F32); red2_o = nc.dram_tensor('red2_o', [1, 1], F32)
    red3_i = nc.dram_tensor('red3_i', [H], F32); red3_o = nc.dram_tensor('red3_o', [H], F32)
    ssm_i = nc.dram_tensor('ssm_i', [SSP * SL], F32)
    ssm_o = nc.dram_tensor('ssm_o', [NCORES * SSP * SL], F32)
    ya_i = nc.dram_tensor('ya_i', [NCORES * SHL * SL], F32)
    ya_o = nc.dram_tensor('ya_o', [NCORES * SHL * SL], F32)
    bounce = nc.dram_tensor('bounce', [64], F32)

    def dview(buf, off, shape):
        n = int(np.prod(shape))
        ap = buf.ap()[off:off + n]
        names = ' '.join(f'd{i}' for i in range(len(shape)))
        return ap.rearrange(f'({names}) -> {names}', **{f'd{i}': s for i, s in enumerate(shape)})

    ctx = contextlib.ExitStack()
    nc._kernel_ctx = ctx

    def sb(name, shape, dtype, stk=None, side=None):
        kw = {'side': side} if side else {}
        return (stk or ctx).enter_context(nc.sbuf_tensor(name, shape, dtype, **kw))
    def psum(name, shape, dtype=F32, stk=None):
        return (stk or ctx).enter_context(nc.psum_tensor(name, shape, dtype))
    def sem(name):
        return Counter(ctx.enter_context(nc.semaphore(name)))

    sO = sem('sO')
    sCC = sem('sCC')
    sAXS = sem('sAXS')
    sAXG = sem('sAXG')
    sPRE = sem('sPRE'); sQSQ = sem('sQSQ'); sSTB = sem('sSTB')
    sXR = sem('sXR'); sBIA = sem('sBIA')
    sPE = sem('sPE'); sA = sem('sA'); sV = sem('sV'); sV2 = sem('sV2')
    sP = sem('sP'); sPV = sem('sPV'); sQK = sem('sQK')

    def stream(name, n):
        return Stream([sem(f'{name}_{i}') for i in range(n)])

    def aux(r, eng, fn, extra=()):
        s = sAXS if eng == 'sync' else sAXG
        r.emit(eng, fn, waits=[W(s)] + list(extra), inc=s, incn=16)
        return (s, s.v)

    def run_block(name, fn):
        r = Rec()
        fn(r)
        with nc.Block(name) as blk:
            r.replay(blk)

    consts_sb = sb('consts_sb', [128, NCONST], F32)
    idb_sb = sb('idb_sb', [128, 128], BF16)
    idf_sb = sb('idf_sb', [128, 128], F32)
    psw_sb = sb('psw_sb', [RD, RD], BF16)
    qnw_sb = sb('qnw_sb', [128, QLC], F32)
    kvnw_sb = sb('kvnw_sb', [128, KLC], F32)
    onesf = sb('onesf', [128, 1], F32)
    xnT = sb('xnT', [128, DC * SL], BF16)
    ws_sb = sb('ws_sb', [128, 5], F32)
    arow = sb('arow', [1, 32], F32)
    qspan = contextlib.ExitStack()
    qa = sb('qa', [128, H * SL], BF16, qspan, side='right')
    qbspan = contextlib.ExitStack()
    qb = sb('qb', [65, H * SL], BF16, qbspan, side='right')
    qsq_sb = sb('qsq_sb', [128, SL], F32, qbspan, side='right')
    proj_stk = contextlib.ExitStack()
    cqT = sb('cqT', [128, QLC * SL], BF16, proj_stk)
    ckT = sb('ckT', [128, KLC * SL], BF16, proj_stk)
    c2_sb = sb('c2_sb', [RD, SL], F32, proj_stk)
    s2_sb = sb('s2_sb', [RD, SL], F32, proj_stk)

    CONST = lambda i: consts_sb[:, i:i + 1]

    def b_pre(r):
        pairs = [(consts_sb[:], bass.AP(consts, 0, [[0, 128], [1, NCONST]])),
                 (idb_sb[:], idb.ap()), (idf_sb[:], idf.ap()), (psw_sb[:], psw.ap()),
                 (qnw_sb[:], qnwT.ap()), (kvnw_sb[:], kvnwT.ap()),
                 (c2_sb[:], c2t.ap()), (s2_sb[:], s2t.ap())]
        for dst, src in pairs:
            r.emit('sync', lambda e, d=dst, s=src: e.dma_start(d, s), inc=sPRE, incn=16)
        r.emit('vector', lambda e: e.memset(onesf[:], 1.0), waits=[W(sPRE)])
    run_block('pre', b_pre)

    def emit_ln(r, tag, stk, src_sb=None, amax_to=None):
        xslot = [sb(f'{tag}_x{i}', [128, D], F32, stk) for i in range(2)] if src_sb is None else None
        scr = sb(f'{tag}_scr', [128, D], F32, stk)
        xn = [sb(f'{tag}_xn{i}', [128, D], BF16, stk) for i in range(2)]
        st = sb(f'{tag}_st', [128, 8], F32, stk)
        ptr = psum(f'{tag}_ptr', [128, 1024], BF16, stk)
        xsl = Slots(2)
        xnsl = Slots(2)
        trsl = Slots(1)
        xst = stream(f'{tag}x', 2) if src_sb is None else None

        def src_ap(t):
            return xslot[t % 2][:] if src_sb is None else src_sb[:, t * D:(t + 1) * D]

        ld_done = {}
        def load(t):
            if src_sb is None and t < TT:
                r.emit('sync', lambda e, t=t: e.dma_start(
                    xslot[t % 2][:], x_in[t * 128:(t + 1) * 128, :]),
                    waits=xsl.acquire(t), inc=xst.sem(t), incn=16)
                ld_done[t] = xst.val(t)
        load(0); load(1)

        for t in range(TT):
            lw = [ld_done[t]] if src_sb is None else []
            r.emit('vector', lambda e, t=t: e.reduce_sum(st[:, 0:1], src_ap(t), axis=AXX), waits=lw)
            r.drain('vector')
            r.emit('vector', lambda e: e.tensor_scalar_mul(st[:, 2:3], st[:, 0:1], 1.0 / D))
            r.drain('vector')
            r.emit('vector', lambda e: e.tensor_mul(st[:, 3:4], st[:, 2:3], st[:, 2:3]))
            r.drain('vector')
            r.emit('scalar', lambda e, t=t: e.activation(scr[:], src_ap(t), AF.Square,
                                                         accum_out=st[:, 1:2]), waits=lw, inc=sA)
            r.drain('scalar')
            r.emit('vector', lambda e: e.scalar_tensor_tensor(
                st[:, 4:5], st[:, 1:2], 1.0 / D, st[:, 3:4], op0=ALU.mult, op1=ALU.subtract),
                waits=[W(sA)], inc=sV)
            r.drain('vector')
            r.emit('scalar', lambda e: e.activation(st[:, 5:6], st[:, 4:5], AF.Sqrt,
                                                    bias=CONST(C_EPS_LN)), waits=[W(sV)], inc=sA)
            r.drain('scalar')
            r.emit('vector', lambda e: e.reciprocal(st[:, 6:7], st[:, 5:6]), waits=[W(sA)])
            r.drain('vector')
            r.emit('vector', lambda e: e.scalar_tensor_tensor(
                st[:, 7:8], st[:, 2:3], -1.0, st[:, 6:7], op0=ALU.mult, op1=ALU.mult), inc=sV)
            r.drain('vector')
            r.emit('scalar', lambda e, t=t: e.activation(xn[t % 2][:], src_ap(t), AF.Identity,
                                                         scale=st[:, 6:7], bias=st[:, 7:8]),
                   waits=[W(sV)] + xnsl.acquire(t), inc=sA)
            xn_ready = W(sA)
            xsl.release(t, sA)
            load(t + 2)
            if amax_to is not None:
                dst = amax_to[:, 0:1] if t == 0 else amax_to[:, 1:2]
                r.emit('vector', lambda e, t=t, dst=dst: e.reduce_max(
                    dst, xn[t % 2][:], axis=AXX, apply_absolute_value=True), waits=[xn_ready])
                r.drain('vector')
                if t > 0:
                    r.emit('vector', lambda e: e.tensor_max(amax_to[:, 0:1], amax_to[:, 0:1],
                                                            amax_to[:, 1:2]))
                    r.drain('vector')
            for c in range(DC):
                k = t * DC + c
                r.emit('tensor', lambda e, c=c, k=k, t=t: e.transpose(
                    ptr[:, (k % 8) * 128:(k % 8) * 128 + 128],
                    xn[t % 2][:, c * 128:(c + 1) * 128], idb_sb[:]),
                    waits=[xn_ready] + trsl.acquire(k), inc=sPE)
                r.emit('vector', lambda e, c=c, k=k, t=t: e.tensor_copy(
                    xnT[:, c * SL + t * 128: c * SL + (t + 1) * 128],
                    ptr[:, (k % 8) * 128:(k % 8) * 128 + 128]), waits=[W(sPE)], inc=sV2)
                trsl.release(k, sV2)
            xnsl.release(t, sV2)
        if amax_to is not None:
            r.drain('vector')

    with contextlib.ExitStack() as stk:
        run_block('ln1', lambda r: emit_ln(r, 'ln1', stk, src_sb=None))

    with contextlib.ExitStack() as stk:
        wqd_bf = sb('wqd_bf', [128, DC * DQL], BF16, stk)
        wkvd_bf = sb('wkvd_bf', [128, DC * DKVL], BF16, stk)
        cqn = sb('cqn', [128, TT * DQL], BF16, stk)
        ckn = sb('ckn', [128, TT * DKVL], BF16, stk)
        sqs = sb('sqs_a', [128, 512], F32, stk)
        rsn = sb('rs_a', [128, 8], F32, stk)
        stg = [sb(f'p2a_stg{i}', [128, max(DQL, DKVL)], F32, stk) for i in range(2)]
        pcq = [psum(f'pcq{i}', [128, 512], F32, stk) for i in range(2)]
        ptr2 = psum('ptr2', [128, 1024], BF16, stk)

        def b_p2a(r):
            stsl = Slots(2)
            psl = Slots(2)
            trsl = Slots(1)
            wqst = stream('p2aw', 2)
            plan = [(wqd[c * 128:(c + 1) * 128, :], DQL, wqd_bf[:, c * DQL:(c + 1) * DQL])
                    for c in range(DC)] + \
                   [(wkvd[c * 128:(c + 1) * 128, :], DKVL, wkvd_bf[:, c * DKVL:(c + 1) * DKVL])
                    for c in range(DC)]
            wq_rdy = wk_rdy = None
            for i, (ap, w_, dst) in enumerate(plan):
                r.emit('sync', lambda e, ap=ap, w_=w_, i=i: e.dma_start(stg[i % 2][:, 0:w_], ap),
                       waits=stsl.acquire(i), inc=wqst.sem(i), incn=16)
                r.emit('vector', lambda e, w_=w_, dst=dst, i=i: e.tensor_copy(dst, stg[i % 2][:, 0:w_]),
                       waits=[wqst.val(i)], inc=sV)
                r.drain('vector')
                stsl.release(i, sV)
                if i == DC - 1: wq_rdy = W(sV)
                if i == 2 * DC - 1: wk_rdy = W(sV)

            def one(wrdy, WD, invc, wbf, cn, cT, nLC, po):
                for t in range(TT):
                    pq = pcq[t % 2]
                    for c in range(DC):
                        r.emit('tensor', lambda e, c=c, t=t, pq=pq, WD=WD, wbf=wbf: e.matmul(
                            pq[:, 0:WD], xnT[:, c * SL + t * 128: c * SL + (t + 1) * 128],
                            wbf[:, c * WD:(c + 1) * WD], start=(c == 0), stop=(c == DC - 1)),
                            waits=([wrdy] + psl.acquire(t)) if c == 0 else (),
                            inc=(sPE if c == DC - 1 else None))
                    r.emit('scalar', lambda e, pq=pq, WD=WD: e.activation(
                        sqs[:, 0:WD], pq[:, 0:WD], AF.Square, accum_out=rsn[:, po:po + 1]),
                        waits=[W(sPE)], inc=sA)
                    r.drain('scalar')
                    r.emit('scalar', lambda e: e.activation(
                        rsn[:, po + 1:po + 2], rsn[:, po:po + 1], AF.Sqrt,
                        scale=CONST(invc), bias=CONST(C_EPS_RMS)), inc=sA)
                    r.emit('vector', lambda e: e.reciprocal(rsn[:, po + 2:po + 3], rsn[:, po + 1:po + 2]),
                           waits=[W(sA)], inc=sV)
                    r.drain('vector')
                    r.emit('scalar', lambda e, pq=pq, WD=WD, t=t, cn=cn: e.activation(
                        cn[:, t * WD:(t + 1) * WD], pq[:, 0:WD], AF.Identity,
                        scale=rsn[:, po + 2:po + 3]), waits=[W(sV)], inc=sA)
                    psl.release(t, sA)
                    cn_rdy = W(sA)
                    for c in range(nLC):
                        k = t * nLC + c
                        r.emit('tensor', lambda e, c=c, t=t, k=k, cn=cn, WD=WD: e.transpose(
                            ptr2[:, (k % 8) * 128:(k % 8) * 128 + 128],
                            cn[:, t * WD + c * 128: t * WD + (c + 1) * 128], idb_sb[:]),
                            waits=[cn_rdy] + trsl.acquire(k), inc=sPE)
                        r.emit('vector', lambda e, c=c, t=t, k=k, cT=cT: e.tensor_copy(
                            cT[:, c * SL + t * 128: c * SL + (t + 1) * 128],
                            ptr2[:, (k % 8) * 128:(k % 8) * 128 + 128]), waits=[W(sPE)], inc=sV2)
                        trsl.release(k, sV2)
            one(wq_rdy, DQL, C_INV_DQL, wqd_bf, cqn, cqT, QLC, 0)
            one(wk_rdy, DKVL, C_INV_DKVL, wkvd_bf, ckn, ckT, KLC, 4)
        run_block('p2a', b_p2a)

    def rope_combine(r, h, prp, pswp, rsc, rt1, rt2, dstB_ap, dstB_w, prev):
        r.emit('scalar', lambda e: e.activation(rsc[0:64, 0:SL], prp[0:64, 0:SL], AF.Identity),
               waits=[W(sPE), (sPE, prev.get('swap', 0)), (sV, prev.get('comb', 0))], inc=sA)
        rsc_rdy = W(sA)
        for (qo, qw) in QC:
            r.emit('tensor', lambda e, qo=qo, qw=qw: e.matmul(
                pswp[0:64, qo:qo + qw], psw_sb[:], rsc[0:64, qo:qo + qw],
                start=True, stop=True),
                waits=[rsc_rdy, (sV, prev.get('comb2', 0))],
                inc=(sPE if qo + qw == SL else None))
        prev['swap'] = sPE.v
        r.emit('vector', lambda e: e.tensor_mul(rt1[0:64, :], rsc[0:64, 0:SL], c2_sb[0:64, :]),
               waits=[rsc_rdy])
        r.drain('vector')
        r.emit('vector', lambda e: e.tensor_mul(rt2[0:64, :], pswp[0:64, 0:SL], s2_sb[0:64, :]),
               waits=[W(sPE)], inc=sV)
        prev['comb2'] = sV.v
        r.drain('vector')
        r.emit('vector', lambda e, dstB_ap=dstB_ap: e.tensor_add(dstB_ap, rt1[0:64, :], rt2[0:64, :]),
               waits=list(dstB_w), inc=sV)
        prev['comb'] = sV.v
        r.drain('vector')
        return W(sV)

    def up_rope_phase(r, stk, tag, w_up, w_rope, nLC, cT_src, fold_sb, dstA_fn, dstB_fn, sink_fn=None):
        stgq = [sb(f'{tag}_stg{i}', [128, 128], F32, stk) for i in range(2)]
        wtb = [sb(f'{tag}_wt{i}', [128, 128], BF16, stk) for i in range(2)]
        rsc = sb(f'{tag}_rsc', [64, SL], BF16, stk)
        rt1 = sb(f'{tag}_rt1', [64, SL], F32, stk)
        rt2 = sb(f'{tag}_rt2', [64, SL], F32, stk)
        pqa = [psum(f'{tag}_pqa0', [128, 1024], F32, stk)]
        prp = psum(f'{tag}_prp', [128, 1024], F32, stk)
        pswp = psum(f'{tag}_psw', [128, 1024], F32, stk)

        plan = []
        for h in range(H):
            for c in range(nLC):
                plan.append((w_up[c * 128:(c + 1) * 128, h * HD:(h + 1) * HD], HD, fold_sb[:, c:c + 1]))
            for c in range(DC):
                plan.append((w_rope[c * 128:(c + 1) * 128, h * RD:(h + 1) * RD], RD, None))
        cast_done = {}
        stsl = Slots(2)
        wtsl = Slots(2)
        wst = stream(f'{tag}w', 2)
        cursor = [0]
        def load_next():
            i = cursor[0]
            if i >= len(plan):
                return
            cursor[0] += 1
            ap, w_, fold = plan[i]
            r.emit('sync', lambda e, ap=ap, w_=w_, i=i: e.dma_start(stgq[i % 2][:, 0:w_], ap),
                   waits=stsl.acquire(i), inc=wst.sem(i), incn=16)
            if fold is not None:
                r.emit('vector', lambda e, w_=w_, i=i, fold=fold: e.tensor_scalar_mul(
                    wtb[i % 2][:, 0:w_], stgq[i % 2][:, 0:w_], fold),
                    waits=[wst.val(i)] + wtsl.acquire(i), inc=sV)
            else:
                r.emit('vector', lambda e, w_=w_, i=i: e.tensor_copy(
                    wtb[i % 2][:, 0:w_], stgq[i % 2][:, 0:w_]),
                    waits=[wst.val(i)] + wtsl.acquire(i), inc=sV)
            r.drain('vector')
            stsl.release(i, sV)
            cast_done[i] = sV.v
        load_next(); load_next()

        pqsl = Slots(1)
        prev = {}
        for h in range(H):
            base = h * (nLC + DC)
            pq = pqa[0]
            for c in range(nLC):
                i = base + c
                mw = [(sV, cast_done[i])] + (pqsl.acquire(h) if c == 0 else [])
                for (qo, qw) in QC:
                    r.emit('tensor', lambda e, c=c, i=i, qo=qo, qw=qw, pq=pq: e.matmul(
                        pq[:, qo:qo + qw], wtb[i % 2][:, 0:HD],
                        cT_src[:, c * SL + qo: c * SL + qo + qw],
                        start=(c == 0), stop=(c == nLC - 1)), waits=mw,
                        inc=(sPE if qo + qw == SL else None))
                wtsl.release(i, sPE)
                load_next()
            da, daw = dstA_fn(h)
            r.emit('scalar', lambda e, da=da, pq=pq: e.activation(da, pq[:, 0:SL], AF.Identity),
                   waits=[W(sPE)] + list(daw), inc=sA)
            pqsl.release(h, sA)
            a_rdy = W(sA)
            for c in range(DC):
                i = base + nLC + c
                mw = [(sV, cast_done[i])] + ([(sPE, prev.get('swap', 0))] if c == 0 else [])
                for (qo, qw) in QC:
                    r.emit('tensor', lambda e, c=c, i=i, qo=qo, qw=qw: e.matmul(
                        prp[0:64, qo:qo + qw], wtb[i % 2][:, 0:RD],
                        xnT[:, c * SL + qo: c * SL + qo + qw],
                        start=(c == 0), stop=(c == DC - 1)), waits=mw,
                        inc=(sPE if qo + qw == SL else None))
                wtsl.release(i, sPE)
                load_next()
            db_ap, db_w = dstB_fn(h)
            b_rdy = rope_combine(r, h, prp, pswp, rsc, rt1, rt2, db_ap, db_w, prev)
            if sink_fn is not None:
                sink_fn(r, h, a_rdy, b_rdy)

    with contextlib.ExitStack() as stk:
        sqsb = sb('p2b_sqs', [128, SL], F32, stk)
        qrow = sb('p2b_qrow', [1, SL], F32, stk)
        pq1 = psum('p2b_pq1', [128, 1024], F32, stk)
        sqprev = {}

        def q_sink(r, h, a_rdy, b_rdy):
            r.emit('scalar', lambda e, h=h: e.activation(
                sqsb[:, 0:SL], qa[:, h * SL:(h + 1) * SL], AF.Square),
                waits=[a_rdy, (sPE, sqprev.get('mm', 0))], inc=sA)
            r.drain('scalar')
            for (qo, qw) in QC:
                r.emit('tensor', lambda e, qo=qo, qw=qw: e.matmul(
                    pq1[0:1, qo:qo + qw], onesf[:], sqsb[:, qo:qo + qw],
                    start=True, stop=False),
                    waits=[W(sA), (sA, sqprev.get('evac', 0))],
                    inc=(sPE if qo + qw == SL else None))
            r.emit('scalar', lambda e, h=h: e.activation(
                sqsb[0:64, 0:SL], qb[0:64, h * SL:(h + 1) * SL], AF.Square),
                waits=[b_rdy, W(sPE)], inc=sA)
            r.drain('scalar')
            for (qo, qw) in QC:
                r.emit('tensor', lambda e, qo=qo, qw=qw: e.matmul(
                    pq1[0:1, qo:qo + qw], onesf[0:64, :], sqsb[0:64, qo:qo + qw],
                    start=False, stop=True), waits=[W(sA)],
                    inc=(sPE if qo + qw == SL else None))
            sqprev['mm'] = sPE.v
            r.emit('scalar', lambda e: e.activation(qrow[0:1, 0:SL], pq1[0:1, 0:SL], AF.Identity),
                   waits=[W(sPE), (sQSQ, sQSQ.v)], inc=sA)
            sqprev['evac'] = sA.v
            r.emit('gpsimd', lambda e, h=h: e.dma_start(qsq_sb[h:h + 1, :], qrow[0:1, 0:SL]),
                   waits=[W(sA)], inc=sQSQ, incn=16)

        def b_p2b(r):
            up_rope_phase(r, stk, 'p2b', wqu, wqr, QLC, cqT, qnw_sb,
                          lambda h: (qa[:, h * SL:(h + 1) * SL], []),
                          lambda h: (qb[0:64, h * SL:(h + 1) * SL], []), q_sink)
            r.emit('gpsimd', lambda e: e.engine_nop(), waits=[W(sQSQ)])
        run_block('p2b', b_p2b)

    kv_store_sems = []
    ws_ready = [None]
    knar_ready = [None]
    with contextlib.ExitStack() as stk:
        knm_loc = sb('knm_loc', [1, H], F32, stk)
        kaS = [sb(f'kaS{i}', [128, SL], BF16, stk) for i in range(2)]
        kbS = [sb(f'kbS{i}', [65, SL], BF16, stk) for i in range(2)]
        sqsk = sb('p2c_sqs', [128, SL], F32, stk)
        krow = sb('p2c_krow', [1, SL], BF16, stk)
        pkn = psum('p2c_pkn', [128, 1024], F32, stk)
        kasl = Slots(2); kbsl = Slots(2); krsl = Slots(1)
        kast = stream('kaw', 2); kbst = stream('kbw', 2); krst = stream('krw', 1)
        kv_store_sems.extend(kast.sems + kbst.sems + krst.sems)
        kprev = {}

        def k_sink(r, h, a_rdy, b_rdy):
            r.emit('sync', lambda e, h=h: e.dma_start(
                dview(kv_in, OFF_KA + h * 128 * SL, [128, SL]), kaS[h % 2][:]),
                waits=[a_rdy], inc=kast.sem(h), incn=16)
            kasl.release(h, kast.sem(h))
            r.emit('sync', lambda e, h=h: e.dma_start(
                dview(kv_in, OFF_KB + h * 65 * SL, [65, SL]), kbS[h % 2][:]),
                waits=[b_rdy], inc=kbst.sem(h), incn=16)
            kbsl.release(h, kbst.sem(h))
            r.emit('scalar', lambda e, h=h: e.activation(
                sqsk[:, 0:SL], kaS[h % 2][:], AF.Square),
                waits=[a_rdy, (sPE, kprev.get('mm', 0))], inc=sA)
            r.drain('scalar')
            for (qo, qw) in QC:
                r.emit('tensor', lambda e, qo=qo, qw=qw: e.matmul(
                    pkn[0:1, qo:qo + qw], onesf[:], sqsk[:, qo:qo + qw],
                    start=True, stop=False),
                    waits=[W(sA), (sA, kprev.get('evac', 0))],
                    inc=(sPE if qo + qw == SL else None))
            r.emit('scalar', lambda e, h=h: e.activation(
                sqsk[0:64, 0:SL], kbS[h % 2][0:64, :], AF.Square),
                waits=[b_rdy, W(sPE)], inc=sA)
            r.drain('scalar')
            for (qo, qw) in QC:
                r.emit('tensor', lambda e, qo=qo, qw=qw: e.matmul(
                    pkn[0:1, qo:qo + qw], onesf[0:64, :], sqsk[0:64, qo:qo + qw],
                    start=False, stop=True), waits=[W(sA)],
                    inc=(sPE if qo + qw == SL else None))
            kprev['mm'] = sPE.v
            r.emit('scalar', lambda e: e.activation(krow[0:1, 0:SL], pkn[0:1, 0:SL], AF.Identity),
                   waits=[W(sPE)] + krsl.acquire(h), inc=sA)
            kprev['evac'] = sA.v
            r.emit('sync', lambda e, h=h: e.dma_start(
                dview(kv_in, OFF_KN + h * SL, [1, SL]), krow[0:1, :]), waits=[W(sA)],
                inc=krst.sem(0), incn=16)
            r.emit('vector', lambda e, h=h: e.reduce_max(
                knm_loc[0:1, h:h + 1], krow[0:1, :], axis=AXX), waits=[W(sA)], inc=sV)
            r.drain('vector')
            krsl.release(h, krst.sem(0))
            kprev['knm'] = sV.v

        def b_p2c(r):
            r.emit('vector', lambda e: e.memset(kbS[0][64:65, :], 1.0))
            r.emit('vector', lambda e: e.memset(kbS[1][64:65, :], 1.0))
            r.drain('vector')
            up_rope_phase(r, stk, 'p2c', wku, wkr, KLC, ckT, kvnw_sb,
                          lambda h: (kaS[h % 2][:], kasl.acquire(h)),
                          lambda h: (kbS[h % 2][0:64, :], kbsl.acquire(h)), k_sink)
            w1 = aux(r, 'sync', lambda e: e.dma_start(red3_i.ap(), knm_loc[0:1, :]),
                     extra=[(sV, kprev['knm'])])
            r.emit('gpsimd', lambda e: e.collective_compute(
                "AllReduce", ALU.max, replica_groups=[list(range(NCORES))],
                ins=[red3_i.ap().opt()], outs=[red3_o.ap().opt()]), waits=[w1], inc=sCC)
            knar_ready[0] = W(sCC)
            r.emit('sync', lambda e: e.nop(),
                   waits=[(s, s.v) for s in (kast.sems + kbst.sems + krst.sems)])
        run_block('p2c', b_p2c)

    with contextlib.ExitStack() as stk2:
        wvu_bf = sb('wvu_bf', [128, KLC * H * HD], BF16, stk2)
        stgv = [sb(f'p2v_stg{i}', [128, 512], F32, stk2) for i in range(2)]
        vSl = [sb(f'p2v_v{i}', [128, H * 129], BF16, stk2) for i in range(2)]
        fstg = [sb(f'p2v_f{i}', [128, D], BF16, stk2) for i in range(2)]
        famx = sb('p2v_famx', [128, 2], F32, stk2)
        pv = psum('p2v_pv', [128, 512], F32, stk2)
        ptr3 = psum('p2v_ptr3', [128, 512], F32, stk2)

        def b_p2v(r):
            np5 = (H * HD) // 512
            stsl = Slots(2)
            wvst = stream('wvw', 2)
            vstt = stream('vsw', 2)
            fwst = stream('fww', 2)
            i = 0
            for c in range(KLC):
                for p5 in range(np5):
                    ap = wvu[c * 128:(c + 1) * 128, p5 * 512:(p5 + 1) * 512]
                    r.emit('sync', lambda e, ap=ap, i=i: e.dma_start(stgv[i % 2][:], ap),
                           waits=stsl.acquire(i), inc=wvst.sem(i), incn=16)
                    r.emit('vector', lambda e, c=c, p5=p5, i=i: e.tensor_scalar_mul(
                        wvu_bf[:, c * H * HD + p5 * 512: c * H * HD + (p5 + 1) * 512],
                        stgv[i % 2][:], kvnw_sb[:, c:c + 1]), waits=[wvst.val(i)], inc=sV)
                    r.drain('vector')
                    stsl.release(i, sV)
                    i += 1
            wv_rdy = W(sV)
            vsl = Slots(2)
            evac_v = 0
            for t in range(TT):
                r.emit('vector', lambda e, t=t: e.memset(vSl[t % 2][:], 1.0),
                       waits=vsl.acquire(t))
                r.drain('vector')
                r.emit('vector', lambda e: e.engine_nop(), inc=sV)
                ones_rdy = W(sV)
                for n5 in range(np5):
                    for c in range(KLC):
                        r.emit('tensor', lambda e, t=t, n5=n5, c=c: e.matmul(
                            pv[:], ckT[:, c * SL + t * 128: c * SL + (t + 1) * 128],
                            wvu_bf[:, c * H * HD + n5 * 512: c * H * HD + (n5 + 1) * 512],
                            start=(c == 0), stop=(c == KLC - 1)),
                            waits=([wv_rdy, (sA, evac_v)] if c == 0 else ()),
                            inc=(sPE if c == KLC - 1 else None))
                    for hh in range(4):
                        h = n5 * 4 + hh
                        r.emit('scalar', lambda e, t=t, h=h, hh=hh: e.activation(
                            vSl[t % 2][:, h * 129: h * 129 + 128],
                            pv[:, hh * 128:(hh + 1) * 128], AF.Identity),
                            waits=[W(sPE), ones_rdy], inc=sA)
                    evac_v = sA.v
                for h in range(H):
                    r.emit('sync', lambda e, t=t, h=h: e.dma_start(
                        dview(kv_in, OFF_V + h * SL * 129 + t * 128 * 129, [128, 129]),
                        vSl[t % 2][:, h * 129:(h + 1) * 129]), waits=[W(sA)],
                        inc=vstt.sem(t), incn=16)
                vsl.release(t, vstt.sem(t))
            agw = [(s, s.v) for s in (kv_store_sems + vstt.sems)]
            r.emit('gpsimd', lambda e: e.collective_compute(
                "AllGather", ALU.bypass, replica_groups=[list(range(NCORES))],
                ins=[kv_in.ap().opt()], outs=[kv_out.ap().opt()]), waits=agw, inc=sCC)
            fsl = Slots(2)
            for c in range(DC):
                r.emit('sync', lambda e, c=c: e.dma_start(
                    fstg[c % 2][:], fw[c * 128:(c + 1) * 128, :]),
                    waits=fsl.acquire(c), inc=fwst.sem(c), incn=16)
                if c == 0:
                    r.emit('vector', lambda e: e.reduce_max(
                        famx[:, 0:1], fstg[0][:], axis=AXX, apply_absolute_value=True),
                        waits=[fwst.val(c)], inc=sV)
                else:
                    r.emit('vector', lambda e, c=c: e.reduce_max(
                        famx[:, 1:2], fstg[c % 2][:], axis=AXX, apply_absolute_value=True),
                        waits=[fwst.val(c)])
                    r.drain('vector')
                    r.emit('vector', lambda e: e.tensor_max(famx[:, 0:1], famx[:, 0:1],
                                                            famx[:, 1:2]), inc=sV)
                r.drain('vector')
                fsl.release(c, sV)
            r.emit('tensor', lambda e: e.transpose(ptr3[0:1, 0:128], famx[:, 0:1], idf_sb[:]),
                   waits=[W(sV)], inc=sPE)
            r.emit('vector', lambda e: e.reduce_max(arow[0:1, 0:1], ptr3[0:1, 0:128], axis=AXX),
                   waits=[W(sPE)])
            r.drain('vector')
            r.emit('vector', lambda e: e.reciprocal(arow[0:1, 1:2], arow[0:1, 0:1]), inc=sV)
            r.drain('vector')
            aux(r, 'sync', lambda e: e.dma_start(bounce.ap()[0:2], arow[0:1, 0:2]),
                extra=[W(sV)])
            aux(r, 'sync', lambda e: e.dma_start(
                ws_sb[:, 0:1], bass.AP(bounce, 0, [[0, 128], [1, 1]])))
            aux(r, 'sync', lambda e: e.dma_start(
                ws_sb[:, 1:2], bass.AP(bounce, 1, [[0, 128], [1, 1]])))
            ws_ready[0] = (sAXS, sAXS.v)
        run_block('p2v', b_p2v)

    proj_stk.close()

    with contextlib.ExitStack() as stk:
        knm = sb('knm', [H, 2], F32, stk)
        stmp = sb('stmp', [H, SL], F32, stk)
        stb = sb('stb', [H, SL], BF16, stk)

        def b_stab(r):
            kngst = stream('kng', 1)
            r.emit('gpsimd', lambda e: e.dma_start(
                knm[:, 0:1], bass.AP(red3_o, 0, [[1, H], [1, 1]])),
                waits=[knar_ready[0]], inc=kngst.sem(0), incn=16)
            r.emit('vector', lambda e: e.engine_nop(), waits=[kngst.val(0)], inc=sV)
            r.emit('vector', lambda e: e.tensor_scalar_mul(stmp[:], qsq_sb[0:H, :], knm[:, 0:1]),
                   waits=[W(sV), (sQSQ, 16 * H)], inc=sV)
            r.drain('vector')
            r.emit('scalar', lambda e: e.activation(stmp[:], stmp[:], AF.Sqrt), waits=[W(sV)], inc=sA)
            r.emit('vector', lambda e: e.tensor_scalar_mul(stb[:], stmp[:], -1.0), waits=[W(sA)], inc=sV)
            r.drain('vector')
            for h in range(H):
                r.emit('gpsimd', lambda e, h=h: e.dma_start(
                    qb[64:65, h * SL:(h + 1) * SL], stb[h:h + 1, :]), waits=[W(sV)], inc=sSTB, incn=16)
            r.emit('gpsimd', lambda e: e.engine_nop(), waits=[(sSTB, 16 * H)])
        run_block('stab', b_stab)

    NSLAB = 3
    NP = 3
    with contextlib.ExitStack() as stk:
        kaA = [sb(f'at_ka{i}', [128, SL], BF16, stk) for i in range(NSLAB)]
        kbA = [sb(f'at_kb{i}', [65, SL], BF16, stk) for i in range(NSLAB)]
        vA = [sb(f'at_v{i}', [128, KT * 129], BF16, stk) for i in range(NSLAB)]
        pS = [sb(f'at_p{i}', [128, SL], BF16, stk) for i in range(NP)]
        aosb = [sb(f'at_ao{i}', [128, 128], BF16, stk) for i in range(2)]
        rq = sb('at_rq', [128, 8], F32, stk)
        psc = [psum(f'at_sc{i}', [128, 1024], F32, stk) for i in range(2)]
        po = [psum(f'at_po{i}', [128, 512], F32, stk) for i in range(3)]
        ptra = psum('at_ptra', [128, 1024], BF16, stk)

        slabs = [(h, src) for h in range(H) for src in range(NCORES)]

        def b_attn(r):
            kv_done = {}
            kvsl = Slots(NSLAB)
            kvst = stream('kvl', NSLAB)
            def load_slab(i):
                if i >= len(slabs):
                    return
                h, src = slabs[i]
                base = src * LBLK
                aw = kvsl.acquire(i) + [W(sCC)]
                r.emit('gpsimd', lambda e, h=h, src=src, i=i: e.dma_start(
                    kaA[i % NSLAB][:], dview(kv_out, base + OFF_KA + h * 128 * SL, [128, SL])),
                    waits=aw, inc=kvst.sem(i), incn=16)
                r.emit('gpsimd', lambda e, h=h, src=src, i=i: e.dma_start(
                    kbA[i % NSLAB][:], dview(kv_out, base + OFF_KB + h * 65 * SL, [65, SL])),
                    inc=kvst.sem(i), incn=16)
                for kt in range(KT):
                    r.emit('gpsimd', lambda e, h=h, src=src, i=i, kt=kt: e.dma_start(
                        vA[i % NSLAB][:, kt * 129:(kt + 1) * 129],
                        dview(kv_out, base + OFF_V + h * SL * 129 + kt * 128 * 129, [128, 129])),
                        inc=kvst.sem(i), incn=16)
                kv_done[i] = kvst.val(i)
            for i in range(min(NSLAB, len(slabs))):
                load_slab(i)

            p_done = {}
            pv_done = {}
            ao_done = {}
            aot_done = {}
            tr_done = {}
            j = 0
            for i, (h, src) in enumerate(slabs):
                jh0 = (i % NCORES) * KT
                for kt in range(KT):
                    jh = jh0 + kt
                    qkw = [kv_done[i]]
                    if j >= 2:
                        qkw.append((sP, p_done[j - 2]))
                    if jh == 0:
                        qkw.append((sSTB, 16 * H))
                    for (qo, qw) in QC:
                        r.emit('tensor', lambda e, i=i, kt=kt, qo=qo, qw=qw, h=h, j=j: e.matmul(
                            psc[j % 2][:, qo:qo + qw],
                            kaA[i % NSLAB][:, kt * 128:(kt + 1) * 128],
                            qa[:, h * SL + qo: h * SL + qo + qw], start=True, stop=False),
                            waits=qkw if qo == 0 else ())
                        r.emit('tensor', lambda e, i=i, kt=kt, qo=qo, qw=qw, h=h, j=j: e.matmul(
                            psc[j % 2][:, qo:qo + qw],
                            kbA[i % NSLAB][:, kt * 128:(kt + 1) * 128],
                            qb[:, h * SL + qo: h * SL + qo + qw], start=False, stop=True),
                            inc=(sQK if qo + qw == SL else None))
                    qk_v = sQK.v
                    ew = [(sQK, qk_v)]
                    if j >= NP:
                        ew.append((sPV, pv_done[j - NP]))
                    r.emit('scalar', lambda e, j=j: e.activation(
                        pS[j % NP][:, 0:SL], psc[j % 2][:, 0:SL], AF.Exp,
                        scale=CONST(C_SCALE_ATT)), waits=ew, inc=sP)
                    p_done[j] = sP.v
                    pvw = [(sP, p_done[j])]
                    if jh == 0 and h >= 1:
                        pvw.append((sV, ao_done[h - 1]))
                    for qt in range(TT):
                        b, o = qt // 3, (qt % 3) * 129
                        st_ = (jh == 0 and qt % 3 == 0)
                        sp_ = (jh == JH - 1 and (qt % 3 == 2 or qt == TT - 1))
                        r.emit('tensor', lambda e, i=i, kt=kt, qt=qt, b=b, o=o, st_=st_, sp_=sp_, j=j: e.matmul(
                            po[b][:, o:o + 129],
                            pS[j % NP][:, qt * 128:(qt + 1) * 128],
                            vA[i % NSLAB][:, kt * 129:(kt + 1) * 129], start=st_, stop=sp_),
                            waits=pvw if qt == 0 else (),
                            inc=(sPV if qt == TT - 1 else None))
                    pv_done[j] = sPV.v
                    j += 1
                if i + NSLAB < len(slabs):
                    kvsl.release(i, sPV)
                    load_slab(i + NSLAB)
                if (i % NCORES) == NCORES - 1:
                    for qt in range(TT):
                        b, o = qt // 3, (qt % 3) * 129
                        r.emit('vector', lambda e, qt=qt, b=b, o=o: e.reciprocal(
                            rq[:, qt:qt + 1], po[b][:, o + 128:o + 129]), waits=[W(sPV)])
                    r.drain('vector')
                    for qt in range(TT):
                        b, o = qt // 3, (qt % 3) * 129
                        vw = [(sPE, tr_done.get((h, qt - 2), 0))]
                        r.emit('vector', lambda e, qt=qt, b=b, o=o: e.tensor_scalar_mul(
                            aosb[qt % 2][:], po[b][:, o:o + 128], rq[:, qt:qt + 1]),
                            waits=vw, inc=sV)
                        sc_v = sV.v
                        r.emit('tensor', lambda e, qt=qt, h=h: e.transpose(
                            ptra[:, (qt % 2) * 128:(qt % 2) * 128 + 128], aosb[qt % 2][:], idb_sb[:]),
                            waits=[(sV, sc_v), (sV2, aot_done.get((h, qt - 1), 0))], inc=sPE)
                        tr_done[(h, qt)] = sPE.v
                        r.emit('vector', lambda e, qt=qt, h=h: e.tensor_copy(
                            qa[:, h * SL + qt * 128: h * SL + (qt + 1) * 128],
                            ptra[:, (qt % 2) * 128:(qt % 2) * 128 + 128]),
                            waits=[(sPE, tr_done[(h, qt)])], inc=sV2)
                        aot_done[(h, qt)] = sV2.v
                    ao_done[h] = sV.v
        run_block('attn', b_attn)
    qbspan.close()

    wq8T = sb('wq8T', [128, DC * D], F8)
    xres = sb('xres', [128, TT * D], F32)
    with contextlib.ExitStack() as stk:
        wstg = [sb(f'wo_stg{i}', [128, 512], F32, stk) for i in range(2)]
        wob = sb('wo_b', [128, DC * 512], BF16, stk)
        pw = [psum(f'wo_pw{i}', [128, 512], F32, stk) for i in range(2)]

        def b_wout(r):
            for t in range(TT):
                r.emit('sync', lambda e, t=t: e.dma_start(
                    xres[:, t * D:(t + 1) * D], x_in[t * 128:(t + 1) * 128, :]), inc=sXR, incn=16)
            xres_ld = (sXR, 16 * TT)
            stsl = Slots(2)
            pwsl = Slots(2)
            wost = stream('wow', 2)
            li = 0
            add_done = {}
            mm_prev = 0
            for n in range(ND5):
                cast_done = {}
                for hc in range(DC):
                    ap = wout_d[hc * 128:(hc + 1) * 128, n * 512:(n + 1) * 512]
                    r.emit('sync', lambda e, ap=ap, li=li: e.dma_start(wstg[li % 2][:], ap),
                           waits=stsl.acquire(li), inc=wost.sem(li), incn=16)
                    r.emit('vector', lambda e, hc=hc, li=li: e.tensor_copy(
                        wob[:, hc * 512:(hc + 1) * 512], wstg[li % 2][:]),
                        waits=[wost.val(li), (sPE, mm_prev)], inc=sV)
                    r.drain('vector')
                    stsl.release(li, sV)
                    cast_done[hc] = sV.v
                    li += 1
                for t in range(TT):
                    k = n * TT + t
                    for hc in range(DC):
                        mw = []
                        if hc == 0:
                            mw = [(sV, cast_done[DC - 1])] + pwsl.acquire(k)
                        r.emit('tensor', lambda e, t=t, hc=hc, k=k: e.matmul(
                            pw[k % 2][:], qa[:, hc * SL + t * 128: hc * SL + (t + 1) * 128],
                            wob[:, hc * 512:(hc + 1) * 512],
                            start=(hc == 0), stop=(hc == DC - 1)), waits=mw,
                            inc=(sPE if hc == DC - 1 else None))
                    r.emit('vector', lambda e, t=t, n=n, k=k: e.tensor_add(
                        xres[:, t * D + n * 512: t * D + (n + 1) * 512],
                        xres[:, t * D + n * 512: t * D + (n + 1) * 512], pw[k % 2][:]),
                        waits=[W(sPE), xres_ld], inc=sV)
                    r.drain('vector')
                    pwsl.release(k, sV)
                mm_prev = sPE.v
        run_block('wout', b_wout)
    qspan.close()

    bias_bc = sb('bias_bc', [128, D], BF16)
    with contextlib.ExitStack() as stk:
        fstg2 = [sb(f'fw2_{i}', [128, D], BF16, stk) for i in range(2)]
        ptrf = psum('fw2_ptr', [128, 1024], BF16, stk)

        def b_fwq(r):
            fwst2 = stream('fw2', 2)
            r.emit('sync', lambda e: e.dma_start(bias_bc[:], bass.AP(fb, 0, [[0, 128], [1, D]])),
                   inc=sBIA, incn=16)
            fsl = Slots(2)
            trsl = Slots(1)
            for c in range(DC):
                r.emit('sync', lambda e, c=c: e.dma_start(fstg2[c % 2][:], fw[c * 128:(c + 1) * 128, :]),
                       waits=fsl.acquire(c), inc=fwst2.sem(c), incn=16)
                ld_v = fwst2.val(c)
                for cc in range(DC):
                    k = c * DC + cc
                    r.emit('tensor', lambda e, c=c, cc=cc, k=k: e.transpose(
                        ptrf[:, (k % 8) * 128:(k % 8) * 128 + 128],
                        fstg2[c % 2][:, cc * 128:(cc + 1) * 128], idb_sb[:]),
                        waits=[ld_v] + trsl.acquire(k), inc=sPE)
                    r.emit('scalar', lambda e, c=c, cc=cc, k=k: e.activation(
                        wq8T[:, cc * D + c * 128: cc * D + (c + 1) * 128],
                        ptrf[:, (k % 8) * 128:(k % 8) * 128 + 128], AF.Identity,
                        scale=ws_sb[:, 1:2]), waits=[W(sPE), ws_ready[0]], inc=sA)
                    trsl.release(k, sA)
                fsl.release(c, sA)
        run_block('fwq', b_fwq)

    def emit_amax_ar(r, tag, amx, red_i, red_o, ptams):
        r.emit('tensor', lambda e: e.transpose(ptams[0:1, 0:128], amx[:, 0:1], idf_sb[:]),
               waits=[W(sV)], inc=sPE)
        r.emit('vector', lambda e: e.reduce_max(arow[0:1, 8:9], ptams[0:1, 0:128], axis=AXX),
               waits=[W(sPE)], inc=sV)
        r.drain('vector')
        w1 = aux(r, 'sync', lambda e: e.dma_start(red_i.ap(), arow[0:1, 8:9]), extra=[W(sV)])
        r.emit('gpsimd', lambda e: e.collective_compute(
            "AllReduce", ALU.max, replica_groups=[list(range(NCORES))],
            ins=[red_i.ap().opt()], outs=[red_o.ap().opt()]), waits=[w1], inc=sCC)
        w2 = aux(r, 'gpsimd', lambda e: e.dma_start(arow[0:1, 9:10], red_o.ap()), extra=[W(sCC)])
        r.emit('vector', lambda e: e.reciprocal(arow[0:1, 10:11], arow[0:1, 9:10]),
               waits=[w2], inc=sV)
        r.drain('vector')
        aux(r, 'sync', lambda e: e.dma_start(bounce.ap()[4:6], arow[0:1, 9:11]), extra=[W(sV)])
        aux(r, 'sync', lambda e: e.dma_start(ws_sb[:, 2:3], bass.AP(bounce, 5, [[0, 128], [1, 1]])))
        w3 = aux(r, 'sync', lambda e: e.dma_start(ws_sb[:, 4:5], bass.AP(bounce, 4, [[0, 128], [1, 1]])))
        r.emit('vector', lambda e: e.tensor_tensor(ws_sb[:, 3:4], ws_sb[:, 0:1], ws_sb[:, 4:5],
                                                   ALU.mult), waits=[w3], inc=sV)
        r.drain('vector')

    def emit_ffn(r, tag, stk):
        xq = sb(f'{tag}_xq', [128, DC * SL], F8, stk)
        tmp = [sb(f'{tag}_tmp{i}', [128, 512], BF16, stk) for i in range(2)]
        pf = [psum(f'{tag}_pf{i}', [128, 512], F32, stk) for i in range(2)]
        for c in range(DC):
            r.emit('scalar', lambda e, c=c: e.activation(
                xq[:, c * SL:(c + 1) * SL], xnT[:, c * SL:(c + 1) * SL], AF.Identity,
                scale=ws_sb[:, 2:3]), waits=[W(sV), W(sV2), W(sAXS)], inc=sA)
        xq_rdy = W(sA)
        pfsl = Slots(2)
        tmsl = Slots(2)
        k = 0
        for t in range(TT):
            for n in range(ND5):
                for c in range(DC):
                    mw = ([xq_rdy] + pfsl.acquire(k)) if c == 0 else ()
                    r.emit('tensor', lambda e, t=t, n=n, c=c, k=k: e.matmul(
                        pf[k % 2][:], xq[:, c * SL + t * 128: c * SL + (t + 1) * 128],
                        wq8T[:, c * D + n * 512: c * D + (n + 1) * 512],
                        start=(c == 0), stop=(c == DC - 1)), waits=mw,
                        inc=(sPE if c == DC - 1 else None))
                r.emit('scalar', lambda e, k=k: e.activation(
                    tmp[k % 2][:], pf[k % 2][:], AF.Identity, scale=ws_sb[:, 3:4]),
                    waits=[W(sPE)] + tmsl.acquire(k), inc=sA)
                pfsl.release(k, sA)
                r.emit('vector', lambda e, k=k, n=n: e.tensor_add(
                    tmp[k % 2][:], tmp[k % 2][:], bias_bc[:, n * 512:(n + 1) * 512]),
                    waits=[W(sA), (sBIA, 16)])
                r.drain('vector')
                r.emit('vector', lambda e, t=t, n=n, k=k: e.tensor_add(
                    xres[:, t * D + n * 512: t * D + (n + 1) * 512],
                    xres[:, t * D + n * 512: t * D + (n + 1) * 512], tmp[k % 2][:]), inc=sV)
                r.drain('vector')
                tmsl.release(k, sV)
                k += 1

    with contextlib.ExitStack() as stk:
        amx2 = sb('amx2', [128, 2], F32, stk)
        ptam = psum('ptam', [128, 512], F32, stk)
        run_block('ln2', lambda r: emit_ln(r, 'ln2', stk, src_sb=xres, amax_to=amx2))
        run_block('ar1', lambda r: emit_amax_ar(r, 'ar1', amx2, red1_i, red1_o, ptam))
        run_block('ffn1', lambda r: emit_ffn(r, 'ffn1', stk))

    with contextlib.ExitStack() as stk:
        run_block('ln3', lambda r: emit_ln(r, 'ln3', stk, src_sb=xres))
    with contextlib.ExitStack() as stk:
        wx_bf = sb('wx_bf', [128, DC * SSP], BF16, stk)
        stgx = [sb(f'sx_stg{i}', [128, SSP], F32, stk) for i in range(2)]
        ssro = sb('ssro', [128, SL], F32, stk)
        pss = [psum(f'pss{i}', [128, 512], F32, stk) for i in range(2)]

        def b_ssmp(r):
            stsl = Slots(2)
            wxst = stream('wxw', 2)
            ssst = stream('ssw', 1)
            for c in range(DC):
                r.emit('sync', lambda e, c=c: e.dma_start(stgx[c % 2][:], wx[c * 128:(c + 1) * 128, :]),
                       waits=stsl.acquire(c), inc=wxst.sem(c), incn=16)
                r.emit('vector', lambda e, c=c: e.tensor_copy(
                    wx_bf[:, c * SSP:(c + 1) * SSP], stgx[c % 2][:]), waits=[wxst.val(c)], inc=sV)
                r.drain('vector')
                stsl.release(c, sV)
            w_rdy = W(sV)
            rows = [(0, SN), (SN, SN), (2 * SN, SH)]
            k = 0
            pssl = Slots(2)
            dma_prev = None
            for (ro, rh) in rows:
                for (qo, qw) in QC:
                    for c in range(DC):
                        mw = ([w_rdy] + pssl.acquire(k)) if c == 0 else ()
                        r.emit('tensor', lambda e, ro=ro, rh=rh, qo=qo, qw=qw, c=c, k=k: e.matmul(
                            pss[k % 2][0:rh, 0:qw],
                            wx_bf[:, c * SSP + ro: c * SSP + ro + rh],
                            xnT[:, c * SL + qo: c * SL + qo + qw],
                            start=(c == 0), stop=(c == DC - 1)), waits=mw,
                            inc=(sPE if c == DC - 1 else None))
                    r.emit('scalar', lambda e, rh=rh, qo=qo, qw=qw, k=k: e.activation(
                        ssro[0:rh, qo:qo + qw], pss[k % 2][0:rh, 0:qw], AF.Identity),
                        waits=[W(sPE)] + ([dma_prev] if dma_prev else []), inc=sA)
                    pssl.release(k, sA)
                    k += 1
                r.emit('sync', lambda e, ro=ro, rh=rh: e.dma_start(
                    dview(ssm_i, ro * SL, [rh, SL]), ssro[0:rh, :]), waits=[W(sA)],
                    inc=ssst.sem(0), incn=16)
                dma_prev = ssst.val(0)
            r.emit('gpsimd', lambda e: e.collective_compute(
                "AllGather", ALU.bypass, replica_groups=[list(range(NCORES))],
                ins=[ssm_i.ap().opt()], outs=[ssm_o.ap().opt()]), waits=[ssst.val(0)], inc=sCC)
        run_block('ssmp', b_ssmp)

    with contextlib.ExitStack() as stk:
        asb = sb('asb', [SHL, SN], F32, stk)
        atc = sb('atc', [128, max(SHL, 1)], F32, stk)
        ssb = sb('ssb', [SH, SHL * 128], F32, stk)
        Bc = [sb(f'sc_B{i}', [SN, SL], F32, stk) for i in range(2)]
        Cc = [sb(f'sc_C{i}', [SN, SL], F32, stk) for i in range(2)]
        dbc = [sb(f'sc_db{i}', [SH, SL], F32, stk) for i in range(2)]
        sin = sb('sc_sin', [128, SL], F32, stk)
        bbuf = sb('sc_b', [128, SL], F32, stk)
        abuf = sb('sc_a', [128, SL], F32, stk)
        hs = sb('sc_hs', [128, SL], F32, stk)
        hc = sb('sc_hc', [128, SL], F32, stk)
        states = sb('sc_st', [128, SHL], F32, stk)
        yrow = sb('sc_y', [1, SL], F32, stk)
        pdb = psum('sc_pdb', [128, 1024], F32, stk)
        py = psum('sc_py', [128, 1024], F32, stk)
        ptr4 = psum('sc_ptr4', [128, 512], F32, stk)

        def b_scan(r):
            aux(r, 'sync', lambda e: e.dma_start(asb[:], a_loc.ap()))
            wsel = aux(r, 'sync', lambda e: e.dma_start(ssb[:], selb.ap()))
            bcst = stream('bcd', 2)
            ywst = stream('yww', 1)
            r.emit('tensor', lambda e: e.transpose(ptr4[0:SN, 0:SHL], asb[:], idf_sb[0:SHL, 0:SHL]),
                   waits=[wsel], inc=sPE)
            r.emit('vector', lambda e: e.tensor_copy(atc[0:SN, 0:SHL], ptr4[0:SN, 0:SHL]),
                   waits=[W(sPE)], inc=sV)
            r.drain('vector')
            NCH = S // SL
            csl = Slots(2)
            prev = {}
            for c in range(NCH):
                base = c * SSP * SL
                lw = [W(sCC)] + csl.acquire(c)
                r.emit('sync', lambda e, c=c, base=base: e.dma_start(
                    Bc[c % 2][:], dview(ssm_o, base, [SN, SL])), waits=lw, inc=bcst.sem(c), incn=16)
                r.emit('sync', lambda e, c=c, base=base: e.dma_start(
                    Cc[c % 2][:], dview(ssm_o, base + SN * SL, [SN, SL])), inc=bcst.sem(c), incn=16)
                r.emit('sync', lambda e, c=c, base=base: e.dma_start(
                    dbc[c % 2][:], dview(ssm_o, base + 2 * SN * SL, [SH, SL])), inc=bcst.sem(c), incn=16)
                ld_v = bcst.val(c)
                for jj in range(SHL):
                    for (qo, qw) in QC:
                        r.emit('tensor', lambda e, jj=jj, c=c, qo=qo, qw=qw: e.matmul(
                            pdb[:, qo:qo + qw], ssb[:, jj * 128:(jj + 1) * 128],
                            dbc[c % 2][:, qo:qo + qw], start=True, stop=True),
                            waits=[ld_v, (sV, prev.get('sin', 0))],
                            inc=(sPE if qo + qw == SL else None))
                    r.emit('vector', lambda e, c=c: e.tensor_add(sin[:], pdb[:, 0:SL], Bc[c % 2][:]),
                           waits=[W(sPE), (sA, prev.get('d', 0))], inc=sV)
                    prev['sin'] = sV.v
                    r.drain('vector')
                    r.emit('scalar', lambda e: e.activation(sin[:], sin[:], AF.Exp), waits=[W(sV)])
                    r.drain('scalar')
                    r.emit('scalar', lambda e: e.activation(sin[:], sin[:], AF.Ln, bias=CONST(C_ONE)),
                           inc=sA)
                    r.drain('scalar')
                    r.emit('scalar', lambda e: e.activation(bbuf[:], sin[:], AF.Square),
                           waits=[(sV, prev.get('scan', 0))], inc=sA)
                    r.drain('scalar')
                    prev['d'] = sA.v
                    r.emit('vector', lambda e, jj=jj: e.tensor_scalar_mul(abuf[:], sin[:],
                                                                          atc[:, jj:jj + 1]),
                           waits=[W(sA), (sA, prev.get('aexp', 0))], inc=sV)
                    r.drain('vector')
                    r.emit('scalar', lambda e: e.activation(abuf[:], abuf[:], AF.Exp),
                           waits=[W(sV)], inc=sA)
                    prev['aexp'] = sA.v
                    init = 0.0 if c == 0 else states[:, jj:jj + 1]
                    r.emit('vector', lambda e, init=init: e.tensor_tensor_scan(
                        hs[:], abuf[:], bbuf[:], init, op0=ALU.mult, op1=ALU.add),
                        waits=[W(sA)], inc=sV)
                    r.drain('vector')
                    r.emit('vector', lambda e, jj=jj: e.tensor_copy(states[:, jj:jj + 1],
                                                                    hs[:, SL - 1:SL]))
                    r.drain('vector')
                    r.emit('vector', lambda e, c=c: e.tensor_mul(hc[:], hs[:], Cc[c % 2][:]),
                           waits=[(sPE, prev.get('ymm', 0))], inc=sV)
                    prev['scan'] = sV.v
                    r.drain('vector')
                    for (qo, qw) in QC:
                        r.emit('tensor', lambda e, qo=qo, qw=qw: e.matmul(
                            py[0:1, qo:qo + qw], onesf[:], hc[:, qo:qo + qw],
                            start=True, stop=True),
                            waits=[W(sV), (sA, prev.get('yevac', 0))],
                            inc=(sPE if qo + qw == SL else None))
                    prev['ymm'] = sPE.v
                    r.emit('scalar', lambda e: e.activation(yrow[0:1, 0:SL], py[0:1, 0:SL], AF.Identity),
                           waits=[W(sPE)] + ([prev['ydma']] if 'ydma' in prev else []), inc=sA)
                    prev['yevac'] = sA.v
                    r.emit('gpsimd', lambda e, c=c, jj=jj: e.dma_start(
                        dview(ya_i, (c * SHL + jj) * SL, [1, SL]), yrow[0:1, :]),
                        waits=[W(sA)], inc=ywst.sem(0), incn=16)
                    prev['ydma'] = ywst.val(0)
                csl.release(c, sV)
            r.emit('gpsimd', lambda e: e.collective_compute(
                "AllToAll", ALU.bypass, replica_groups=[list(range(NCORES))],
                ins=[ya_i.ap().opt()], outs=[ya_o.ap().opt()]), waits=[ywst.val(0)], inc=sCC)
        with allow_ici_all_to_all():
            run_block('scan', b_scan)

    with contextlib.ExitStack() as stk:
        gateT = sb('gateT', [128, DC * SL], BF16, stk)
        bg_sb = sb('bg_sb', [128, DC], F32, stk)
        dp_sb = sb('dp_sb', [128, DC], F32, stk)
        stgg = [sb(f'g_stg{i}', [128, 128], F32, stk) for i in range(2)]
        wgb = [sb(f'g_wgb{i}', [128, DC * 128], BF16, stk) for i in range(2)]
        ybc = sb('g_ybc', [128, SL], F32, stk)
        yt1 = sb('g_yt1', [128, SL], F32, stk)
        pg = [psum(f'g_pg{i}', [128, 1024], F32, stk) for i in range(2)]

        def b_gate(r):
            aux(r, 'sync', lambda e: e.dma_start(bg_sb[:], bgT.ap()))
            aux_v = aux(r, 'sync', lambda e: e.dma_start(dp_sb[:], dpT.ap()))
            stsl = Slots(2)
            wgsl = Slots(2)
            pgsl = Slots(2)
            wgst = stream('wgw', 2)
            ybst = stream('ybl', 1)
            li = 0
            cast_done = {}
            def load_m(m):
                nonlocal li
                if m >= DC:
                    return
                for c in range(DC):
                    ap = wg[c * 128:(c + 1) * 128, m * 128:(m + 1) * 128]
                    r.emit('sync', lambda e, ap=ap, li=li: e.dma_start(stgg[li % 2][:], ap),
                           waits=stsl.acquire(li) + (wgsl.acquire(m) if c == 0 else []),
                           inc=wgst.sem(li), incn=16)
                    r.emit('vector', lambda e, c=c, m=m, li=li: e.tensor_copy(
                        wgb[m % 2][:, c * 128:(c + 1) * 128], stgg[li % 2][:]),
                        waits=[wgst.val(li)], inc=sV)
                    r.drain('vector')
                    stsl.release(li, sV)
                    li += 1
                cast_done[m] = sV.v
            load_m(0); load_m(1)
            sig_done = {}
            for m in range(DC):
                pgw = [(sV, cast_done[m])] + pgsl.acquire(m)
                for c in range(DC):
                    for (qo, qw) in QC:
                        r.emit('tensor', lambda e, m=m, c=c, qo=qo, qw=qw: e.matmul(
                            pg[m % 2][:, qo:qo + qw], wgb[m % 2][:, c * 128:(c + 1) * 128],
                            xnT[:, c * SL + qo: c * SL + qo + qw],
                            start=(c == 0), stop=(c == DC - 1)),
                            waits=pgw if (c == 0 and qo == 0) else (),
                            inc=(sPE if (c == DC - 1 and qo + qw == SL) else None))
                wgsl.release(m, sPE)
                load_m(m + 2)
                r.emit('scalar', lambda e, m=m: e.activation(
                    gateT[:, m * SL:(m + 1) * SL], pg[m % 2][:, 0:SL], AF.Sigmoid,
                    bias=bg_sb[:, m:m + 1]), waits=[W(sPE), aux_v], inc=sA)
                sig_done[m] = sA.v
                pgsl.release(m, sA)
            gates_all = W(sPE)
            yb_prev = 0
            for m in range(DC):
                for half in range(HPT):
                    r.emit('gpsimd', lambda e, m=m, half=half: e.dma_start(
                        ybc[half * HDS:(half + 1) * HDS, :],
                        bass.AP(ya_o, (HPT * m + half) * SL, [[0, HDS], [1, SL]])),
                        waits=[W(sCC), (sV, yb_prev)], inc=ybst.sem(0), incn=16)
                r.emit('vector', lambda e, m=m: e.tensor_scalar_mul(
                    yt1[:], xnT[:, m * SL:(m + 1) * SL], dp_sb[:, m:m + 1]),
                    waits=[gates_all, (sA, sig_done[m])], inc=None)
                r.drain('vector')
                r.emit('vector', lambda e, m=m: e.tensor_add(yt1[:], yt1[:], ybc[:]),
                       waits=[ybst.val(0)])
                r.drain('vector')
                r.emit('vector', lambda e, m=m: e.tensor_mul(
                    xnT[:, m * SL:(m + 1) * SL], yt1[:], gateT[:, m * SL:(m + 1) * SL]), inc=sV)
                r.drain('vector')
                yb_prev = sV.v
        run_block('gate', b_gate)

    with contextlib.ExitStack() as stk:
        wstg2 = [sb(f'woo_stg{i}', [128, 512], F32, stk) for i in range(2)]
        woob = sb('woo_b', [128, DC * 512], BF16, stk)
        pw2 = [psum(f'woo_pw{i}', [128, 512], F32, stk) for i in range(2)]

        def b_wo(r):
            stsl = Slots(2)
            pwsl = Slots(2)
            wost2 = stream('wo2', 2)
            li = 0
            mm_prev = 0
            for n in range(ND5):
                cast_done = None
                for c in range(DC):
                    ap = wo_d[c * 128:(c + 1) * 128, n * 512:(n + 1) * 512]
                    r.emit('sync', lambda e, ap=ap, li=li: e.dma_start(wstg2[li % 2][:], ap),
                           waits=stsl.acquire(li), inc=wost2.sem(li), incn=16)
                    r.emit('vector', lambda e, c=c, li=li: e.tensor_copy(
                        woob[:, c * 512:(c + 1) * 512], wstg2[li % 2][:]),
                        waits=[wost2.val(li), (sPE, mm_prev)], inc=sV)
                    r.drain('vector')
                    stsl.release(li, sV)
                    li += 1
                cast_done = W(sV)
                for t in range(TT):
                    k = n * TT + t
                    for c in range(DC):
                        mw = ([cast_done] + pwsl.acquire(k)) if c == 0 else ()
                        r.emit('tensor', lambda e, t=t, c=c, k=k: e.matmul(
                            pw2[k % 2][:], xnT[:, c * SL + t * 128: c * SL + (t + 1) * 128],
                            woob[:, c * 512:(c + 1) * 512],
                            start=(c == 0), stop=(c == DC - 1)), waits=mw,
                            inc=(sPE if c == DC - 1 else None))
                    r.emit('vector', lambda e, t=t, n=n, k=k: e.tensor_add(
                        xres[:, t * D + n * 512: t * D + (n + 1) * 512],
                        xres[:, t * D + n * 512: t * D + (n + 1) * 512], pw2[k % 2][:]),
                        waits=[W(sPE)], inc=sV)
                    r.drain('vector')
                    pwsl.release(k, sV)
                mm_prev = sPE.v
        run_block('wo', b_wo)

    with contextlib.ExitStack() as stk:
        amx4 = sb('amx4', [128, 2], F32, stk)
        ptam2 = psum('ptam2', [128, 512], F32, stk)
        run_block('ln4', lambda r: emit_ln(r, 'ln4', stk, src_sb=xres, amax_to=amx4))
        run_block('ar2', lambda r: emit_amax_ar(r, 'ar2', amx4, red2_i, red2_o, ptam2))
        run_block('ffn2', lambda r: emit_ffn(r, 'ffn2', stk))

    def b_store(r):
        for t in range(TT):
            r.emit('sync', lambda e, t=t: e.dma_start(
                out[t * 128:(t + 1) * 128, :], xres[:, t * D:(t + 1) * D]),
                waits=[W(sV), W(sA)], inc=sO, incn=16)
        r.emit('sync', lambda e: e.nop(), waits=[W(sO)])
    run_block('store', b_store)

    return nc



def _host_consts(cfg):
    v = np.zeros(NCONST, dtype=np.float64)
    v[C_INV_DQL] = 1.0 / cfg['DQL']
    v[C_INV_DKVL] = 1.0 / cfg['DKVL']
    v[C_INV_D] = 1.0 / cfg['D']
    v[C_EPS_LN] = 1e-5
    v[C_EPS_RMS] = 1e-6
    v[C_SCALE_ATT] = 1.0 / np.sqrt(cfg['HD'] + cfg['RD'])
    v[C_ONE] = 1.0
    v[C_NEG1] = -1.0
    return v.astype(np.float32).reshape(1, NCONST)


def _host_rope(cfg, core):
    S, SL, RD = cfg['S'], cfg['SL'], cfg['RD']
    f = 10000.0 ** (-(np.arange(0, RD, 2, dtype=np.float64)) / RD)
    t = np.arange(core * SL, (core + 1) * SL, dtype=np.float64)
    ang = np.outer(f, t)
    C2 = np.repeat(np.cos(ang), 2, axis=0)
    S2 = np.repeat(np.sin(ang), 2, axis=0)
    S2[0::2, :] *= -1.0
    return C2.astype(np.float32), S2.astype(np.float32)


def make_inputs(cfg, core, full):
    SL, D, H, RD = cfg['SL'], cfg['D'], cfg['H'], cfg['RD']
    DQL, DKVL, SH, SN, SHL = cfg['DQL'], cfg['DKVL'], cfg['SH'], cfg['SN'], cfg['SHL']
    QLC, KLC, DC, HDS = cfg['QLC'], cfg['KLC'], cfg['DC'], cfg['HDS']
    f32 = lambda a: np.ascontiguousarray(a, dtype=np.float32)
    C2, S2 = _host_rope(cfg, core)
    psw_m = np.zeros((RD, RD), dtype=np.float32)
    for rrr in range(RD):
        psw_m[rrr, rrr ^ 1] = 1.0
    sel = np.zeros((SH, SHL * 128), dtype=np.float32)
    for j in range(SHL):
        sel[SHL * core + j, j * 128:(j + 1) * 128] = 1.0
    d = {
        'x_in': f32(full['x'][core * SL:(core + 1) * SL]),
        'wqd': f32(full['Wq_down']), 'wkvd': f32(full['Wkv_down']),
        'wqu': f32(full['Wq_up']), 'wku': f32(full['Wk_up']), 'wvu': f32(full['Wv_up']),
        'wqr': f32(full['Wq_rope']), 'wkr': f32(full['Wk_rope']),
        'wout': f32(full['Wout']),
        'qnwT': f32(full['q_norm_w'].reshape(QLC, 128).T),
        'kvnwT': f32(full['kv_norm_w'].reshape(KLC, 128).T),
        'wx': f32(full['Wx']), 'wg': f32(full['Wg']), 'wo': f32(full['Wo']),
        'bgT': f32(full['bg'].reshape(DC, 128).T),
        'dpT': f32(np.repeat(full['Dp'], HDS).reshape(DC, 128).T),
        'a_loc': f32(full['A'][SHL * core: SHL * (core + 1)]),
        'selb': sel,
        'fw': np.ascontiguousarray(full['fw'], dtype=ml_dtypes.bfloat16),
        'fb': np.ascontiguousarray(full['fb'], dtype=ml_dtypes.bfloat16),
        'c2t': C2, 's2t': S2,
        'idb': np.eye(128, dtype=ml_dtypes.bfloat16),
        'idf': np.eye(128, dtype=np.float32),
        'psw': psw_m.astype(ml_dtypes.bfloat16),
        'consts': _host_consts(cfg),
    }
    return d


def kernel(L0_attn_a_x, L0_attn_a_Wq_down, L0_attn_a_q_norm_w, L0_attn_a_Wq_up,
           L0_attn_a_Wq_rope, L0_attn_a_Wkv_down, L0_attn_a_kv_norm_w, L0_attn_a_Wk_up,
           L0_attn_a_Wv_up, L0_attn_a_Wk_rope, L0_attn_a_Wout, L0_attn_b_Wx, L0_attn_b_Wg,
           L0_attn_b_bg, L0_attn_b_Wo, L0_attn_b_A, L0_attn_b_Dp, L0_ffn_weight, L0_ffn_bias):
    cfg = make_cfg()
    S, D, SL = cfg['S'], cfg['D'], cfg['SL']
    full = dict(
        x=np.asarray(L0_attn_a_x, dtype=np.float32).reshape(S, D),
        Wq_down=L0_attn_a_Wq_down, q_norm_w=L0_attn_a_q_norm_w, Wq_up=L0_attn_a_Wq_up,
        Wq_rope=L0_attn_a_Wq_rope, Wkv_down=L0_attn_a_Wkv_down, kv_norm_w=L0_attn_a_kv_norm_w,
        Wk_up=L0_attn_a_Wk_up, Wv_up=L0_attn_a_Wv_up, Wk_rope=L0_attn_a_Wk_rope,
        Wout=L0_attn_a_Wout, Wx=L0_attn_b_Wx, Wg=L0_attn_b_Wg, bg=L0_attn_b_bg,
        Wo=L0_attn_b_Wo, A=L0_attn_b_A, Dp=L0_attn_b_Dp,
        fw=L0_ffn_weight, fb=L0_ffn_bias,
    )
    nc = build_nc(cfg)
    in_maps = [make_inputs(cfg, core, full) for core in range(NCORES)]
    res = run_bass_kernel_spmd(nc, in_maps, core_ids=list(range(NCORES)))
    outs = [np.asarray(r['out'], dtype=np.float32) for r in res.results]
    return np.concatenate(outs, axis=0).reshape(1, S, D)


# version 06
# speedup vs baseline: 1.0942x; 1.0002x over previous
import contextlib
import numpy as np
import ml_dtypes

import concourse.bass as bass
import concourse.mybir as mybir
from concourse.bass_utils import run_bass_kernel_spmd
from concourse.aot_env import allow_ici_all_to_all

F32 = mybir.dt.float32
BF16 = mybir.dt.bfloat16
F8 = mybir.dt.float8e4
AF = mybir.ActivationFunctionType
ALU = mybir.AluOpType
AXX = mybir.AxisListType.X

NCORES = 8
C_INV_DQL, C_INV_DKVL, C_INV_D, C_EPS_LN, C_EPS_RMS, C_SCALE_ATT, C_ONE, C_NEG1 = range(8)
NCONST = 8
ENGS = ('sync', 'gpsimd', 'tensor', 'scalar', 'vector')


def make_cfg(S=8192, D=2048, H=16, HD=128, RD=64, DQL=384, DKVL=512, SH=32, SN=128):
    c = dict(S=S, D=D, H=H, HD=HD, RD=RD, DQL=DQL, DKVL=DKVL, SH=SH, SN=SN)
    c['SL'] = S // NCORES
    c['TT'] = c['SL'] // 128
    c['DC'] = D // 128
    c['QLC'] = DQL // 128
    c['KLC'] = DKVL // 128
    c['SHL'] = SH // NCORES
    c['HDS'] = D // SH
    c['HPT'] = 128 // c['HDS']
    c['KT'] = c['SL'] // 128
    c['ND5'] = D // 512
    assert c['SL'] % 128 == 0 and D % 512 == 0 and RD == 64 and HD == 128
    assert DQL % 128 == 0 and DKVL % 128 == 0 and SN == 128 and c['HPT'] == 2
    assert c['TT'] <= 8 and SH % NCORES == 0
    return c


class Counter:
    __slots__ = ('h', 'v')
    def __init__(self, h): self.h = h; self.v = 0


class Rec:
    def __init__(self):
        self.ops = {e: [] for e in ENGS}
        self._waited = {e: {} for e in ENGS}

    def emit(self, eng, fn, waits=(), inc=None, incn=1):
        wl = []
        seen = self._waited[eng]
        for (c, v) in waits:
            if v <= 0:
                continue
            if seen.get(id(c), -1) >= v:
                continue
            seen[id(c)] = v
            wl.append((c, v))
        if inc is not None:
            inc.v += incn
        self.ops[eng].append((fn, wl, inc, incn))

    def drain(self, eng):
        self.ops[eng].append((lambda e: e.drain(), [], None, 0))

    def replay(self, blk):
        for eng in ENGS:
            ops = self.ops[eng]
            if not ops:
                continue
            def section(e, _ops=ops):
                for fn, waits, inc, incn in _ops:
                    for (c, v) in waits:
                        e.wait_ge(c.h, v)
                    instr = fn(e)
                    if inc is not None:
                        instr.then_inc(inc.h, incn)
            getattr(blk, eng)(section)


def W(c):
    return (c, c.v)


class Slots:
    def __init__(self, n):
        self.n = n
        self.free = [None] * n
    def acquire(self, i):
        f = self.free[i % self.n]
        return [f] if f is not None else []
    def release(self, i, counter):
        self.free[i % self.n] = (counter, counter.v)


class Stream:
    def __init__(self, sems):
        self.sems = sems
        self.n = len(sems)
    def sem(self, i):
        return self.sems[i % self.n]
    def val(self, i):
        s = self.sems[i % self.n]
        return (s, s.v)


def qchunks(total, w=512):
    o, outl = 0, []
    while o < total:
        s = min(w, total - o); outl.append((o, s)); o += s
    return outl



def build_nc(cfg):
    S, D, H, HD, RD = cfg['S'], cfg['D'], cfg['H'], cfg['HD'], cfg['RD']
    DQL, DKVL, SH, SN = cfg['DQL'], cfg['DKVL'], cfg['SH'], cfg['SN']
    SL, TT, DC, QLC, KLC = cfg['SL'], cfg['TT'], cfg['DC'], cfg['QLC'], cfg['KLC']
    SHL, HDS, HPT, KT, ND5 = cfg['SHL'], cfg['HDS'], cfg['HPT'], cfg['KT'], cfg['ND5']
    SSP = 2 * SN + SH
    QC = qchunks(SL)
    JH = NCORES * KT

    nc = bass.Bass("TRN2", target_bir_lowering=False, debug=False)

    def din(name, shape, dtype=F32):
        return nc.dram_tensor(name, shape, dtype, kind="ExternalInput")

    x_in = din('x_in', [SL, D])
    wqd = din('wqd', [D, DQL]); wkvd = din('wkvd', [D, DKVL])
    wqu = din('wqu', [DQL, H * HD]); wku = din('wku', [DKVL, H * HD]); wvu = din('wvu', [DKVL, H * HD])
    wqr = din('wqr', [D, H * RD]); wkr = din('wkr', [D, H * RD])
    wout_d = din('wout', [H * HD, D])
    qnwT = din('qnwT', [128, QLC]); kvnwT = din('kvnwT', [128, KLC])
    wx = din('wx', [D, SSP]); wg = din('wg', [D, D]); wo_d = din('wo', [D, D])
    bgT = din('bgT', [128, DC]); dpT = din('dpT', [128, DC])
    a_loc = din('a_loc', [SHL, SN])
    selb = din('selb', [SH, SHL * 128])
    fw = din('fw', [D, D], BF16); fb = din('fb', [D], BF16)
    c2t = din('c2t', [RD, SL]); s2t = din('s2t', [RD, SL])
    idb = din('idb', [128, 128], BF16); idf = din('idf', [128, 128])
    psw = din('psw', [RD, RD], BF16)
    consts = din('consts', [1, NCONST])
    out = nc.dram_tensor('out', [SL, D], F32, kind="ExternalOutput")

    LBLK = H * 128 * SL + H * 65 * SL + H * SL * 129 + H * SL
    OFF_KA = 0
    OFF_KB = H * 128 * SL
    OFF_V = OFF_KB + H * 65 * SL
    OFF_KN = OFF_V + H * SL * 129
    kv_in = nc.dram_tensor('kv_in', [LBLK], BF16)
    kv_out = nc.dram_tensor('kv_out', [NCORES * LBLK], BF16)
    red1_i = nc.dram_tensor('red1_i', [1, 1], F32); red1_o = nc.dram_tensor('red1_o', [1, 1], F32)
    red2_i = nc.dram_tensor('red2_i', [1, 1], F32); red2_o = nc.dram_tensor('red2_o', [1, 1], F32)
    red3_i = nc.dram_tensor('red3_i', [H], F32); red3_o = nc.dram_tensor('red3_o', [H], F32)
    ssm_i = nc.dram_tensor('ssm_i', [SSP * SL], F32)
    ssm_o = nc.dram_tensor('ssm_o', [NCORES * SSP * SL], F32)
    ya_i = nc.dram_tensor('ya_i', [NCORES * SHL * SL], F32)
    ya_o = nc.dram_tensor('ya_o', [NCORES * SHL * SL], F32)
    bounce = nc.dram_tensor('bounce', [64], F32)

    def dview(buf, off, shape):
        n = int(np.prod(shape))
        ap = buf.ap()[off:off + n]
        names = ' '.join(f'd{i}' for i in range(len(shape)))
        return ap.rearrange(f'({names}) -> {names}', **{f'd{i}': s for i, s in enumerate(shape)})

    ctx = contextlib.ExitStack()
    nc._kernel_ctx = ctx

    def sb(name, shape, dtype, stk=None, side=None):
        kw = {'side': side} if side else {}
        return (stk or ctx).enter_context(nc.sbuf_tensor(name, shape, dtype, **kw))
    def psum(name, shape, dtype=F32, stk=None):
        return (stk or ctx).enter_context(nc.psum_tensor(name, shape, dtype))
    def sem(name):
        return Counter(ctx.enter_context(nc.semaphore(name)))

    sO = sem('sO')
    sCC = sem('sCC')
    sAXS = sem('sAXS')
    sAXG = sem('sAXG')
    sPRE = sem('sPRE'); sQSQ = sem('sQSQ'); sSTB = sem('sSTB')
    sXR = sem('sXR'); sBIA = sem('sBIA')
    sPE = sem('sPE'); sA = sem('sA'); sV = sem('sV'); sV2 = sem('sV2')
    sP = sem('sP'); sPV = sem('sPV'); sQK = sem('sQK')

    def stream(name, n):
        return Stream([sem(f'{name}_{i}') for i in range(n)])

    def aux(r, eng, fn, extra=()):
        s = sAXS if eng == 'sync' else sAXG
        r.emit(eng, fn, waits=[W(s)] + list(extra), inc=s, incn=16)
        return (s, s.v)

    def run_block(name, fn):
        r = Rec()
        fn(r)
        with nc.Block(name) as blk:
            r.replay(blk)

    consts_sb = sb('consts_sb', [128, NCONST], F32)
    idb_sb = sb('idb_sb', [128, 128], BF16)
    idf_sb = sb('idf_sb', [128, 128], F32)
    psw_sb = sb('psw_sb', [RD, RD], BF16)
    qnw_sb = sb('qnw_sb', [128, QLC], F32)
    kvnw_sb = sb('kvnw_sb', [128, KLC], F32)
    onesf = sb('onesf', [128, 1], F32)
    xnT = sb('xnT', [128, DC * SL], BF16)
    ws_sb = sb('ws_sb', [128, 5], F32)
    arow = sb('arow', [1, 32], F32)
    qspan = contextlib.ExitStack()
    qa = sb('qa', [128, H * SL], BF16, qspan, side='right')
    qbspan = contextlib.ExitStack()
    qb = sb('qb', [65, H * SL], BF16, qbspan, side='right')
    qsq_sb = sb('qsq_sb', [128, SL], F32, qbspan, side='right')
    proj_stk = contextlib.ExitStack()
    cqT = sb('cqT', [128, QLC * SL], BF16, proj_stk)
    ckT = sb('ckT', [128, KLC * SL], BF16, proj_stk)
    c2_sb = sb('c2_sb', [RD, SL], F32, proj_stk)
    s2_sb = sb('s2_sb', [RD, SL], F32, proj_stk)

    CONST = lambda i: consts_sb[:, i:i + 1]

    def b_pre(r):
        pairs = [(consts_sb[:], bass.AP(consts, 0, [[0, 128], [1, NCONST]])),
                 (idb_sb[:], idb.ap()), (idf_sb[:], idf.ap()), (psw_sb[:], psw.ap()),
                 (qnw_sb[:], qnwT.ap()), (kvnw_sb[:], kvnwT.ap()),
                 (c2_sb[:], c2t.ap()), (s2_sb[:], s2t.ap())]
        for dst, src in pairs:
            r.emit('sync', lambda e, d=dst, s=src: e.dma_start(d, s), inc=sPRE, incn=16)
        r.emit('vector', lambda e: e.memset(onesf[:], 1.0), waits=[W(sPRE)])
    run_block('pre', b_pre)

    def emit_ln(r, tag, stk, src_sb=None, amax_to=None):
        xslot = [sb(f'{tag}_x{i}', [128, D], F32, stk) for i in range(2)] if src_sb is None else None
        scr = sb(f'{tag}_scr', [128, D], F32, stk)
        xn = [sb(f'{tag}_xn{i}', [128, D], BF16, stk) for i in range(2)]
        st = sb(f'{tag}_st', [128, 8], F32, stk)
        ptr = psum(f'{tag}_ptr', [128, 1024], BF16, stk)
        xsl = Slots(2)
        xnsl = Slots(2)
        trsl = Slots(1)
        xst = stream(f'{tag}x', 2) if src_sb is None else None

        def src_ap(t):
            return xslot[t % 2][:] if src_sb is None else src_sb[:, t * D:(t + 1) * D]

        ld_done = {}
        def load(t):
            if src_sb is None and t < TT:
                r.emit('sync', lambda e, t=t: e.dma_start(
                    xslot[t % 2][:], x_in[t * 128:(t + 1) * 128, :]),
                    waits=xsl.acquire(t), inc=xst.sem(t), incn=16)
                ld_done[t] = xst.val(t)
        load(0); load(1)

        for t in range(TT):
            lw = [ld_done[t]] if src_sb is None else []
            r.emit('vector', lambda e, t=t: e.reduce_sum(st[:, 0:1], src_ap(t), axis=AXX), waits=lw)
            r.drain('vector')
            r.emit('vector', lambda e: e.tensor_scalar_mul(st[:, 2:3], st[:, 0:1], 1.0 / D))
            r.drain('vector')
            r.emit('vector', lambda e: e.tensor_mul(st[:, 3:4], st[:, 2:3], st[:, 2:3]))
            r.drain('vector')
            r.emit('scalar', lambda e, t=t: e.activation(scr[:], src_ap(t), AF.Square,
                                                         accum_out=st[:, 1:2]), waits=lw, inc=sA)
            r.drain('scalar')
            r.emit('vector', lambda e: e.scalar_tensor_tensor(
                st[:, 4:5], st[:, 1:2], 1.0 / D, st[:, 3:4], op0=ALU.mult, op1=ALU.subtract),
                waits=[W(sA)], inc=sV)
            r.drain('vector')
            r.emit('scalar', lambda e: e.activation(st[:, 5:6], st[:, 4:5], AF.Sqrt,
                                                    bias=CONST(C_EPS_LN)), waits=[W(sV)], inc=sA)
            r.drain('scalar')
            r.emit('vector', lambda e: e.reciprocal(st[:, 6:7], st[:, 5:6]), waits=[W(sA)])
            r.drain('vector')
            r.emit('vector', lambda e: e.scalar_tensor_tensor(
                st[:, 7:8], st[:, 2:3], -1.0, st[:, 6:7], op0=ALU.mult, op1=ALU.mult), inc=sV)
            r.drain('vector')
            r.emit('scalar', lambda e, t=t: e.activation(xn[t % 2][:], src_ap(t), AF.Identity,
                                                         scale=st[:, 6:7], bias=st[:, 7:8]),
                   waits=[W(sV)] + xnsl.acquire(t), inc=sA)
            xn_ready = W(sA)
            xsl.release(t, sA)
            load(t + 2)
            if amax_to is not None:
                dst = amax_to[:, 0:1] if t == 0 else amax_to[:, 1:2]
                r.emit('vector', lambda e, t=t, dst=dst: e.reduce_max(
                    dst, xn[t % 2][:], axis=AXX, apply_absolute_value=True), waits=[xn_ready])
                r.drain('vector')
                if t > 0:
                    r.emit('vector', lambda e: e.tensor_max(amax_to[:, 0:1], amax_to[:, 0:1],
                                                            amax_to[:, 1:2]))
                    r.drain('vector')
            for c in range(DC):
                k = t * DC + c
                r.emit('tensor', lambda e, c=c, k=k, t=t: e.transpose(
                    ptr[:, (k % 8) * 128:(k % 8) * 128 + 128],
                    xn[t % 2][:, c * 128:(c + 1) * 128], idb_sb[:]),
                    waits=[xn_ready] + trsl.acquire(k), inc=sPE)
                r.emit('vector', lambda e, c=c, k=k, t=t: e.tensor_copy(
                    xnT[:, c * SL + t * 128: c * SL + (t + 1) * 128],
                    ptr[:, (k % 8) * 128:(k % 8) * 128 + 128]), waits=[W(sPE)], inc=sV2)
                trsl.release(k, sV2)
            xnsl.release(t, sV2)
        if amax_to is not None:
            r.drain('vector')

    with contextlib.ExitStack() as stk:
        run_block('ln1', lambda r: emit_ln(r, 'ln1', stk, src_sb=None))

    with contextlib.ExitStack() as stk:
        wqd_bf = sb('wqd_bf', [128, DC * DQL], BF16, stk)
        wkvd_bf = sb('wkvd_bf', [128, DC * DKVL], BF16, stk)
        cqn = sb('cqn', [128, TT * DQL], BF16, stk)
        ckn = sb('ckn', [128, TT * DKVL], BF16, stk)
        sqs = sb('sqs_a', [128, 512], F32, stk)
        rsn = sb('rs_a', [128, 8], F32, stk)
        stg = [sb(f'p2a_stg{i}', [128, max(DQL, DKVL)], F32, stk) for i in range(2)]
        pcq = [psum(f'pcq{i}', [128, 512], F32, stk) for i in range(2)]
        ptr2 = psum('ptr2', [128, 1024], BF16, stk)

        def b_p2a(r):
            stsl = Slots(2)
            psl = Slots(2)
            trsl = Slots(1)
            wqst = stream('p2aw', 2)
            plan = [(wqd[c * 128:(c + 1) * 128, :], DQL, wqd_bf[:, c * DQL:(c + 1) * DQL])
                    for c in range(DC)] + \
                   [(wkvd[c * 128:(c + 1) * 128, :], DKVL, wkvd_bf[:, c * DKVL:(c + 1) * DKVL])
                    for c in range(DC)]
            wq_rdy = wk_rdy = None
            for i, (ap, w_, dst) in enumerate(plan):
                r.emit('sync', lambda e, ap=ap, w_=w_, i=i: e.dma_start(stg[i % 2][:, 0:w_], ap),
                       waits=stsl.acquire(i), inc=wqst.sem(i), incn=16)
                r.emit('vector', lambda e, w_=w_, dst=dst, i=i: e.tensor_copy(dst, stg[i % 2][:, 0:w_]),
                       waits=[wqst.val(i)], inc=sV)
                stsl.release(i, sV)
                if i == DC - 1: wq_rdy = W(sV)
                if i == 2 * DC - 1: wk_rdy = W(sV)

            def one(wrdy, WD, invc, wbf, cn, cT, nLC, po):
                for t in range(TT):
                    pq = pcq[t % 2]
                    for c in range(DC):
                        r.emit('tensor', lambda e, c=c, t=t, pq=pq, WD=WD, wbf=wbf: e.matmul(
                            pq[:, 0:WD], xnT[:, c * SL + t * 128: c * SL + (t + 1) * 128],
                            wbf[:, c * WD:(c + 1) * WD], start=(c == 0), stop=(c == DC - 1)),
                            waits=([wrdy] + psl.acquire(t)) if c == 0 else (),
                            inc=(sPE if c == DC - 1 else None))
                    r.emit('scalar', lambda e, pq=pq, WD=WD: e.activation(
                        sqs[:, 0:WD], pq[:, 0:WD], AF.Square, accum_out=rsn[:, po:po + 1]),
                        waits=[W(sPE)], inc=sA)
                    r.drain('scalar')
                    r.emit('scalar', lambda e: e.activation(
                        rsn[:, po + 1:po + 2], rsn[:, po:po + 1], AF.Sqrt,
                        scale=CONST(invc), bias=CONST(C_EPS_RMS)), inc=sA)
                    r.emit('vector', lambda e: e.reciprocal(rsn[:, po + 2:po + 3], rsn[:, po + 1:po + 2]),
                           waits=[W(sA)], inc=sV)
                    r.drain('vector')
                    r.emit('scalar', lambda e, pq=pq, WD=WD, t=t, cn=cn: e.activation(
                        cn[:, t * WD:(t + 1) * WD], pq[:, 0:WD], AF.Identity,
                        scale=rsn[:, po + 2:po + 3]), waits=[W(sV)], inc=sA)
                    psl.release(t, sA)
                    cn_rdy = W(sA)
                    for c in range(nLC):
                        k = t * nLC + c
                        r.emit('tensor', lambda e, c=c, t=t, k=k, cn=cn, WD=WD: e.transpose(
                            ptr2[:, (k % 8) * 128:(k % 8) * 128 + 128],
                            cn[:, t * WD + c * 128: t * WD + (c + 1) * 128], idb_sb[:]),
                            waits=[cn_rdy] + trsl.acquire(k), inc=sPE)
                        r.emit('vector', lambda e, c=c, t=t, k=k, cT=cT: e.tensor_copy(
                            cT[:, c * SL + t * 128: c * SL + (t + 1) * 128],
                            ptr2[:, (k % 8) * 128:(k % 8) * 128 + 128]), waits=[W(sPE)], inc=sV2)
                        trsl.release(k, sV2)
            one(wq_rdy, DQL, C_INV_DQL, wqd_bf, cqn, cqT, QLC, 0)
            one(wk_rdy, DKVL, C_INV_DKVL, wkvd_bf, ckn, ckT, KLC, 4)
        run_block('p2a', b_p2a)

    def rope_combine(r, h, prp, pswp, rsc, rt1, rt2, dstB_ap, dstB_w, prev):
        r.emit('scalar', lambda e: e.activation(rsc[0:64, 0:SL], prp[0:64, 0:SL], AF.Identity),
               waits=[W(sPE), (sPE, prev.get('swap', 0)), (sV, prev.get('comb', 0))], inc=sA)
        rsc_rdy = W(sA)
        for (qo, qw) in QC:
            r.emit('tensor', lambda e, qo=qo, qw=qw: e.matmul(
                pswp[0:64, qo:qo + qw], psw_sb[:], rsc[0:64, qo:qo + qw],
                start=True, stop=True),
                waits=[rsc_rdy, (sV, prev.get('comb2', 0))],
                inc=(sPE if qo + qw == SL else None))
        prev['swap'] = sPE.v
        r.emit('vector', lambda e: e.tensor_mul(rt1[0:64, :], rsc[0:64, 0:SL], c2_sb[0:64, :]),
               waits=[rsc_rdy])
        r.drain('vector')
        r.emit('vector', lambda e: e.tensor_mul(rt2[0:64, :], pswp[0:64, 0:SL], s2_sb[0:64, :]),
               waits=[W(sPE)], inc=sV)
        prev['comb2'] = sV.v
        r.drain('vector')
        r.emit('vector', lambda e, dstB_ap=dstB_ap: e.tensor_add(dstB_ap, rt1[0:64, :], rt2[0:64, :]),
               waits=list(dstB_w), inc=sV)
        prev['comb'] = sV.v
        r.drain('vector')
        return W(sV)

    def up_rope_phase(r, stk, tag, w_up, w_rope, nLC, cT_src, fold_sb, dstA_fn, dstB_fn, sink_fn=None):
        stgq = [sb(f'{tag}_stg{i}', [128, 128], F32, stk) for i in range(2)]
        wtb = [sb(f'{tag}_wt{i}', [128, 128], BF16, stk) for i in range(2)]
        rsc = sb(f'{tag}_rsc', [64, SL], BF16, stk)
        rt1 = sb(f'{tag}_rt1', [64, SL], F32, stk)
        rt2 = sb(f'{tag}_rt2', [64, SL], F32, stk)
        pqa = [psum(f'{tag}_pqa0', [128, 1024], F32, stk)]
        prp = psum(f'{tag}_prp', [128, 1024], F32, stk)
        pswp = psum(f'{tag}_psw', [128, 1024], F32, stk)

        plan = []
        for h in range(H):
            for c in range(nLC):
                plan.append((w_up[c * 128:(c + 1) * 128, h * HD:(h + 1) * HD], HD, fold_sb[:, c:c + 1]))
            for c in range(DC):
                plan.append((w_rope[c * 128:(c + 1) * 128, h * RD:(h + 1) * RD], RD, None))
        cast_done = {}
        stsl = Slots(2)
        wtsl = Slots(2)
        wst = stream(f'{tag}w', 2)
        cursor = [0]
        def load_next():
            i = cursor[0]
            if i >= len(plan):
                return
            cursor[0] += 1
            ap, w_, fold = plan[i]
            r.emit('sync', lambda e, ap=ap, w_=w_, i=i: e.dma_start(stgq[i % 2][:, 0:w_], ap),
                   waits=stsl.acquire(i), inc=wst.sem(i), incn=16)
            if fold is not None:
                r.emit('vector', lambda e, w_=w_, i=i, fold=fold: e.tensor_scalar_mul(
                    wtb[i % 2][:, 0:w_], stgq[i % 2][:, 0:w_], fold),
                    waits=[wst.val(i)] + wtsl.acquire(i), inc=sV)
            else:
                r.emit('vector', lambda e, w_=w_, i=i: e.tensor_copy(
                    wtb[i % 2][:, 0:w_], stgq[i % 2][:, 0:w_]),
                    waits=[wst.val(i)] + wtsl.acquire(i), inc=sV)
            stsl.release(i, sV)
            cast_done[i] = sV.v
        load_next(); load_next()

        pqsl = Slots(1)
        prev = {}
        for h in range(H):
            base = h * (nLC + DC)
            pq = pqa[0]
            for c in range(nLC):
                i = base + c
                mw = [(sV, cast_done[i])] + (pqsl.acquire(h) if c == 0 else [])
                for (qo, qw) in QC:
                    r.emit('tensor', lambda e, c=c, i=i, qo=qo, qw=qw, pq=pq: e.matmul(
                        pq[:, qo:qo + qw], wtb[i % 2][:, 0:HD],
                        cT_src[:, c * SL + qo: c * SL + qo + qw],
                        start=(c == 0), stop=(c == nLC - 1)), waits=mw,
                        inc=(sPE if qo + qw == SL else None))
                wtsl.release(i, sPE)
                load_next()
            da, daw = dstA_fn(h)
            r.emit('scalar', lambda e, da=da, pq=pq: e.activation(da, pq[:, 0:SL], AF.Identity),
                   waits=[W(sPE)] + list(daw), inc=sA)
            pqsl.release(h, sA)
            a_rdy = W(sA)
            for c in range(DC):
                i = base + nLC + c
                mw = [(sV, cast_done[i])] + ([(sPE, prev.get('swap', 0))] if c == 0 else [])
                for (qo, qw) in QC:
                    r.emit('tensor', lambda e, c=c, i=i, qo=qo, qw=qw: e.matmul(
                        prp[0:64, qo:qo + qw], wtb[i % 2][:, 0:RD],
                        xnT[:, c * SL + qo: c * SL + qo + qw],
                        start=(c == 0), stop=(c == DC - 1)), waits=mw,
                        inc=(sPE if qo + qw == SL else None))
                wtsl.release(i, sPE)
                load_next()
            db_ap, db_w = dstB_fn(h)
            b_rdy = rope_combine(r, h, prp, pswp, rsc, rt1, rt2, db_ap, db_w, prev)
            if sink_fn is not None:
                sink_fn(r, h, a_rdy, b_rdy)

    with contextlib.ExitStack() as stk:
        sqsb = sb('p2b_sqs', [128, SL], F32, stk)
        qrow = sb('p2b_qrow', [1, SL], F32, stk)
        pq1 = psum('p2b_pq1', [128, 1024], F32, stk)
        sqprev = {}

        def q_sink(r, h, a_rdy, b_rdy):
            r.emit('scalar', lambda e, h=h: e.activation(
                sqsb[:, 0:SL], qa[:, h * SL:(h + 1) * SL], AF.Square),
                waits=[a_rdy, (sPE, sqprev.get('mm', 0))], inc=sA)
            r.drain('scalar')
            for (qo, qw) in QC:
                r.emit('tensor', lambda e, qo=qo, qw=qw: e.matmul(
                    pq1[0:1, qo:qo + qw], onesf[:], sqsb[:, qo:qo + qw],
                    start=True, stop=False),
                    waits=[W(sA), (sA, sqprev.get('evac', 0))],
                    inc=(sPE if qo + qw == SL else None))
            r.emit('scalar', lambda e, h=h: e.activation(
                sqsb[0:64, 0:SL], qb[0:64, h * SL:(h + 1) * SL], AF.Square),
                waits=[b_rdy, W(sPE)], inc=sA)
            r.drain('scalar')
            for (qo, qw) in QC:
                r.emit('tensor', lambda e, qo=qo, qw=qw: e.matmul(
                    pq1[0:1, qo:qo + qw], onesf[0:64, :], sqsb[0:64, qo:qo + qw],
                    start=False, stop=True), waits=[W(sA)],
                    inc=(sPE if qo + qw == SL else None))
            sqprev['mm'] = sPE.v
            r.emit('scalar', lambda e: e.activation(qrow[0:1, 0:SL], pq1[0:1, 0:SL], AF.Identity),
                   waits=[W(sPE), (sQSQ, sQSQ.v)], inc=sA)
            sqprev['evac'] = sA.v
            r.emit('gpsimd', lambda e, h=h: e.dma_start(qsq_sb[h:h + 1, :], qrow[0:1, 0:SL]),
                   waits=[W(sA)], inc=sQSQ, incn=16)

        def b_p2b(r):
            up_rope_phase(r, stk, 'p2b', wqu, wqr, QLC, cqT, qnw_sb,
                          lambda h: (qa[:, h * SL:(h + 1) * SL], []),
                          lambda h: (qb[0:64, h * SL:(h + 1) * SL], []), q_sink)
            r.emit('gpsimd', lambda e: e.engine_nop(), waits=[W(sQSQ)])
        run_block('p2b', b_p2b)

    kv_store_sems = []
    ws_ready = [None]
    knar_ready = [None]
    with contextlib.ExitStack() as stk:
        knm_loc = sb('knm_loc', [1, H], F32, stk)
        kaS = [sb(f'kaS{i}', [128, SL], BF16, stk) for i in range(2)]
        kbS = [sb(f'kbS{i}', [65, SL], BF16, stk) for i in range(2)]
        sqsk = sb('p2c_sqs', [128, SL], F32, stk)
        krow = sb('p2c_krow', [1, SL], BF16, stk)
        pkn = psum('p2c_pkn', [128, 1024], F32, stk)
        kasl = Slots(2); kbsl = Slots(2); krsl = Slots(1)
        kast = stream('kaw', 2); kbst = stream('kbw', 2); krst = stream('krw', 1)
        kv_store_sems.extend(kast.sems + kbst.sems + krst.sems)
        kprev = {}

        def k_sink(r, h, a_rdy, b_rdy):
            r.emit('sync', lambda e, h=h: e.dma_start(
                dview(kv_in, OFF_KA + h * 128 * SL, [128, SL]), kaS[h % 2][:]),
                waits=[a_rdy], inc=kast.sem(h), incn=16)
            kasl.release(h, kast.sem(h))
            r.emit('sync', lambda e, h=h: e.dma_start(
                dview(kv_in, OFF_KB + h * 65 * SL, [65, SL]), kbS[h % 2][:]),
                waits=[b_rdy], inc=kbst.sem(h), incn=16)
            kbsl.release(h, kbst.sem(h))
            r.emit('scalar', lambda e, h=h: e.activation(
                sqsk[:, 0:SL], kaS[h % 2][:], AF.Square),
                waits=[a_rdy, (sPE, kprev.get('mm', 0))], inc=sA)
            r.drain('scalar')
            for (qo, qw) in QC:
                r.emit('tensor', lambda e, qo=qo, qw=qw: e.matmul(
                    pkn[0:1, qo:qo + qw], onesf[:], sqsk[:, qo:qo + qw],
                    start=True, stop=False),
                    waits=[W(sA), (sA, kprev.get('evac', 0))],
                    inc=(sPE if qo + qw == SL else None))
            r.emit('scalar', lambda e, h=h: e.activation(
                sqsk[0:64, 0:SL], kbS[h % 2][0:64, :], AF.Square),
                waits=[b_rdy, W(sPE)], inc=sA)
            r.drain('scalar')
            for (qo, qw) in QC:
                r.emit('tensor', lambda e, qo=qo, qw=qw: e.matmul(
                    pkn[0:1, qo:qo + qw], onesf[0:64, :], sqsk[0:64, qo:qo + qw],
                    start=False, stop=True), waits=[W(sA)],
                    inc=(sPE if qo + qw == SL else None))
            kprev['mm'] = sPE.v
            r.emit('scalar', lambda e: e.activation(krow[0:1, 0:SL], pkn[0:1, 0:SL], AF.Identity),
                   waits=[W(sPE)] + krsl.acquire(h), inc=sA)
            kprev['evac'] = sA.v
            r.emit('sync', lambda e, h=h: e.dma_start(
                dview(kv_in, OFF_KN + h * SL, [1, SL]), krow[0:1, :]), waits=[W(sA)],
                inc=krst.sem(0), incn=16)
            r.emit('vector', lambda e, h=h: e.reduce_max(
                knm_loc[0:1, h:h + 1], krow[0:1, :], axis=AXX), waits=[W(sA)], inc=sV)
            r.drain('vector')
            krsl.release(h, krst.sem(0))
            kprev['knm'] = sV.v

        def b_p2c(r):
            r.emit('vector', lambda e: e.memset(kbS[0][64:65, :], 1.0))
            r.emit('vector', lambda e: e.memset(kbS[1][64:65, :], 1.0))
            r.drain('vector')
            up_rope_phase(r, stk, 'p2c', wku, wkr, KLC, ckT, kvnw_sb,
                          lambda h: (kaS[h % 2][:], kasl.acquire(h)),
                          lambda h: (kbS[h % 2][0:64, :], kbsl.acquire(h)), k_sink)
            w1 = aux(r, 'sync', lambda e: e.dma_start(red3_i.ap(), knm_loc[0:1, :]),
                     extra=[(sV, kprev['knm'])])
            r.emit('gpsimd', lambda e: e.collective_compute(
                "AllReduce", ALU.max, replica_groups=[list(range(NCORES))],
                ins=[red3_i.ap().opt()], outs=[red3_o.ap().opt()]), waits=[w1], inc=sCC)
            knar_ready[0] = W(sCC)
            r.emit('sync', lambda e: e.nop(),
                   waits=[(s, s.v) for s in (kast.sems + kbst.sems + krst.sems)])
        run_block('p2c', b_p2c)

    with contextlib.ExitStack() as stk2:
        wvu_bf = sb('wvu_bf', [128, KLC * H * HD], BF16, stk2)
        stgv = [sb(f'p2v_stg{i}', [128, 512], F32, stk2) for i in range(2)]
        vSl = [sb(f'p2v_v{i}', [128, H * 129], BF16, stk2) for i in range(2)]
        fstg = [sb(f'p2v_f{i}', [128, D], BF16, stk2) for i in range(2)]
        famx = sb('p2v_famx', [128, 2], F32, stk2)
        pv = psum('p2v_pv', [128, 512], F32, stk2)
        ptr3 = psum('p2v_ptr3', [128, 512], F32, stk2)

        def b_p2v(r):
            np5 = (H * HD) // 512
            stsl = Slots(2)
            wvst = stream('wvw', 2)
            vstt = stream('vsw', 2)
            fwst = stream('fww', 2)
            i = 0
            for c in range(KLC):
                for p5 in range(np5):
                    ap = wvu[c * 128:(c + 1) * 128, p5 * 512:(p5 + 1) * 512]
                    r.emit('sync', lambda e, ap=ap, i=i: e.dma_start(stgv[i % 2][:], ap),
                           waits=stsl.acquire(i), inc=wvst.sem(i), incn=16)
                    r.emit('vector', lambda e, c=c, p5=p5, i=i: e.tensor_scalar_mul(
                        wvu_bf[:, c * H * HD + p5 * 512: c * H * HD + (p5 + 1) * 512],
                        stgv[i % 2][:], kvnw_sb[:, c:c + 1]), waits=[wvst.val(i)], inc=sV)
                    stsl.release(i, sV)
                    i += 1
            wv_rdy = W(sV)
            vsl = Slots(2)
            evac_v = 0
            for t in range(TT):
                r.emit('vector', lambda e, t=t: e.memset(vSl[t % 2][:], 1.0),
                       waits=vsl.acquire(t))
                r.drain('vector')
                r.emit('vector', lambda e: e.engine_nop(), inc=sV)
                ones_rdy = W(sV)
                for n5 in range(np5):
                    for c in range(KLC):
                        r.emit('tensor', lambda e, t=t, n5=n5, c=c: e.matmul(
                            pv[:], ckT[:, c * SL + t * 128: c * SL + (t + 1) * 128],
                            wvu_bf[:, c * H * HD + n5 * 512: c * H * HD + (n5 + 1) * 512],
                            start=(c == 0), stop=(c == KLC - 1)),
                            waits=([wv_rdy, (sA, evac_v)] if c == 0 else ()),
                            inc=(sPE if c == KLC - 1 else None))
                    for hh in range(4):
                        h = n5 * 4 + hh
                        r.emit('scalar', lambda e, t=t, h=h, hh=hh: e.activation(
                            vSl[t % 2][:, h * 129: h * 129 + 128],
                            pv[:, hh * 128:(hh + 1) * 128], AF.Identity),
                            waits=[W(sPE), ones_rdy], inc=sA)
                    evac_v = sA.v
                for h in range(H):
                    r.emit('sync', lambda e, t=t, h=h: e.dma_start(
                        dview(kv_in, OFF_V + h * SL * 129 + t * 128 * 129, [128, 129]),
                        vSl[t % 2][:, h * 129:(h + 1) * 129]), waits=[W(sA)],
                        inc=vstt.sem(t), incn=16)
                vsl.release(t, vstt.sem(t))
            agw = [(s, s.v) for s in (kv_store_sems + vstt.sems)]
            r.emit('gpsimd', lambda e: e.collective_compute(
                "AllGather", ALU.bypass, replica_groups=[list(range(NCORES))],
                ins=[kv_in.ap().opt()], outs=[kv_out.ap().opt()]), waits=agw, inc=sCC)
            fsl = Slots(2)
            for c in range(DC):
                r.emit('sync', lambda e, c=c: e.dma_start(
                    fstg[c % 2][:], fw[c * 128:(c + 1) * 128, :]),
                    waits=fsl.acquire(c), inc=fwst.sem(c), incn=16)
                if c == 0:
                    r.emit('vector', lambda e: e.reduce_max(
                        famx[:, 0:1], fstg[0][:], axis=AXX, apply_absolute_value=True),
                        waits=[fwst.val(c)], inc=sV)
                else:
                    r.emit('vector', lambda e, c=c: e.reduce_max(
                        famx[:, 1:2], fstg[c % 2][:], axis=AXX, apply_absolute_value=True),
                        waits=[fwst.val(c)])
                    r.drain('vector')
                    r.emit('vector', lambda e: e.tensor_max(famx[:, 0:1], famx[:, 0:1],
                                                            famx[:, 1:2]), inc=sV)
                r.drain('vector')
                fsl.release(c, sV)
            r.emit('tensor', lambda e: e.transpose(ptr3[0:1, 0:128], famx[:, 0:1], idf_sb[:]),
                   waits=[W(sV)], inc=sPE)
            r.emit('vector', lambda e: e.reduce_max(arow[0:1, 0:1], ptr3[0:1, 0:128], axis=AXX),
                   waits=[W(sPE)])
            r.drain('vector')
            r.emit('vector', lambda e: e.reciprocal(arow[0:1, 1:2], arow[0:1, 0:1]), inc=sV)
            r.drain('vector')
            aux(r, 'sync', lambda e: e.dma_start(bounce.ap()[0:2], arow[0:1, 0:2]),
                extra=[W(sV)])
            aux(r, 'sync', lambda e: e.dma_start(
                ws_sb[:, 0:1], bass.AP(bounce, 0, [[0, 128], [1, 1]])))
            aux(r, 'sync', lambda e: e.dma_start(
                ws_sb[:, 1:2], bass.AP(bounce, 1, [[0, 128], [1, 1]])))
            ws_ready[0] = (sAXS, sAXS.v)
        run_block('p2v', b_p2v)

    proj_stk.close()

    with contextlib.ExitStack() as stk:
        knm = sb('knm', [H, 2], F32, stk)
        stmp = sb('stmp', [H, SL], F32, stk)
        stb = sb('stb', [H, SL], BF16, stk)

        def b_stab(r):
            kngst = stream('kng', 1)
            r.emit('gpsimd', lambda e: e.dma_start(
                knm[:, 0:1], bass.AP(red3_o, 0, [[1, H], [1, 1]])),
                waits=[knar_ready[0]], inc=kngst.sem(0), incn=16)
            r.emit('vector', lambda e: e.engine_nop(), waits=[kngst.val(0)], inc=sV)
            r.emit('vector', lambda e: e.tensor_scalar_mul(stmp[:], qsq_sb[0:H, :], knm[:, 0:1]),
                   waits=[W(sV), (sQSQ, 16 * H)], inc=sV)
            r.drain('vector')
            r.emit('scalar', lambda e: e.activation(stmp[:], stmp[:], AF.Sqrt), waits=[W(sV)], inc=sA)
            r.emit('vector', lambda e: e.tensor_scalar_mul(stb[:], stmp[:], -1.0), waits=[W(sA)], inc=sV)
            r.drain('vector')
            for h in range(H):
                r.emit('gpsimd', lambda e, h=h: e.dma_start(
                    qb[64:65, h * SL:(h + 1) * SL], stb[h:h + 1, :]), waits=[W(sV)], inc=sSTB, incn=16)
            r.emit('gpsimd', lambda e: e.engine_nop(), waits=[(sSTB, 16 * H)])
        run_block('stab', b_stab)

    NSLAB = 3
    NP = 3
    with contextlib.ExitStack() as stk:
        kaA = [sb(f'at_ka{i}', [128, SL], BF16, stk) for i in range(NSLAB)]
        kbA = [sb(f'at_kb{i}', [65, SL], BF16, stk) for i in range(NSLAB)]
        vA = [sb(f'at_v{i}', [128, KT * 129], BF16, stk) for i in range(NSLAB)]
        pS = [sb(f'at_p{i}', [128, SL], BF16, stk) for i in range(NP)]
        aosb = [sb(f'at_ao{i}', [128, 128], BF16, stk) for i in range(2)]
        rq = sb('at_rq', [128, 8], F32, stk)
        psc = [psum(f'at_sc{i}', [128, 1024], F32, stk) for i in range(2)]
        po = [psum(f'at_po{i}', [128, 512], F32, stk) for i in range(3)]
        ptra = psum('at_ptra', [128, 1024], BF16, stk)

        slabs = [(h, src) for h in range(H) for src in range(NCORES)]

        def b_attn(r):
            kv_done = {}
            kvsl = Slots(NSLAB)
            kvst = stream('kvl', NSLAB)
            def load_slab(i):
                if i >= len(slabs):
                    return
                h, src = slabs[i]
                base = src * LBLK
                aw = kvsl.acquire(i) + [W(sCC)]
                r.emit('gpsimd', lambda e, h=h, src=src, i=i: e.dma_start(
                    kaA[i % NSLAB][:], dview(kv_out, base + OFF_KA + h * 128 * SL, [128, SL])),
                    waits=aw, inc=kvst.sem(i), incn=16)
                r.emit('gpsimd', lambda e, h=h, src=src, i=i: e.dma_start(
                    kbA[i % NSLAB][:], dview(kv_out, base + OFF_KB + h * 65 * SL, [65, SL])),
                    inc=kvst.sem(i), incn=16)
                for kt in range(KT):
                    r.emit('gpsimd', lambda e, h=h, src=src, i=i, kt=kt: e.dma_start(
                        vA[i % NSLAB][:, kt * 129:(kt + 1) * 129],
                        dview(kv_out, base + OFF_V + h * SL * 129 + kt * 128 * 129, [128, 129])),
                        inc=kvst.sem(i), incn=16)
                kv_done[i] = kvst.val(i)
            for i in range(min(NSLAB, len(slabs))):
                load_slab(i)

            p_done = {}
            pv_done = {}
            ao_done = {}
            aot_done = {}
            tr_done = {}
            j = 0
            for i, (h, src) in enumerate(slabs):
                jh0 = (i % NCORES) * KT
                for kt in range(KT):
                    jh = jh0 + kt
                    qkw = [kv_done[i]]
                    if j >= 2:
                        qkw.append((sP, p_done[j - 2]))
                    if jh == 0:
                        qkw.append((sSTB, 16 * H))
                    for (qo, qw) in QC:
                        r.emit('tensor', lambda e, i=i, kt=kt, qo=qo, qw=qw, h=h, j=j: e.matmul(
                            psc[j % 2][:, qo:qo + qw],
                            kaA[i % NSLAB][:, kt * 128:(kt + 1) * 128],
                            qa[:, h * SL + qo: h * SL + qo + qw], start=True, stop=False),
                            waits=qkw if qo == 0 else ())
                        r.emit('tensor', lambda e, i=i, kt=kt, qo=qo, qw=qw, h=h, j=j: e.matmul(
                            psc[j % 2][:, qo:qo + qw],
                            kbA[i % NSLAB][:, kt * 128:(kt + 1) * 128],
                            qb[:, h * SL + qo: h * SL + qo + qw], start=False, stop=True),
                            inc=(sQK if qo + qw == SL else None))
                    qk_v = sQK.v
                    ew = [(sQK, qk_v)]
                    if j >= NP:
                        ew.append((sPV, pv_done[j - NP]))
                    r.emit('scalar', lambda e, j=j: e.activation(
                        pS[j % NP][:, 0:SL], psc[j % 2][:, 0:SL], AF.Exp,
                        scale=CONST(C_SCALE_ATT)), waits=ew, inc=sP)
                    p_done[j] = sP.v
                    pvw = [(sP, p_done[j])]
                    if jh == 0 and h >= 1:
                        pvw.append((sV, ao_done[h - 1]))
                    for qt in range(TT):
                        b, o = qt // 3, (qt % 3) * 129
                        st_ = (jh == 0 and qt % 3 == 0)
                        sp_ = (jh == JH - 1 and (qt % 3 == 2 or qt == TT - 1))
                        r.emit('tensor', lambda e, i=i, kt=kt, qt=qt, b=b, o=o, st_=st_, sp_=sp_, j=j: e.matmul(
                            po[b][:, o:o + 129],
                            pS[j % NP][:, qt * 128:(qt + 1) * 128],
                            vA[i % NSLAB][:, kt * 129:(kt + 1) * 129], start=st_, stop=sp_),
                            waits=pvw if qt == 0 else (),
                            inc=(sPV if qt == TT - 1 else None))
                    pv_done[j] = sPV.v
                    j += 1
                if i + NSLAB < len(slabs):
                    kvsl.release(i, sPV)
                    load_slab(i + NSLAB)
                if (i % NCORES) == NCORES - 1:
                    for qt in range(TT):
                        b, o = qt // 3, (qt % 3) * 129
                        r.emit('vector', lambda e, qt=qt, b=b, o=o: e.reciprocal(
                            rq[:, qt:qt + 1], po[b][:, o + 128:o + 129]), waits=[W(sPV)])
                    r.drain('vector')
                    for qt in range(TT):
                        b, o = qt // 3, (qt % 3) * 129
                        vw = [(sPE, tr_done.get((h, qt - 2), 0))]
                        r.emit('vector', lambda e, qt=qt, b=b, o=o: e.tensor_scalar_mul(
                            aosb[qt % 2][:], po[b][:, o:o + 128], rq[:, qt:qt + 1]),
                            waits=vw, inc=sV)
                        sc_v = sV.v
                        r.emit('tensor', lambda e, qt=qt, h=h: e.transpose(
                            ptra[:, (qt % 2) * 128:(qt % 2) * 128 + 128], aosb[qt % 2][:], idb_sb[:]),
                            waits=[(sV, sc_v), (sV2, aot_done.get((h, qt - 1), 0))], inc=sPE)
                        tr_done[(h, qt)] = sPE.v
                        r.emit('vector', lambda e, qt=qt, h=h: e.tensor_copy(
                            qa[:, h * SL + qt * 128: h * SL + (qt + 1) * 128],
                            ptra[:, (qt % 2) * 128:(qt % 2) * 128 + 128]),
                            waits=[(sPE, tr_done[(h, qt)])], inc=sV2)
                        aot_done[(h, qt)] = sV2.v
                    ao_done[h] = sV.v
        run_block('attn', b_attn)
    qbspan.close()

    wq8T = sb('wq8T', [128, DC * D], F8)
    xres = sb('xres', [128, TT * D], F32)
    with contextlib.ExitStack() as stk:
        wstg = [sb(f'wo_stg{i}', [128, 512], F32, stk) for i in range(2)]
        wob = sb('wo_b', [128, DC * 512], BF16, stk)
        pw = [psum(f'wo_pw{i}', [128, 512], F32, stk) for i in range(2)]

        def b_wout(r):
            for t in range(TT):
                r.emit('sync', lambda e, t=t: e.dma_start(
                    xres[:, t * D:(t + 1) * D], x_in[t * 128:(t + 1) * 128, :]), inc=sXR, incn=16)
            xres_ld = (sXR, 16 * TT)
            stsl = Slots(2)
            pwsl = Slots(2)
            wost = stream('wow', 2)
            li = 0
            add_done = {}
            mm_prev = 0
            for n in range(ND5):
                cast_done = {}
                for hc in range(DC):
                    ap = wout_d[hc * 128:(hc + 1) * 128, n * 512:(n + 1) * 512]
                    r.emit('sync', lambda e, ap=ap, li=li: e.dma_start(wstg[li % 2][:], ap),
                           waits=stsl.acquire(li), inc=wost.sem(li), incn=16)
                    r.emit('vector', lambda e, hc=hc, li=li: e.tensor_copy(
                        wob[:, hc * 512:(hc + 1) * 512], wstg[li % 2][:]),
                        waits=[wost.val(li), (sPE, mm_prev)], inc=sV)
                    stsl.release(li, sV)
                    cast_done[hc] = sV.v
                    li += 1
                for t in range(TT):
                    k = n * TT + t
                    for hc in range(DC):
                        mw = []
                        if hc == 0:
                            mw = [(sV, cast_done[DC - 1])] + pwsl.acquire(k)
                        r.emit('tensor', lambda e, t=t, hc=hc, k=k: e.matmul(
                            pw[k % 2][:], qa[:, hc * SL + t * 128: hc * SL + (t + 1) * 128],
                            wob[:, hc * 512:(hc + 1) * 512],
                            start=(hc == 0), stop=(hc == DC - 1)), waits=mw,
                            inc=(sPE if hc == DC - 1 else None))
                    r.emit('vector', lambda e, t=t, n=n, k=k: e.tensor_add(
                        xres[:, t * D + n * 512: t * D + (n + 1) * 512],
                        xres[:, t * D + n * 512: t * D + (n + 1) * 512], pw[k % 2][:]),
                        waits=[W(sPE), xres_ld], inc=sV)
                    r.drain('vector')
                    pwsl.release(k, sV)
                mm_prev = sPE.v
        run_block('wout', b_wout)
    qspan.close()

    bias_bc = sb('bias_bc', [128, D], BF16)
    with contextlib.ExitStack() as stk:
        fstg2 = [sb(f'fw2_{i}', [128, D], BF16, stk) for i in range(2)]
        ptrf = psum('fw2_ptr', [128, 1024], BF16, stk)

        def b_fwq(r):
            fwst2 = stream('fw2', 2)
            r.emit('sync', lambda e: e.dma_start(bias_bc[:], bass.AP(fb, 0, [[0, 128], [1, D]])),
                   inc=sBIA, incn=16)
            fsl = Slots(2)
            trsl = Slots(1)
            for c in range(DC):
                r.emit('sync', lambda e, c=c: e.dma_start(fstg2[c % 2][:], fw[c * 128:(c + 1) * 128, :]),
                       waits=fsl.acquire(c), inc=fwst2.sem(c), incn=16)
                ld_v = fwst2.val(c)
                for cc in range(DC):
                    k = c * DC + cc
                    r.emit('tensor', lambda e, c=c, cc=cc, k=k: e.transpose(
                        ptrf[:, (k % 8) * 128:(k % 8) * 128 + 128],
                        fstg2[c % 2][:, cc * 128:(cc + 1) * 128], idb_sb[:]),
                        waits=[ld_v] + trsl.acquire(k), inc=sPE)
                    r.emit('scalar', lambda e, c=c, cc=cc, k=k: e.activation(
                        wq8T[:, cc * D + c * 128: cc * D + (c + 1) * 128],
                        ptrf[:, (k % 8) * 128:(k % 8) * 128 + 128], AF.Identity,
                        scale=ws_sb[:, 1:2]), waits=[W(sPE), ws_ready[0]], inc=sA)
                    trsl.release(k, sA)
                fsl.release(c, sA)
        run_block('fwq', b_fwq)

    def emit_amax_ar(r, tag, amx, red_i, red_o, ptams):
        r.emit('tensor', lambda e: e.transpose(ptams[0:1, 0:128], amx[:, 0:1], idf_sb[:]),
               waits=[W(sV)], inc=sPE)
        r.emit('vector', lambda e: e.reduce_max(arow[0:1, 8:9], ptams[0:1, 0:128], axis=AXX),
               waits=[W(sPE)], inc=sV)
        r.drain('vector')
        w1 = aux(r, 'sync', lambda e: e.dma_start(red_i.ap(), arow[0:1, 8:9]), extra=[W(sV)])
        r.emit('gpsimd', lambda e: e.collective_compute(
            "AllReduce", ALU.max, replica_groups=[list(range(NCORES))],
            ins=[red_i.ap().opt()], outs=[red_o.ap().opt()]), waits=[w1], inc=sCC)
        w2 = aux(r, 'gpsimd', lambda e: e.dma_start(arow[0:1, 9:10], red_o.ap()), extra=[W(sCC)])
        r.emit('vector', lambda e: e.reciprocal(arow[0:1, 10:11], arow[0:1, 9:10]),
               waits=[w2], inc=sV)
        r.drain('vector')
        aux(r, 'sync', lambda e: e.dma_start(bounce.ap()[4:6], arow[0:1, 9:11]), extra=[W(sV)])
        aux(r, 'sync', lambda e: e.dma_start(ws_sb[:, 2:3], bass.AP(bounce, 5, [[0, 128], [1, 1]])))
        w3 = aux(r, 'sync', lambda e: e.dma_start(ws_sb[:, 4:5], bass.AP(bounce, 4, [[0, 128], [1, 1]])))
        r.emit('vector', lambda e: e.tensor_tensor(ws_sb[:, 3:4], ws_sb[:, 0:1], ws_sb[:, 4:5],
                                                   ALU.mult), waits=[w3], inc=sV)
        r.drain('vector')

    def emit_ffn(r, tag, stk):
        xq = sb(f'{tag}_xq', [128, DC * SL], F8, stk)
        tmp = [sb(f'{tag}_tmp{i}', [128, 512], BF16, stk) for i in range(2)]
        pf = [psum(f'{tag}_pf{i}', [128, 512], F32, stk) for i in range(2)]
        for c in range(DC):
            r.emit('scalar', lambda e, c=c: e.activation(
                xq[:, c * SL:(c + 1) * SL], xnT[:, c * SL:(c + 1) * SL], AF.Identity,
                scale=ws_sb[:, 2:3]), waits=[W(sV), W(sV2), W(sAXS)], inc=sA)
        xq_rdy = W(sA)
        pfsl = Slots(2)
        tmsl = Slots(2)
        k = 0
        for t in range(TT):
            for n in range(ND5):
                for c in range(DC):
                    mw = ([xq_rdy] + pfsl.acquire(k)) if c == 0 else ()
                    r.emit('tensor', lambda e, t=t, n=n, c=c, k=k: e.matmul(
                        pf[k % 2][:], xq[:, c * SL + t * 128: c * SL + (t + 1) * 128],
                        wq8T[:, c * D + n * 512: c * D + (n + 1) * 512],
                        start=(c == 0), stop=(c == DC - 1)), waits=mw,
                        inc=(sPE if c == DC - 1 else None))
                r.emit('scalar', lambda e, k=k: e.activation(
                    tmp[k % 2][:], pf[k % 2][:], AF.Identity, scale=ws_sb[:, 3:4]),
                    waits=[W(sPE)] + tmsl.acquire(k), inc=sA)
                pfsl.release(k, sA)
                r.emit('vector', lambda e, k=k, n=n: e.tensor_add(
                    tmp[k % 2][:], tmp[k % 2][:], bias_bc[:, n * 512:(n + 1) * 512]),
                    waits=[W(sA), (sBIA, 16)])
                r.drain('vector')
                r.emit('vector', lambda e, t=t, n=n, k=k: e.tensor_add(
                    xres[:, t * D + n * 512: t * D + (n + 1) * 512],
                    xres[:, t * D + n * 512: t * D + (n + 1) * 512], tmp[k % 2][:]), inc=sV)
                r.drain('vector')
                tmsl.release(k, sV)
                k += 1

    with contextlib.ExitStack() as stk:
        amx2 = sb('amx2', [128, 2], F32, stk)
        ptam = psum('ptam', [128, 512], F32, stk)
        run_block('ln2', lambda r: emit_ln(r, 'ln2', stk, src_sb=xres, amax_to=amx2))
        run_block('ar1', lambda r: emit_amax_ar(r, 'ar1', amx2, red1_i, red1_o, ptam))
        run_block('ffn1', lambda r: emit_ffn(r, 'ffn1', stk))

    with contextlib.ExitStack() as stk:
        run_block('ln3', lambda r: emit_ln(r, 'ln3', stk, src_sb=xres))
    with contextlib.ExitStack() as stk:
        wx_bf = sb('wx_bf', [128, DC * SSP], BF16, stk)
        stgx = [sb(f'sx_stg{i}', [128, SSP], F32, stk) for i in range(2)]
        ssro = sb('ssro', [128, SL], F32, stk)
        pss = [psum(f'pss{i}', [128, 512], F32, stk) for i in range(2)]

        def b_ssmp(r):
            stsl = Slots(2)
            wxst = stream('wxw', 2)
            ssst = stream('ssw', 1)
            for c in range(DC):
                r.emit('sync', lambda e, c=c: e.dma_start(stgx[c % 2][:], wx[c * 128:(c + 1) * 128, :]),
                       waits=stsl.acquire(c), inc=wxst.sem(c), incn=16)
                r.emit('vector', lambda e, c=c: e.tensor_copy(
                    wx_bf[:, c * SSP:(c + 1) * SSP], stgx[c % 2][:]), waits=[wxst.val(c)], inc=sV)
                stsl.release(c, sV)
            w_rdy = W(sV)
            rows = [(0, SN), (SN, SN), (2 * SN, SH)]
            k = 0
            pssl = Slots(2)
            dma_prev = None
            for (ro, rh) in rows:
                for (qo, qw) in QC:
                    for c in range(DC):
                        mw = ([w_rdy] + pssl.acquire(k)) if c == 0 else ()
                        r.emit('tensor', lambda e, ro=ro, rh=rh, qo=qo, qw=qw, c=c, k=k: e.matmul(
                            pss[k % 2][0:rh, 0:qw],
                            wx_bf[:, c * SSP + ro: c * SSP + ro + rh],
                            xnT[:, c * SL + qo: c * SL + qo + qw],
                            start=(c == 0), stop=(c == DC - 1)), waits=mw,
                            inc=(sPE if c == DC - 1 else None))
                    r.emit('scalar', lambda e, rh=rh, qo=qo, qw=qw, k=k: e.activation(
                        ssro[0:rh, qo:qo + qw], pss[k % 2][0:rh, 0:qw], AF.Identity),
                        waits=[W(sPE)] + ([dma_prev] if dma_prev else []), inc=sA)
                    pssl.release(k, sA)
                    k += 1
                r.emit('sync', lambda e, ro=ro, rh=rh: e.dma_start(
                    dview(ssm_i, ro * SL, [rh, SL]), ssro[0:rh, :]), waits=[W(sA)],
                    inc=ssst.sem(0), incn=16)
                dma_prev = ssst.val(0)
            r.emit('gpsimd', lambda e: e.collective_compute(
                "AllGather", ALU.bypass, replica_groups=[list(range(NCORES))],
                ins=[ssm_i.ap().opt()], outs=[ssm_o.ap().opt()]), waits=[ssst.val(0)], inc=sCC)
        run_block('ssmp', b_ssmp)

    with contextlib.ExitStack() as stk:
        asb = sb('asb', [SHL, SN], F32, stk)
        atc = sb('atc', [128, max(SHL, 1)], F32, stk)
        ssb = sb('ssb', [SH, SHL * 128], F32, stk)
        Bc = [sb(f'sc_B{i}', [SN, SL], F32, stk) for i in range(2)]
        Cc = [sb(f'sc_C{i}', [SN, SL], F32, stk) for i in range(2)]
        dbc = [sb(f'sc_db{i}', [SH, SL], F32, stk) for i in range(2)]
        sin = sb('sc_sin', [128, SL], F32, stk)
        bbuf = sb('sc_b', [128, SL], F32, stk)
        abuf = sb('sc_a', [128, SL], F32, stk)
        hs = sb('sc_hs', [128, SL], F32, stk)
        hc = sb('sc_hc', [128, SL], F32, stk)
        states = sb('sc_st', [128, SHL], F32, stk)
        yrow = sb('sc_y', [1, SL], F32, stk)
        pdb = psum('sc_pdb', [128, 1024], F32, stk)
        py = psum('sc_py', [128, 1024], F32, stk)
        ptr4 = psum('sc_ptr4', [128, 512], F32, stk)

        def b_scan(r):
            aux(r, 'sync', lambda e: e.dma_start(asb[:], a_loc.ap()))
            wsel = aux(r, 'sync', lambda e: e.dma_start(ssb[:], selb.ap()))
            bcst = stream('bcd', 2)
            ywst = stream('yww', 1)
            r.emit('tensor', lambda e: e.transpose(ptr4[0:SN, 0:SHL], asb[:], idf_sb[0:SHL, 0:SHL]),
                   waits=[wsel], inc=sPE)
            r.emit('vector', lambda e: e.tensor_copy(atc[0:SN, 0:SHL], ptr4[0:SN, 0:SHL]),
                   waits=[W(sPE)], inc=sV)
            r.drain('vector')
            NCH = S // SL
            csl = Slots(2)
            prev = {}
            for c in range(NCH):
                base = c * SSP * SL
                lw = [W(sCC)] + csl.acquire(c)
                r.emit('sync', lambda e, c=c, base=base: e.dma_start(
                    Bc[c % 2][:], dview(ssm_o, base, [SN, SL])), waits=lw, inc=bcst.sem(c), incn=16)
                r.emit('sync', lambda e, c=c, base=base: e.dma_start(
                    Cc[c % 2][:], dview(ssm_o, base + SN * SL, [SN, SL])), inc=bcst.sem(c), incn=16)
                r.emit('sync', lambda e, c=c, base=base: e.dma_start(
                    dbc[c % 2][:], dview(ssm_o, base + 2 * SN * SL, [SH, SL])), inc=bcst.sem(c), incn=16)
                ld_v = bcst.val(c)
                for jj in range(SHL):
                    for (qo, qw) in QC:
                        r.emit('tensor', lambda e, jj=jj, c=c, qo=qo, qw=qw: e.matmul(
                            pdb[:, qo:qo + qw], ssb[:, jj * 128:(jj + 1) * 128],
                            dbc[c % 2][:, qo:qo + qw], start=True, stop=True),
                            waits=[ld_v, (sV, prev.get('sin', 0))],
                            inc=(sPE if qo + qw == SL else None))
                    r.emit('vector', lambda e, c=c: e.tensor_add(sin[:], pdb[:, 0:SL], Bc[c % 2][:]),
                           waits=[W(sPE), (sA, prev.get('d', 0))], inc=sV)
                    prev['sin'] = sV.v
                    r.drain('vector')
                    r.emit('scalar', lambda e: e.activation(sin[:], sin[:], AF.Exp), waits=[W(sV)])
                    r.drain('scalar')
                    r.emit('scalar', lambda e: e.activation(sin[:], sin[:], AF.Ln, bias=CONST(C_ONE)),
                           inc=sA)
                    r.drain('scalar')
                    r.emit('scalar', lambda e: e.activation(bbuf[:], sin[:], AF.Square),
                           waits=[(sV, prev.get('scan', 0))], inc=sA)
                    r.drain('scalar')
                    prev['d'] = sA.v
                    r.emit('vector', lambda e, jj=jj: e.tensor_scalar_mul(abuf[:], sin[:],
                                                                          atc[:, jj:jj + 1]),
                           waits=[W(sA), (sA, prev.get('aexp', 0))], inc=sV)
                    r.drain('vector')
                    r.emit('scalar', lambda e: e.activation(abuf[:], abuf[:], AF.Exp),
                           waits=[W(sV)], inc=sA)
                    prev['aexp'] = sA.v
                    init = 0.0 if c == 0 else states[:, jj:jj + 1]
                    r.emit('vector', lambda e, init=init: e.tensor_tensor_scan(
                        hs[:], abuf[:], bbuf[:], init, op0=ALU.mult, op1=ALU.add),
                        waits=[W(sA)], inc=sV)
                    r.drain('vector')
                    r.emit('vector', lambda e, jj=jj: e.tensor_copy(states[:, jj:jj + 1],
                                                                    hs[:, SL - 1:SL]))
                    r.drain('vector')
                    r.emit('vector', lambda e, c=c: e.tensor_mul(hc[:], hs[:], Cc[c % 2][:]),
                           waits=[(sPE, prev.get('ymm', 0))], inc=sV)
                    prev['scan'] = sV.v
                    r.drain('vector')
                    for (qo, qw) in QC:
                        r.emit('tensor', lambda e, qo=qo, qw=qw: e.matmul(
                            py[0:1, qo:qo + qw], onesf[:], hc[:, qo:qo + qw],
                            start=True, stop=True),
                            waits=[W(sV), (sA, prev.get('yevac', 0))],
                            inc=(sPE if qo + qw == SL else None))
                    prev['ymm'] = sPE.v
                    r.emit('scalar', lambda e: e.activation(yrow[0:1, 0:SL], py[0:1, 0:SL], AF.Identity),
                           waits=[W(sPE)] + ([prev['ydma']] if 'ydma' in prev else []), inc=sA)
                    prev['yevac'] = sA.v
                    r.emit('gpsimd', lambda e, c=c, jj=jj: e.dma_start(
                        dview(ya_i, (c * SHL + jj) * SL, [1, SL]), yrow[0:1, :]),
                        waits=[W(sA)], inc=ywst.sem(0), incn=16)
                    prev['ydma'] = ywst.val(0)
                csl.release(c, sV)
            r.emit('gpsimd', lambda e: e.collective_compute(
                "AllToAll", ALU.bypass, replica_groups=[list(range(NCORES))],
                ins=[ya_i.ap().opt()], outs=[ya_o.ap().opt()]), waits=[ywst.val(0)], inc=sCC)
        with allow_ici_all_to_all():
            run_block('scan', b_scan)

    with contextlib.ExitStack() as stk:
        gateT = sb('gateT', [128, DC * SL], BF16, stk)
        bg_sb = sb('bg_sb', [128, DC], F32, stk)
        dp_sb = sb('dp_sb', [128, DC], F32, stk)
        stgg = [sb(f'g_stg{i}', [128, 128], F32, stk) for i in range(2)]
        wgb = [sb(f'g_wgb{i}', [128, DC * 128], BF16, stk) for i in range(2)]
        ybc = sb('g_ybc', [128, SL], F32, stk)
        yt1 = sb('g_yt1', [128, SL], F32, stk)
        pg = [psum(f'g_pg{i}', [128, 1024], F32, stk) for i in range(2)]

        def b_gate(r):
            aux(r, 'sync', lambda e: e.dma_start(bg_sb[:], bgT.ap()))
            aux_v = aux(r, 'sync', lambda e: e.dma_start(dp_sb[:], dpT.ap()))
            stsl = Slots(2)
            wgsl = Slots(2)
            pgsl = Slots(2)
            wgst = stream('wgw', 2)
            ybst = stream('ybl', 1)
            li = 0
            cast_done = {}
            def load_m(m):
                nonlocal li
                if m >= DC:
                    return
                for c in range(DC):
                    ap = wg[c * 128:(c + 1) * 128, m * 128:(m + 1) * 128]
                    r.emit('sync', lambda e, ap=ap, li=li: e.dma_start(stgg[li % 2][:], ap),
                           waits=stsl.acquire(li) + (wgsl.acquire(m) if c == 0 else []),
                           inc=wgst.sem(li), incn=16)
                    r.emit('vector', lambda e, c=c, m=m, li=li: e.tensor_copy(
                        wgb[m % 2][:, c * 128:(c + 1) * 128], stgg[li % 2][:]),
                        waits=[wgst.val(li)], inc=sV)
                    stsl.release(li, sV)
                    li += 1
                cast_done[m] = sV.v
            load_m(0); load_m(1)
            sig_done = {}
            for m in range(DC):
                pgw = [(sV, cast_done[m])] + pgsl.acquire(m)
                for c in range(DC):
                    for (qo, qw) in QC:
                        r.emit('tensor', lambda e, m=m, c=c, qo=qo, qw=qw: e.matmul(
                            pg[m % 2][:, qo:qo + qw], wgb[m % 2][:, c * 128:(c + 1) * 128],
                            xnT[:, c * SL + qo: c * SL + qo + qw],
                            start=(c == 0), stop=(c == DC - 1)),
                            waits=pgw if (c == 0 and qo == 0) else (),
                            inc=(sPE if (c == DC - 1 and qo + qw == SL) else None))
                wgsl.release(m, sPE)
                load_m(m + 2)
                r.emit('scalar', lambda e, m=m: e.activation(
                    gateT[:, m * SL:(m + 1) * SL], pg[m % 2][:, 0:SL], AF.Sigmoid,
                    bias=bg_sb[:, m:m + 1]), waits=[W(sPE), aux_v], inc=sA)
                sig_done[m] = sA.v
                pgsl.release(m, sA)
            gates_all = W(sPE)
            yb_prev = 0
            for m in range(DC):
                for half in range(HPT):
                    r.emit('gpsimd', lambda e, m=m, half=half: e.dma_start(
                        ybc[half * HDS:(half + 1) * HDS, :],
                        bass.AP(ya_o, (HPT * m + half) * SL, [[0, HDS], [1, SL]])),
                        waits=[W(sCC), (sV, yb_prev)], inc=ybst.sem(0), incn=16)
                r.emit('vector', lambda e, m=m: e.tensor_scalar_mul(
                    yt1[:], xnT[:, m * SL:(m + 1) * SL], dp_sb[:, m:m + 1]),
                    waits=[gates_all, (sA, sig_done[m])], inc=None)
                r.drain('vector')
                r.emit('vector', lambda e, m=m: e.tensor_add(yt1[:], yt1[:], ybc[:]),
                       waits=[ybst.val(0)])
                r.drain('vector')
                r.emit('vector', lambda e, m=m: e.tensor_mul(
                    xnT[:, m * SL:(m + 1) * SL], yt1[:], gateT[:, m * SL:(m + 1) * SL]), inc=sV)
                r.drain('vector')
                yb_prev = sV.v
        run_block('gate', b_gate)

    with contextlib.ExitStack() as stk:
        wstg2 = [sb(f'woo_stg{i}', [128, 512], F32, stk) for i in range(2)]
        woob = sb('woo_b', [128, DC * 512], BF16, stk)
        pw2 = [psum(f'woo_pw{i}', [128, 512], F32, stk) for i in range(2)]

        def b_wo(r):
            stsl = Slots(2)
            pwsl = Slots(2)
            wost2 = stream('wo2', 2)
            li = 0
            mm_prev = 0
            for n in range(ND5):
                cast_done = None
                for c in range(DC):
                    ap = wo_d[c * 128:(c + 1) * 128, n * 512:(n + 1) * 512]
                    r.emit('sync', lambda e, ap=ap, li=li: e.dma_start(wstg2[li % 2][:], ap),
                           waits=stsl.acquire(li), inc=wost2.sem(li), incn=16)
                    r.emit('vector', lambda e, c=c, li=li: e.tensor_copy(
                        woob[:, c * 512:(c + 1) * 512], wstg2[li % 2][:]),
                        waits=[wost2.val(li), (sPE, mm_prev)], inc=sV)
                    stsl.release(li, sV)
                    li += 1
                cast_done = W(sV)
                for t in range(TT):
                    k = n * TT + t
                    for c in range(DC):
                        mw = ([cast_done] + pwsl.acquire(k)) if c == 0 else ()
                        r.emit('tensor', lambda e, t=t, c=c, k=k: e.matmul(
                            pw2[k % 2][:], xnT[:, c * SL + t * 128: c * SL + (t + 1) * 128],
                            woob[:, c * 512:(c + 1) * 512],
                            start=(c == 0), stop=(c == DC - 1)), waits=mw,
                            inc=(sPE if c == DC - 1 else None))
                    r.emit('vector', lambda e, t=t, n=n, k=k: e.tensor_add(
                        xres[:, t * D + n * 512: t * D + (n + 1) * 512],
                        xres[:, t * D + n * 512: t * D + (n + 1) * 512], pw2[k % 2][:]),
                        waits=[W(sPE)], inc=sV)
                    r.drain('vector')
                    pwsl.release(k, sV)
                mm_prev = sPE.v
        run_block('wo', b_wo)

    with contextlib.ExitStack() as stk:
        amx4 = sb('amx4', [128, 2], F32, stk)
        ptam2 = psum('ptam2', [128, 512], F32, stk)
        run_block('ln4', lambda r: emit_ln(r, 'ln4', stk, src_sb=xres, amax_to=amx4))
        run_block('ar2', lambda r: emit_amax_ar(r, 'ar2', amx4, red2_i, red2_o, ptam2))
        run_block('ffn2', lambda r: emit_ffn(r, 'ffn2', stk))

    def b_store(r):
        for t in range(TT):
            r.emit('sync', lambda e, t=t: e.dma_start(
                out[t * 128:(t + 1) * 128, :], xres[:, t * D:(t + 1) * D]),
                waits=[W(sV), W(sA)], inc=sO, incn=16)
        r.emit('sync', lambda e: e.nop(), waits=[W(sO)])
    run_block('store', b_store)

    return nc



def _host_consts(cfg):
    v = np.zeros(NCONST, dtype=np.float64)
    v[C_INV_DQL] = 1.0 / cfg['DQL']
    v[C_INV_DKVL] = 1.0 / cfg['DKVL']
    v[C_INV_D] = 1.0 / cfg['D']
    v[C_EPS_LN] = 1e-5
    v[C_EPS_RMS] = 1e-6
    v[C_SCALE_ATT] = 1.0 / np.sqrt(cfg['HD'] + cfg['RD'])
    v[C_ONE] = 1.0
    v[C_NEG1] = -1.0
    return v.astype(np.float32).reshape(1, NCONST)


def _host_rope(cfg, core):
    S, SL, RD = cfg['S'], cfg['SL'], cfg['RD']
    f = 10000.0 ** (-(np.arange(0, RD, 2, dtype=np.float64)) / RD)
    t = np.arange(core * SL, (core + 1) * SL, dtype=np.float64)
    ang = np.outer(f, t)
    C2 = np.repeat(np.cos(ang), 2, axis=0)
    S2 = np.repeat(np.sin(ang), 2, axis=0)
    S2[0::2, :] *= -1.0
    return C2.astype(np.float32), S2.astype(np.float32)


def make_inputs(cfg, core, full):
    SL, D, H, RD = cfg['SL'], cfg['D'], cfg['H'], cfg['RD']
    DQL, DKVL, SH, SN, SHL = cfg['DQL'], cfg['DKVL'], cfg['SH'], cfg['SN'], cfg['SHL']
    QLC, KLC, DC, HDS = cfg['QLC'], cfg['KLC'], cfg['DC'], cfg['HDS']
    f32 = lambda a: np.ascontiguousarray(a, dtype=np.float32)
    C2, S2 = _host_rope(cfg, core)
    psw_m = np.zeros((RD, RD), dtype=np.float32)
    for rrr in range(RD):
        psw_m[rrr, rrr ^ 1] = 1.0
    sel = np.zeros((SH, SHL * 128), dtype=np.float32)
    for j in range(SHL):
        sel[SHL * core + j, j * 128:(j + 1) * 128] = 1.0
    d = {
        'x_in': f32(full['x'][core * SL:(core + 1) * SL]),
        'wqd': f32(full['Wq_down']), 'wkvd': f32(full['Wkv_down']),
        'wqu': f32(full['Wq_up']), 'wku': f32(full['Wk_up']), 'wvu': f32(full['Wv_up']),
        'wqr': f32(full['Wq_rope']), 'wkr': f32(full['Wk_rope']),
        'wout': f32(full['Wout']),
        'qnwT': f32(full['q_norm_w'].reshape(QLC, 128).T),
        'kvnwT': f32(full['kv_norm_w'].reshape(KLC, 128).T),
        'wx': f32(full['Wx']), 'wg': f32(full['Wg']), 'wo': f32(full['Wo']),
        'bgT': f32(full['bg'].reshape(DC, 128).T),
        'dpT': f32(np.repeat(full['Dp'], HDS).reshape(DC, 128).T),
        'a_loc': f32(full['A'][SHL * core: SHL * (core + 1)]),
        'selb': sel,
        'fw': np.ascontiguousarray(full['fw'], dtype=ml_dtypes.bfloat16),
        'fb': np.ascontiguousarray(full['fb'], dtype=ml_dtypes.bfloat16),
        'c2t': C2, 's2t': S2,
        'idb': np.eye(128, dtype=ml_dtypes.bfloat16),
        'idf': np.eye(128, dtype=np.float32),
        'psw': psw_m.astype(ml_dtypes.bfloat16),
        'consts': _host_consts(cfg),
    }
    return d


def kernel(L0_attn_a_x, L0_attn_a_Wq_down, L0_attn_a_q_norm_w, L0_attn_a_Wq_up,
           L0_attn_a_Wq_rope, L0_attn_a_Wkv_down, L0_attn_a_kv_norm_w, L0_attn_a_Wk_up,
           L0_attn_a_Wv_up, L0_attn_a_Wk_rope, L0_attn_a_Wout, L0_attn_b_Wx, L0_attn_b_Wg,
           L0_attn_b_bg, L0_attn_b_Wo, L0_attn_b_A, L0_attn_b_Dp, L0_ffn_weight, L0_ffn_bias):
    cfg = make_cfg()
    S, D, SL = cfg['S'], cfg['D'], cfg['SL']
    full = dict(
        x=np.asarray(L0_attn_a_x, dtype=np.float32).reshape(S, D),
        Wq_down=L0_attn_a_Wq_down, q_norm_w=L0_attn_a_q_norm_w, Wq_up=L0_attn_a_Wq_up,
        Wq_rope=L0_attn_a_Wq_rope, Wkv_down=L0_attn_a_Wkv_down, kv_norm_w=L0_attn_a_kv_norm_w,
        Wk_up=L0_attn_a_Wk_up, Wv_up=L0_attn_a_Wv_up, Wk_rope=L0_attn_a_Wk_rope,
        Wout=L0_attn_a_Wout, Wx=L0_attn_b_Wx, Wg=L0_attn_b_Wg, bg=L0_attn_b_bg,
        Wo=L0_attn_b_Wo, A=L0_attn_b_A, Dp=L0_attn_b_Dp,
        fw=L0_ffn_weight, fb=L0_ffn_bias,
    )
    nc = build_nc(cfg)
    in_maps = [make_inputs(cfg, core, full) for core in range(NCORES)]
    res = run_bass_kernel_spmd(nc, in_maps, core_ids=list(range(NCORES)))
    outs = [np.asarray(r['out'], dtype=np.float32) for r in res.results]
    return np.concatenate(outs, axis=0).reshape(1, S, D)
